# version 07
# baseline (speedup 1.0000x reference)
import numpy as np
import ml_dtypes
import concourse.bass as bass
import concourse.mybir as mybir
from concourse.bass_utils import run_bass_kernel_spmd

F32 = mybir.dt.float32
BF16 = mybir.dt.bfloat16
FP8 = mybir.dt.float8e4
AF = mybir.ActivationFunctionType
ALU = mybir.AluOpType
AX = mybir.AxisListType

D = 1024
SEQ = 4096
NBATCH = 4
DC = 256
INNER = 4096
NT_OWN = 16
NT_ALL = 32
KTU = [8, 16, 24, 32]
MAGIC = 12582912.0
LN_EPS = 1e-5
C_SAFETY = -0.1275

_CACHE = {}


def _build_nc(dbg=None):
    nc = bass.Bass("TRN2", target_bir_lowering=False)

    dx_own = nc.dram_tensor("x_own", [2048, D], F32, kind="ExternalInput").ap()
    dx_all = nc.dram_tensor("x_all", [SEQ, D], F32, kind="ExternalInput").ap()
    dw_dq = nc.dram_tensor("w_dq", [D, DC], F32, kind="ExternalInput").ap()
    dw_dkv = nc.dram_tensor("w_dkv", [D, DC], F32, kind="ExternalInput").ap()
    dw_uq = nc.dram_tensor("w_uq", [DC, D], F32, kind="ExternalInput").ap()
    dw_uk = nc.dram_tensor("w_uk", [DC, D], F32, kind="ExternalInput").ap()
    dw_uv = nc.dram_tensor("w_uv", [DC, D], F32, kind="ExternalInput").ap()
    dw_o = nc.dram_tensor("w_o", [D, D], F32, kind="ExternalInput").ap()
    dwg = nc.dram_tensor("wg_t", [D, INNER], F32, kind="ExternalInput").ap()
    dwu = nc.dram_tensor("wu_t", [D, INNER], F32, kind="ExternalInput").ap()
    dwd = nc.dram_tensor("wd_t", [INNER, D], F32, kind="ExternalInput").ap()
    dmasks = nc.dram_tensor("masks", [128, 2560], BF16, kind="ExternalInput").ap()
    dout = nc.dram_tensor("out", [2048, D], F32, kind="ExternalOutput").ap()

    ident_c = nc.inline_tensor(np.eye(128, dtype=ml_dtypes.bfloat16), "ident_c")
    identf_c = nc.inline_tensor(np.eye(128, dtype=np.float32), "identf_c")
    onescol_c = nc.inline_tensor(np.ones((128, 2), dtype=ml_dtypes.bfloat16), "onescol_c")
    onescol_f_c = nc.inline_tensor(np.ones((128, 1), dtype=np.float32), "onescol_f_c")
    onesrow_f_c = nc.inline_tensor(np.ones((1, 128), dtype=np.float32), "onesrow_f_c")
    bo = np.zeros((128, 2), dtype=np.float32)
    bo[0:64, 0] = 1.0
    bo[64:128, 1] = 1.0
    blockones_c = nc.inline_tensor(bo, "blockones_c")
    sel = np.zeros((2, 256), dtype=np.float32)
    sel[0, 0:128] = 1.0
    sel[1, 128:256] = 1.0
    selAB_c = nc.inline_tensor(sel, "selAB_c")
    selE = np.zeros((16, 8 * 128), dtype=np.float32)
    for hp in range(8):
        selE[2 * hp, hp * 128:hp * 128 + 64] = 1.0
        selE[2 * hp + 1, hp * 128 + 64:hp * 128 + 128] = 1.0
    selE_c = nc.inline_tensor(selE, "selE_c")

    base0 = (nc.sbuf_base + 31) // 32 * 32
    top = nc.sbuf_top // 32 * 32
    cur = [base0]

    sboff = {}

    def sb(name, shape, dtype):
        o = cur[0]
        t = nc.alloc_sbuf_tensor_at(name, shape, dtype, offset=o)
        sboff[name] = o
        nbytes = shape[1] * mybir.dt.size(dtype)
        cur[0] = o + ((nbytes + 31) // 32) * 32
        assert cur[0] <= top, f"sbuf overflow at {name}"
        return t

    X = sb("X", [128, NT_OWN * D], F32)
    ident = sb("ident", [128, 128], BF16)
    ident_f = sb("ident_f", [128, 128], F32)
    onescol = sb("onescol", [128, 2], BF16)
    onescol_f = sb("onescol_f", [128, 1], F32)
    onesrow_f = sb("onesrow_f", [1, 128], F32)
    blockones = sb("blockones", [128, 2], F32)
    selAB = sb("selAB", [2, 256], F32)
    selEs = sb("selEs", [16, 8 * 128], F32)
    masks = sb("masks", [128, 2560], BF16)
    sums = sb("sums", [128, 4], F32)
    sumsq = sb("sumsq", [128, 4], F32)
    rstds = sb("rstds", [128, 4], F32)
    negmr = sb("negmr", [128, 4], F32)
    tmp4a = sb("tmp4a", [128, 4], F32)
    tmp4b = sb("tmp4b", [128, 4], F32)
    biasA = sb("biasA", [128, 1], F32)
    biasB = sb("biasB", [128, 1], F32)
    qk_sb = sb("qk_sb", [2, 2], F32)
    qk2_sb = sb("qk2_sb", [2, 1], F32)
    ltmp = sb("ltmp", [33, 512], F32)
    wst = sb("wst", [128, 192], F32)
    wtot = sb("wtot", [128, 3], F32)
    wsc = sb("wsc", [1, 8], F32)
    swv = sb("swv", [128, 3], F32)
    mwv = sb("mwv", [128, 3], F32)
    am_sb = sb("am_sb", [128, 2], F32)
    sa_sb = sb("sa_sb", [128, 2], F32)
    inv_sb = sb("inv_sb", [128, 3], F32)
    scr4k = sb("scr4k", [128, D], F32)

    R = cur[0]
    avail = top - R
    assert avail >= 126976, f"era region too small: {avail}"

    def at(name, shape, dtype, off, size):
        nbytes = shape[1] * mybir.dt.size(dtype)
        assert nbytes <= size, f"{name}: {nbytes} > {size}"
        assert R + off + nbytes <= top, f"{name} beyond top"
        return nc.alloc_sbuf_tensor_at(name, shape, dtype, offset=R + off)

    xstage = at("xstage", [128, 2 * D], F32, 0, 8192)
    h1tok = at("h1tok", [128, 2 * D], BF16, 8192, 4096)
    h1T = at("h1T", [128, 8 * 512], BF16, 12288, 8192)
    cqT = at("cqT", [128, 2 * 2048], BF16, 20480, 8192)
    ckvT = at("ckvT", [128, 2 * SEQ], BF16, 28672, 16384)
    wdq_sb = at("wdq_sb", [128, 8 * DC], BF16, 45056, 4096)
    wdkv_sb = at("wdkv_sb", [128, 8 * DC], BF16, 49152, 4096)
    fstage = at("fstage", [128, 2 * 512], F32, 53248, 4096)
    wuq_sb = at("wuq_sb", [128, 2 * D], BF16, 57344, 4096)
    wuk_sb = at("wuk_sb", [128, 2 * D], BF16, 61440, 4096)
    wuv_sb = at("wuv_sb", [128, 2 * D], BF16, 65536, 4096)
    wo_sb = at("wo_sb", [128, 8 * D], BF16, 69632, 16384)
    OT = at("OT", [128, 8 * 2048], BF16, 86016, 32768)
    l_all = at("l_all", [16, 2048], F32, 118784, 8192)
    wf_dq = at("wf_dq", [128, 8 * DC], F32, 86016, 8192)
    wf_dkv = at("wf_dkv", [128, 8 * DC], F32, 94208, 8192)
    wf_uq = at("wf_uq", [128, 2 * D], F32, 102400, 8192)
    wf_uk = at("wf_uk", [128, 2 * D], F32, 110592, 8192)
    wf_uv = at("wf_uv", [128, 2 * D], F32, 118784, 8192)
    QT = at("QT", [128, 2048], BF16, 0, 4096)
    PT = at("PT", [128, 4 * 512], BF16, 4096, 4096)
    sqscr = at("sqscr", [128, 2048], BF16, 8192, 4096)
    KT_s = at("KT_s", [128, SEQ], BF16, 12288, 8192)
    V_s = at("V_s", [128, SEQ], BF16, 45056, 8192)
    wgq = at("wgq", [128, 8 * INNER], FP8, 20480, 32768)
    wuqq = at("wuqq", [128, 8 * INNER], FP8, 57344, 32768)
    wdqq = at("wdqq", [128, 32 * D], FP8, 90112, 32768)
    gu_sb = at("gu_sb", [128, INNER], F32, 0, 16384)
    fring = at("fring", [128, 8 * 512], F32, 0, 16384)
    qh2T = at("qh2T", [128, 8 * 128], BF16, 122880, 2048)
    qh2_sb = at("qh2_sb", [128, D], BF16, 124928, 2048)
    qguT = at("qguT", [128, 32 * 128], BF16, 0, 8192)
    gsil = nc.alloc_sbuf_tensor_at("gsil", [128, 512], F32, offset=sboff["scr4k"])
    dbgb = at("dbgb", [128, 1024], F32, 20480, 32768) if dbg else None
    assert R + 126976 <= top

    def ps(name, shape, dtype, bank):
        nc.psum_base = bank
        return nc.alloc_psum_tensor(name, shape, dtype)

    psTrA = ps("psTrA", [128, 1024], BF16, 0)
    psP0 = ps("psP0", [128, 512], F32, 1)
    psP1 = ps("psP1", [128, 512], F32, 2)
    psSA0 = ps("psSA0", [128, 512], F32, 0)
    psSA1 = ps("psSA1", [128, 512], F32, 1)
    psSB0 = ps("psSB0", [128, 512], F32, 2)
    psSB1 = ps("psSB1", [128, 512], F32, 3)
    psLA = ps("psLA", [128, 512], F32, 4)
    psOA = ps("psOA", [128, 512], F32, 5)
    psLB = ps("psLB", [128, 512], F32, 6)
    psOB = ps("psOB", [128, 512], F32, 7)
    psN0 = ps("psN0", [2, 1], F32, 4)
    psN1 = ps("psN1", [2, 1], F32, 5)
    psBC0 = ps("psBC0", [128, 1], F32, 6)
    psBC1 = ps("psBC1", [128, 1], F32, 7)
    psW0 = ps("psW0", [128, 512], F32, 0)
    psW1 = ps("psW1", [128, 512], F32, 1)
    psR = ps("psR", [128, 512], F32, 2)
    psT1 = ps("psT1", [1, 1], F32, 3)
    psTrC = ps("psTrC", [128, 1024], BF16, 0)
    psTrCf = ps("psTrCf", [128, 512], F32, 0)
    psG0 = ps("psG0", [128, 512], F32, 1)
    psG1 = ps("psG1", [128, 512], F32, 2)
    psU0 = ps("psU0", [128, 512], F32, 3)
    psU1 = ps("psU1", [128, 512], F32, 4)
    psD0 = ps("psD0", [128, 512], F32, 5)
    psD1 = ps("psD1", [128, 512], F32, 6)
    nc.psum_base = 8

    DMA_SEMS = ("dL", "dS", "dF", "dQ", "dQ2")
    sem_names = ["dL", "dS", "dF", "dQ", "dQ2", "tS", "vS", "scS", "gS"]
    sems = {n: nc.alloc_semaphore(n) for n in sem_names}
    cnt = {n: 0 for n in sem_names}
    OPS = {"sync": [], "tensor": [], "vector": [], "scalar": [], "gpsimd": []}

    def W(eng, sem, val):
        if val > 0:
            OPS[eng].append(("w", sem, val))

    def E(eng, fn, inc=None):
        n = 16 if inc in DMA_SEMS else 1
        OPS[eng].append(("e", fn, (inc, n) if inc else None))
        if inc:
            cnt[inc] += n

    _SELF = {"vector": "vS", "scalar": "scS", "gpsimd": "gS", "tensor": "tS"}

    def ECH(eng, fn):
        s = _SELF[eng]
        E(eng, fn, s)
        W(eng, s, cnt[s])

    for dst, src in [(ident, ident_c), (ident_f, identf_c), (onescol, onescol_c), (onescol_f, onescol_f_c),
                     (onesrow_f, onesrow_f_c), (blockones, blockones_c),
                     (selAB, selAB_c), (selEs, selE_c)]:
        E("sync", lambda e, d=dst, s=src: e.dma_start(out=d[:, :], in_=s.ap()), "dL")
    E("sync", lambda e: e.dma_start(out=masks[:, :], in_=dmasks), "dL")
    for t in range(NT_OWN):
        E("sync", lambda e, t=t: e.dma_start(out=X[:, t * D:(t + 1) * D],
                                             in_=dx_own[t * 128:(t + 1) * 128, :]), "dL")
    for ck in range(8):
        E("sync", lambda e, ck=ck: e.dma_start(out=wf_dq[:, ck * DC:(ck + 1) * DC],
                                               in_=dw_dq[ck * 128:(ck + 1) * 128, :]), "dL")
        E("sync", lambda e, ck=ck: e.dma_start(out=wf_dkv[:, ck * DC:(ck + 1) * DC],
                                               in_=dw_dkv[ck * 128:(ck + 1) * 128, :]), "dL")
    for cc in range(2):
        E("sync", lambda e, cc=cc: e.dma_start(out=wf_uq[:, cc * D:(cc + 1) * D],
                                               in_=dw_uq[cc * 128:(cc + 1) * 128, :]), "dL")
        E("sync", lambda e, cc=cc: e.dma_start(out=wf_uk[:, cc * D:(cc + 1) * D],
                                               in_=dw_uk[cc * 128:(cc + 1) * 128, :]), "dL")
        E("sync", lambda e, cc=cc: e.dma_start(out=wf_uv[:, cc * D:(cc + 1) * D],
                                               in_=dw_uv[cc * 128:(cc + 1) * 128, :]), "dL")
    dl_stage5 = cnt["dL"]
    W("vector", "dL", dl_stage5)
    E("vector", lambda e: e.tensor_copy(wdq_sb[:, :], wf_dq[:, :]), "vS")
    E("vector", lambda e: e.tensor_copy(wdkv_sb[:, :], wf_dkv[:, :]), "vS")
    E("vector", lambda e: e.tensor_copy(wuq_sb[:, :], wf_uq[:, :]), "vS")
    E("vector", lambda e: e.tensor_copy(wuk_sb[:, :], wf_uk[:, :]), "vS")
    E("vector", lambda e: e.tensor_copy(wuv_sb[:, :], wf_uv[:, :]), "vS")
    wo_cast = []
    for pc in range(16):
        ck, hh = pc // 2, (pc % 2) * 512
        if pc >= 2:
            W("sync", "vS", wo_cast[pc - 2])
        E("sync", lambda e, ck=ck, hh=hh, sl=pc % 2: e.dma_start(
            out=fstage[:, sl * 512:(sl + 1) * 512],
            in_=dw_o[ck * 128:(ck + 1) * 128, hh:hh + 512]), "dL")
        W("vector", "dL", cnt["dL"])
        E("vector", lambda e, ck=ck, hh=hh, sl=pc % 2: e.tensor_copy(
            wo_sb[:, ck * D + hh:ck * D + hh + 512], fstage[:, sl * 512:(sl + 1) * 512]), "vS")
        wo_cast.append(cnt["vS"])
    wcast_all = cnt["vS"]
    dl_head = cnt["dL"]

    h1evac = {}
    projevac = []
    xload = {}
    h1done = {}
    sqdone = {}
    tr_done = {}
    chain_done = {}
    npr = 0
    for g in range(12):
        own = g >= 8
        for j in range(4):
            ti = g * 4 + j
            if not own:
                prev = ti - 2
                if prev >= 0 and prev in h1done:
                    W("sync", "vS", h1done[prev])
                    W("sync", "scS", sqdone[prev])
                E("sync", lambda e, ti=ti, sl=ti % 2: e.dma_start(
                    out=xstage[:, sl * D:(sl + 1) * D],
                    in_=dx_all[ti * 128:(ti + 1) * 128, :]), "dL")
                xload[ti] = cnt["dL"]
                xap = xstage[:, (ti % 2) * D:((ti % 2) + 1) * D]
            else:
                xload[ti] = dl_head
                xap = X[:, (ti - 32) * D:(ti - 31) * D]
            W("vector", "dL", xload[ti])
            ECH("vector", lambda e, x=xap, j=j: e.reduce_sum(sums[:, j:j + 1], x, AX.X))
            W("scalar", "dL", xload[ti])
            if ti >= 1:
                W("scalar", "vS", h1done.get(ti - 1, 0))
            E("scalar", lambda e, x=xap, j=j: e.activation(
                scr4k[:, :], x, AF.Square, accum_out=sumsq[:, j:j + 1]), "scS")
            sqdone[ti] = cnt["scS"]
            W("vector", "scS", sqdone[ti])
            ECH("vector", lambda e, j=j: e.tensor_scalar(
                tmp4a[:, j:j + 1], sums[:, j:j + 1], 1.0 / D, None, ALU.mult))
            ECH("vector", lambda e, j=j: e.tensor_scalar(
                tmp4b[:, j:j + 1], tmp4a[:, j:j + 1], tmp4a[:, j:j + 1], -1.0,
                ALU.mult, ALU.mult))
            ECH("vector", lambda e, j=j: e.scalar_tensor_tensor(
                tmp4b[:, j:j + 1], sumsq[:, j:j + 1], 1.0 / D,
                tmp4b[:, j:j + 1], ALU.mult, ALU.add))
            E("vector", lambda e, j=j: e.tensor_scalar(
                rstds[:, j:j + 1], tmp4b[:, j:j + 1], LN_EPS, None, ALU.add), "vS")
            W("scalar", "vS", cnt["vS"])
            E("scalar", lambda e, j=j: e.activation(
                rstds[:, j:j + 1], rstds[:, j:j + 1], AF.Sqrt), "scS")
            W("vector", "scS", cnt["scS"])
            ECH("vector", lambda e, j=j: e.reciprocal(rstds[:, j:j + 1], rstds[:, j:j + 1]))
            ECH("vector", lambda e, j=j: e.tensor_scalar(
                negmr[:, j:j + 1], tmp4a[:, j:j + 1], rstds[:, j:j + 1], -1.0,
                ALU.mult, ALU.mult))
            sl = j % 2
            slot_prev = (g, j - 2) if j >= 2 else (g - 1, j + 2)
            if slot_prev in tr_done:
                W("vector", "tS", tr_done[slot_prev])
            E("vector", lambda e, x=xap, j=j, sl=sl: e.tensor_scalar(
                h1tok[:, sl * D:(sl + 1) * D], x, rstds[:, j:j + 1], negmr[:, j:j + 1],
                ALU.mult, ALU.add), "vS")
            h1done[ti] = cnt["vS"]
            W("tensor", "vS", h1done[ti])
            tr_prev = (g, j - 1) if j >= 1 else (g - 1, 3)
            if tr_prev in h1evac:
                W("tensor", "vS", h1evac[tr_prev])
            for ck in range(8):
                E("tensor", lambda e, sl=sl, ck=ck: e.transpose(
                    psTrA[:, ck * 128:(ck + 1) * 128],
                    h1tok[:, sl * D + ck * 128:sl * D + (ck + 1) * 128], ident[:, :]),
                    "tS" if ck == 7 else None)
            tr_done[(g, j)] = cnt["tS"]
            W("vector", "tS", tr_done[(g, j)])
            E("vector", lambda e, j=j: e.tensor_copy(
                h1T[:, :].rearrange("p (c t) -> p c t", c=8)[:, :, j * 128:(j + 1) * 128],
                psTrA[:, :].rearrange("p (c t) -> p c t", c=8)), "vS")
            h1evac[(g, j)] = cnt["vS"]
        wtile = wdq_sb if own else wdkv_sb
        dstT = cqT if own else ckvT
        gcol = (g - 8) * 512 if own else g * 512
        span = 2048 if own else SEQ
        for m in range(2):
            bank = [psP0, psP1][npr % 2]
            if npr >= 2:
                W("tensor", "vS", projevac[npr - 2])
            W("tensor", "vS", h1evac[(g, 3)])
            W("tensor", "vS", wcast_all)
            for ck in range(8):
                E("tensor", lambda e, b=bank, wt=wtile, ck=ck, m=m: e.matmul(
                    b[:, :], wt[:, ck * DC + m * 128:ck * DC + (m + 1) * 128],
                    h1T[:, ck * 512:(ck + 1) * 512],
                    start=(ck == 0), stop=(ck == 7)), "tS" if ck == 7 else None)
            W("vector", "tS", cnt["tS"])
            E("vector", lambda e, b=bank, d=dstT, m=m, gc=gcol, sp=span: e.tensor_copy(
                d[:, m * sp + gc:m * sp + gc + 512], b[:, :]), "vS")
            projevac.append(cnt["vS"])
            npr += 1
    phaseA_tS = cnt["tS"]
    phaseA_vS = cnt["vS"]

    def _dump(pairs):
        for nm, ap in pairs:
            dt_ = ap.dtype
            dten = nc.dram_tensor("dump_" + nm, list(ap.shape), dt_,
                                  kind="ExternalOutput").ap()
            W("gpsimd", "vS", cnt["vS"])
            W("gpsimd", "scS", cnt["scS"])
            W("gpsimd", "tS", cnt["tS"])
            E("gpsimd", lambda e, d=dten, s=ap: e.dma_start(out=d, in_=s), "dS")
        W("gpsimd", "dS", cnt["dS"])

    if dbg == "A":
        _dump([("ckvT", ckvT[:, :]), ("cqT", cqT[:, :]), ("wdq", wdq_sb[:, :]),
               ("wuk", wuk_sb[:, :]), ("wo", wo_sb[:, :]), ("X", X[:, 0:2048]),
               ("h1T", h1T[:, :]), ("h1tok", h1tok[:, :]), ("sums", sums[:, :]),
               ("sumsq", sumsq[:, :]), ("rstds", rstds[:, :]), ("negmr", negmr[:, :]),
               ("tmp4a", tmp4a[:, :]), ("tmp4b", tmp4b[:, :])])

    ffn_srcs = []
    for wi, dw in enumerate([dwg, dwu, dwd]):
        nrow = D if wi < 2 else INNER
        ncolw = INNER if wi < 2 else D
        idx = 0
        for ck in range(nrow // 128):
            for hh in range(0, ncolw, 512):
                ffn_srcs.append((wi, idx, dw[ck * 128:(ck + 1) * 128, hh:hh + 512]))
                idx += 1
    assert len(ffn_srcs) == 192
    ffn_ptr = [0]
    ffn_gdone = []

    def emit_ffn_mean(nsteps):
        for _ in range(nsteps):
            i = ffn_ptr[0]
            if i >= len(ffn_srcs):
                return
            wi, idx, src = ffn_srcs[i]
            slot = i % 2
            if i < 2:
                W("gpsimd", "vS", wcast_all)
            if i >= 2:
                W("gpsimd", "scS", ffn_gdone[i - 2])
            E("gpsimd", lambda e, s=src, sl=slot: e.dma_start(
                out=fstage[:, sl * 512:(sl + 1) * 512], in_=s), "dF")
            W("scalar", "dF", cnt["dF"])
            col = wi * 64 + idx
            E("scalar", lambda e, sl=slot, col=col: e.activation(
                fstage[:, sl * 512:(sl + 1) * 512], fstage[:, sl * 512:(sl + 1) * 512], AF.Abs,
                accum_out=wst[:, col:col + 1]), "scS")
            ffn_gdone.append(cnt["scS"])
            ffn_ptr[0] += 1

    if dbg != "A":
        u_evac = {}
        exp_cnt = {}
        mask_cnt = {}
        lav_cnt = {}
        s_cnt = {}
        last_exp_prev_hp = 0
        last_lav_prev = 0

        for hp in range(8):
            W("tensor", "scS", last_exp_prev_hp)
            if hp == 0:
                W("tensor", "vS", phaseA_vS)
            else:
                W("tensor", "vS", u_evac[(hp - 1, 3)])
            bev = []
            for nb in range(8):
                bank = [psSA0, psSA1][nb % 2]
                if nb >= 2:
                    W("tensor", "vS", bev[nb - 2])
                for cc in range(2):
                    E("tensor", lambda e, b=bank, cc=cc, hp=hp, nb=nb: e.matmul(
                        b[:, :], wuk_sb[:, cc * D + hp * 128:cc * D + (hp + 1) * 128],
                        ckvT[:, cc * SEQ + nb * 512:cc * SEQ + nb * 512 + 512],
                        start=(cc == 0), stop=(cc == 1)), "tS" if cc == 1 else None)
                W("vector", "tS", cnt["tS"])
                E("vector", lambda e, b=bank, nb=nb: e.tensor_copy(
                    KT_s[:, nb * 512:(nb + 1) * 512], b[:, :]), "vS")
                bev.append(cnt["vS"])
            for nb in range(4):
                bank = [psSB0, psSB1][nb % 2]
                if nb >= 2:
                    W("tensor", "vS", bev[8 + nb - 2])
                for cc in range(2):
                    E("tensor", lambda e, b=bank, cc=cc, hp=hp, nb=nb: e.matmul(
                        b[:, :], wuq_sb[:, cc * D + hp * 128:cc * D + (hp + 1) * 128],
                        cqT[:, cc * 2048 + nb * 512:cc * 2048 + nb * 512 + 512],
                        start=(cc == 0), stop=(cc == 1)), "tS" if cc == 1 else None)
                W("vector", "tS", cnt["tS"])
                E("vector", lambda e, b=bank, nb=nb: e.tensor_copy(
                    QT[:, nb * 512:(nb + 1) * 512], b[:, :]), "vS")
                bev.append(cnt["vS"])
            for kt in range(32):
                bank = [psLA, psOA][kt % 2]
                if kt >= 2:
                    W("tensor", "vS", bev[12 + kt - 2])
                for cc in range(2):
                    E("tensor", lambda e, b=bank, cc=cc, hp=hp, kt=kt: e.matmul(
                        b[:, 0:128], ckvT[:, cc * SEQ + kt * 128:cc * SEQ + (kt + 1) * 128],
                        wuv_sb[:, cc * D + hp * 128:cc * D + (hp + 1) * 128],
                        start=(cc == 0), stop=(cc == 1)), "tS" if cc == 1 else None)
                W("vector", "tS", cnt["tS"])
                E("vector", lambda e, b=bank, kt=kt: e.tensor_copy(
                    V_s[:, kt * 128:(kt + 1) * 128], b[:, 0:128]), "vS")
                bev.append(cnt["vS"])
            kv_built = cnt["vS"]

            W("scalar", "vS", kv_built)
            E("scalar", lambda e: e.activation(sqscr[:, :], QT[:, :], AF.Square), "scS")
            W("vector", "scS", cnt["scS"])
            E("vector", lambda e: e.reduce_max(tmp4a[:, 0:1], sqscr[:, :], AX.X), "vS")
            E("scalar", lambda e: e.activation(sqscr[:, :], KT_s[:, 0:2048], AF.Square), "scS")
            W("vector", "scS", cnt["scS"])
            E("vector", lambda e: e.reduce_max(tmp4a[:, 1:2], sqscr[:, :], AX.X), "vS")
            W("scalar", "vS", cnt["vS"])
            E("scalar", lambda e: e.activation(sqscr[:, :], KT_s[:, 2048:4096], AF.Square), "scS")
            W("vector", "scS", cnt["scS"])
            ECH("vector", lambda e: e.reduce_max(tmp4b[:, 0:1], sqscr[:, :], AX.X))
            E("vector", lambda e: e.tensor_max(tmp4a[:, 1:2], tmp4a[:, 1:2], tmp4b[:, 0:1]), "vS")
            W("tensor", "vS", cnt["vS"])
            E("tensor", lambda e: e.matmul(psN0[:, :], blockones[:, :], tmp4a[:, 0:1],
                                           start=True, stop=True), "tS")
            E("tensor", lambda e: e.matmul(psN1[:, :], blockones[:, :], tmp4a[:, 1:2],
                                           start=True, stop=True), "tS")
            W("vector", "tS", cnt["tS"])
            E("vector", lambda e: e.tensor_copy(qk_sb[:, 0:1], psN0[:, :]), "vS")
            E("vector", lambda e: e.tensor_copy(qk_sb[:, 1:2], psN1[:, :]), "vS")
            W("vector", "vS", cnt["vS"])
            W("scalar", "vS", cnt["vS"])
            E("scalar", lambda e: e.activation(qk_sb[:, :], qk_sb[:, :], AF.Sqrt), "scS")
            W("vector", "scS", cnt["scS"])
            ECH("vector", lambda e: e.tensor_scalar(qk2_sb[:, :], qk_sb[:, 0:1], qk_sb[:, 1:2],
                                                    C_SAFETY, ALU.mult, ALU.mult))
            W("tensor", "vS", cnt["vS"])
            E("tensor", lambda e: e.matmul(psBC0[:, :], selAB[:, 0:128], qk2_sb[:, :],
                                           start=True, stop=True), "tS")
            E("tensor", lambda e: e.matmul(psBC1[:, :], selAB[:, 128:256], qk2_sb[:, :],
                                           start=True, stop=True), "tS")
            W("vector", "tS", cnt["tS"])
            E("vector", lambda e: e.tensor_copy(biasA[:, :], psBC0[:, :]), "vS")
            E("vector", lambda e: e.tensor_copy(biasB[:, :], psBC1[:, :]), "vS")
            bias_ready = cnt["vS"]

            emit_ffn_mean(24)

            for u in range(4):
                KT = KTU[u]
                if u == 0:
                    W("tensor", "vS", bias_ready)
                else:
                    W("tensor", "vS", u_evac[(hp, u - 1)])

                def emit_S(kt, hp=hp, u=u, KT=KT, bias_ready=bias_ready,
                           last_lav_prev=last_lav_prev):
                    o_ = 0 if kt < KT - 4 else 128 * (kt - (KT - 4))
                    cols = 512 - o_
                    slot = kt % 2
                    pA = [psSA0, psSA1][kt % 2]
                    pB = [psSB0, psSB1][kt % 2]
                    if kt >= 1:
                        W("tensor", "scS", exp_cnt[(hp, u, kt - 1)])
                        if (hp, u, kt - 1) in mask_cnt:
                            W("tensor", "gS", mask_cnt[(hp, u, kt - 1)])
                    elif u > 0:
                        W("tensor", "scS", exp_cnt[(hp, u - 1, KTU[u - 1] - 2 + kt)])
                    elif u == 0:
                        W("tensor", "vS", bias_ready)
                    E("tensor", lambda e, p=pA, o_=o_, kt=kt, u=u: e.matmul(
                        p[:, o_:512], KT_s[0:64, kt * 128:(kt + 1) * 128],
                        QT[0:64, u * 512 + o_:(u + 1) * 512],
                        start=True, stop=True, tile_position=(0, 0)), "tS")
                    E("tensor", lambda e, p=pB, o_=o_, kt=kt, u=u: e.matmul(
                        p[:, o_:512], KT_s[64:128, kt * 128:(kt + 1) * 128],
                        QT[64:128, u * 512 + o_:(u + 1) * 512],
                        start=True, stop=True, tile_position=(64, 0)), "tS")
                    s_cnt[(hp, u, kt)] = cnt["tS"]
                    W("scalar", "tS", s_cnt[(hp, u, kt)])
                    if kt >= 2:
                        W("scalar", "tS", lav_cnt[(hp, u, kt - 2)])
                    else:
                        W("scalar", "tS", last_lav_prev)
                    W("scalar", "vS", bias_ready)
                    E("scalar", lambda e, p=pA, o_=o_, c=cols, sl=slot: e.activation(
                        PT[:, sl * 1024:sl * 1024 + c], p[:, o_:512], AF.Exp,
                        bias=biasA[:, :], scale=0.125), "scS")
                    E("scalar", lambda e, p=pB, o_=o_, c=cols, sl=slot: e.activation(
                        PT[:, sl * 1024 + 512:sl * 1024 + 512 + c], p[:, o_:512], AF.Exp,
                        bias=biasB[:, :], scale=0.125), "scS")
                    exp_cnt[(hp, u, kt)] = cnt["scS"]
                    if kt >= KT - 8:
                        jp = kt - (KT - 8)
                        if jp < 4:
                            mc0, mw, moff = 0, 128 * (jp + 1), [0, 128, 384, 768][jp]
                        else:
                            mc0, mw = 0, 512 - o_
                            moff = [1280, 1792, 2176, 2432][jp - 4]
                        W("gpsimd", "scS", exp_cnt[(hp, u, kt)])
                        E("gpsimd", lambda e, sl=slot, mc0=mc0, mw=mw, moff=moff: e.tensor_mul(
                            PT[:, sl * 1024 + mc0:sl * 1024 + mc0 + mw],
                            PT[:, sl * 1024 + mc0:sl * 1024 + mc0 + mw],
                            masks[:, moff:moff + mw]), "gS")
                        E("gpsimd", lambda e, sl=slot, mc0=mc0, mw=mw, moff=moff: e.tensor_mul(
                            PT[:, sl * 1024 + 512 + mc0:sl * 1024 + 512 + mc0 + mw],
                            PT[:, sl * 1024 + 512 + mc0:sl * 1024 + 512 + mc0 + mw],
                            masks[:, moff:moff + mw]), "gS")
                        mask_cnt[(hp, u, kt)] = cnt["gS"]

                def emit_lav(kt, hp=hp, u=u, KT=KT, waits=True):
                    o_ = 0 if kt < KT - 4 else 128 * (kt - (KT - 4))
                    cols = 512 - o_
                    slot = kt % 2
                    if waits:
                        W("tensor", "scS", exp_cnt[(hp, u, kt)])
                        if (hp, u, kt) in mask_cnt:
                            W("tensor", "gS", mask_cnt[(hp, u, kt)])
                    st_, sp_ = (kt == 0), (kt == KT - 1)
                    E("tensor", lambda e, sl=slot, o_=o_, c=cols, st_=st_, sp_=sp_: e.matmul(
                        psLA[0:1, o_:512], onescol[:, 0:1], PT[:, sl * 1024:sl * 1024 + c],
                        start=st_, stop=sp_, tile_position=(0, 0)), "tS")
                    E("tensor", lambda e, sl=slot, o_=o_, c=cols, st_=st_, sp_=sp_: e.matmul(
                        psLB[32:33, o_:512], onescol[:, 1:2],
                        PT[:, sl * 1024 + 512:sl * 1024 + 512 + c],
                        start=st_, stop=sp_, tile_position=(0, 32)), "tS")
                    E("tensor", lambda e, sl=slot, o_=o_, kt=kt, c=cols, st_=st_, sp_=sp_: e.matmul(
                        psOA[0:64, o_:512], V_s[:, kt * 128:kt * 128 + 64],
                        PT[:, sl * 1024:sl * 1024 + c],
                        start=st_, stop=sp_, tile_position=(0, 0)), "tS")
                    E("tensor", lambda e, sl=slot, o_=o_, kt=kt, c=cols, st_=st_, sp_=sp_: e.matmul(
                        psOB[64:128, o_:512], V_s[:, kt * 128 + 64:(kt + 1) * 128],
                        PT[:, sl * 1024 + 512:sl * 1024 + 512 + c],
                        start=st_, stop=sp_, tile_position=(0, 64)), "tS")
                    lav_cnt[(hp, u, kt)] = cnt["tS"]

                for kt in range(KT):
                    emit_S(kt)
                    if kt >= 1:
                        emit_lav(kt - 1, waits=False)
                emit_lav(KT - 1)

                last_lav_prev = lav_cnt[(hp, u, KT - 1)]
                W("vector", "tS", last_lav_prev)
                W("vector", "dS", cnt["dS"])
                E("vector", lambda e, hp=hp, u=u: e.tensor_copy(
                    OT[0:64, hp * 2048 + u * 512:hp * 2048 + (u + 1) * 512], psOA[0:64, :]), "vS")
                E("vector", lambda e, hp=hp, u=u: e.tensor_copy(
                    OT[64:128, hp * 2048 + u * 512:hp * 2048 + (u + 1) * 512], psOB[64:128, :]), "vS")
                E("vector", lambda e: e.tensor_copy(ltmp[0:1, :], psLA[0:1, :]), "vS")
                E("vector", lambda e: e.tensor_copy(ltmp[32:33, :], psLB[32:33, :]), "vS")
                if dbg == "B" and hp == 7 and u == 3:
                    E("vector", lambda e: e.tensor_copy(dbgb[:, 0:512], psLA[:, :]), "vS")
                    E("vector", lambda e: e.tensor_copy(dbgb[:, 512:1024], psOA[:, :]), "vS")
                u_evac[(hp, u)] = cnt["vS"]
                W("gpsimd", "vS", u_evac[(hp, u)])
                E("gpsimd", lambda e, hp=hp, u=u: e.dma_start(
                    out=l_all[2 * hp:2 * hp + 1, u * 512:(u + 1) * 512], in_=ltmp[0:1, :]), "dS")
                E("gpsimd", lambda e, hp=hp, u=u: e.dma_start(
                    out=l_all[2 * hp + 1:2 * hp + 2, u * 512:(u + 1) * 512],
                    in_=ltmp[32:33, :]), "dS")
            last_exp_prev_hp = exp_cnt[(hp, 3, KTU[3] - 1)]

        W("vector", "dS", cnt["dS"])
        E("vector", lambda e: e.reciprocal(l_all[:, :], l_all[:, :]), "vS")
        W("tensor", "scS", cnt["scS"])
        W("tensor", "vS", cnt["vS"])
        wo_ev = {}
        for u in range(4):
            for hp in range(8):
                if (u, hp) != (0, 0):
                    W("tensor", "vS", cnt["vS"])
                E("tensor", lambda e, hp=hp, u=u: e.matmul(
                    psR[:, :], selEs[:, hp * 128:(hp + 1) * 128],
                    l_all[:, u * 512:(u + 1) * 512], start=True, stop=True), "tS")
                W("vector", "tS", cnt["tS"])
                E("vector", lambda e, hp=hp, u=u: e.tensor_mul(
                    OT[:, hp * 2048 + u * 512:hp * 2048 + (u + 1) * 512],
                    OT[:, hp * 2048 + u * 512:hp * 2048 + (u + 1) * 512], psR[:, :]), "vS")
            norm_u = cnt["vS"]
            for tt in range(4):
                ti = u * 4 + tt
                for nbk in range(2):
                    bank = [psW0, psW1][nbk]
                    pk = u * 8 + tt * 2 + nbk - 2
                    if pk >= 0:
                        W("tensor", "vS", wo_ev[pk])
                    W("tensor", "vS", norm_u)
                    for hp in range(8):
                        E("tensor", lambda e, b=bank, hp=hp, u=u, tt=tt, nbk=nbk: e.matmul(
                            b[:, :],
                            OT[:, hp * 2048 + u * 512 + tt * 128:hp * 2048 + u * 512 + (tt + 1) * 128],
                            wo_sb[:, hp * D + nbk * 512:hp * D + nbk * 512 + 512],
                            start=(hp == 0), stop=(hp == 7)), "tS" if hp == 7 else None)
                    W("vector", "tS", cnt["tS"])
                    E("vector", lambda e, b=bank, ti=ti, nbk=nbk: e.scalar_tensor_tensor(
                        X[:, ti * D + nbk * 512:ti * D + nbk * 512 + 512], b[:, :], 1.0,
                        X[:, ti * D + nbk * 512:ti * D + nbk * 512 + 512],
                        ALU.mult, ALU.add), "vS")
                    wo_ev[u * 8 + tt * 2 + nbk] = cnt["vS"]
        tB3 = cnt["tS"]
        vB3 = cnt["vS"]


        if dbg == "B":
            _dump([("X", X[:, :]), ("l_all", l_all[:, :]), ("OT", OT[:, :]),
                   ("biasA", biasA[:, :]), ("biasB", biasB[:, :]),
                   ("KT", KT_s[:, :]), ("QT", QT[:, :]), ("V", V_s[:, :]),
                   ("PT", PT[:, :]), ("qk", qk_sb[:, :]), ("qk2", qk2_sb[:, :]),
                   ("dbgb", dbgb[:, :])])
    if dbg not in ("A", "B"):
        emit_ffn_mean(len(ffn_srcs))
        gMean = cnt["scS"]
        W("vector", "scS", gMean)
        W("vector", "tS", tB3)
        for wi in range(3):
            E("vector", lambda e, wi=wi: e.reduce_sum(
                wtot[:, wi:wi + 1], wst[:, wi * 64:(wi + 1) * 64], AX.X), "vS")
        W("tensor", "scS", cnt["scS"])
        W("tensor", "vS", cnt["vS"])
        for wi in range(3):
            if wi > 0:
                W("tensor", "vS", cnt["vS"])
            E("tensor", lambda e, wi=wi: e.matmul(
                psT1[:, :], onescol_f[:, :], wtot[:, wi:wi + 1], start=True, stop=True), "tS")
            W("vector", "tS", cnt["tS"])
            E("vector", lambda e, wi=wi: e.tensor_copy(wsc[:, wi:wi + 1], psT1[:, :]), "vS")
        W("vector", "vS", cnt["vS"])
        ECH("vector", lambda e: e.tensor_scalar(wsc[:, 3:6], wsc[:, 0:3], 1.0 / (D * INNER), 1e-5,
                                                ALU.mult, ALU.max))
        ECH("vector", lambda e: e.reciprocal(wsc[:, 0:3], wsc[:, 3:6]))
        ECH("vector", lambda e: e.tensor_scalar(wsc[:, 6:8], wsc[:, 3:5], 1.0 / 127, None, ALU.mult))
        ECH("vector", lambda e: e.tensor_scalar(wsc[:, 5:6], wsc[:, 5:6], 1.0 / 127, None, ALU.mult))
        W("tensor", "vS", cnt["vS"])
        for wi in range(3):
            if wi > 0:
                W("tensor", "vS", cnt["vS"])
            E("tensor", lambda e, wi=wi: e.matmul(psBC0[:, :], onesrow_f[:, :],
                                                  wsc[0:1, wi:wi + 1], start=True, stop=True), "tS")
            W("vector", "tS", cnt["tS"])
            E("vector", lambda e, wi=wi: e.tensor_copy(swv[:, wi:wi + 1], psBC0[:, :]), "vS")
        for wi, scol in enumerate([6, 7, 5]):
            if wi > 0:
                W("tensor", "vS", cnt["vS"])
            E("tensor", lambda e, scol=scol: e.matmul(psBC1[:, :], onesrow_f[:, :],
                                                      wsc[0:1, scol:scol + 1],
                                                      start=True, stop=True), "tS")
            W("vector", "tS", cnt["tS"])
            E("vector", lambda e, wi=wi: e.tensor_copy(mwv[:, wi:wi + 1], psBC1[:, :]), "vS")
        wsc_ready = cnt["vS"]

        qdst = {0: wgq, 1: wuqq, 2: wdqq}
        qdone = []
        W("gpsimd", "tS", tB3)
        for i, (wi, idx, src) in enumerate(ffn_srcs):
            slot = i % 8
            lq = "sync" if i % 2 == 0 else "scalar"
            dsem = "dQ" if i % 2 == 0 else "dQ2"
            if i < 8:
                W(lq, "scS", gMean)
                W(lq, "vS", wsc_ready)
                W(lq, "tS", tB3)
            else:
                W(lq, "vS" if (i - 8) % 2 == 0 else "gS", qdone[i - 8])
            E(lq, lambda e, s=src, sl=slot: e.dma_start(
                out=fring[:, sl * 512:(sl + 1) * 512], in_=s), dsem)
            eng = "vector" if i % 2 == 0 else "gpsimd"
            sem = "vS" if i % 2 == 0 else "gS"
            W(eng, dsem, cnt[dsem])
            W(eng, "vS", wsc_ready)
            if wi < 2:
                dcol = (idx // 8) * INNER + (idx % 8) * 512
            else:
                dcol = (idx // 2) * D + (idx % 2) * 512
            E(eng, lambda e, sl=slot, wi=wi: e.tensor_scalar(
                fring[:, sl * 512:(sl + 1) * 512], fring[:, sl * 512:(sl + 1) * 512],
                swv[:, wi:wi + 1], 1.0, ALU.mult, ALU.min))
            E(eng, lambda e, sl=slot: e.tensor_scalar(
                fring[:, sl * 512:(sl + 1) * 512], fring[:, sl * 512:(sl + 1) * 512],
                -1.0, MAGIC, ALU.max, ALU.add))
            E(eng, lambda e, sl=slot, wi=wi, dcol=dcol: e.tensor_scalar(
                qdst[wi][:, dcol:dcol + 512], fring[:, sl * 512:(sl + 1) * 512],
                MAGIC, None, ALU.subtract), sem)
            qdone.append(cnt[sem])
        quant_v = cnt["vS"]
        quant_g = cnt["gS"]


        if dbg == "C0":
            _dump([("wgq", wgq[:, :]), ("wuqq", wuqq[:, :]), ("wdqq", wdqq[:, :]),
                   ("wsc", wsc[:, :]), ("swv", swv[:, :]), ("mwv", mwv[:, :]),
                   ("wst", wst[:, :]), ("wtot", wtot[:, :])])
    if dbg not in ("A", "B", "C0"):
        tile_done = {}
        for ti in range(NT_OWN):
            xap = X[:, ti * D:(ti + 1) * D]
            if ti == 0:
                W("vector", "gS", quant_g)
                W("vector", "tS", tB3)
                W("scalar", "tS", tB3)
            E("vector", lambda e, x=xap: e.reduce_sum(sums[:, 0:1], x, AX.X), "vS")
            W("scalar", "vS", cnt["vS"])
            E("scalar", lambda e, x=xap: e.activation(
                scr4k[:, :], x, AF.Square, accum_out=sumsq[:, 0:1]), "scS")
            W("vector", "scS", cnt["scS"])
            ECH("vector", lambda e: e.tensor_scalar(tmp4a[:, 0:1], sums[:, 0:1], 1.0 / D, None, ALU.mult))
            ECH("vector", lambda e: e.tensor_scalar(tmp4b[:, 0:1], tmp4a[:, 0:1], tmp4a[:, 0:1], -1.0, ALU.mult, ALU.mult))
            ECH("vector", lambda e: e.scalar_tensor_tensor(tmp4b[:, 0:1], sumsq[:, 0:1], 1.0 / D,
                                                         tmp4b[:, 0:1], ALU.mult, ALU.add))
            E("vector", lambda e: e.tensor_scalar(
                rstds[:, 0:1], tmp4b[:, 0:1], LN_EPS, None, ALU.add), "vS")
            W("scalar", "vS", cnt["vS"])
            E("scalar", lambda e: e.activation(rstds[:, 0:1], rstds[:, 0:1], AF.Sqrt), "scS")
            W("vector", "scS", cnt["scS"])
            ECH("vector", lambda e: e.reciprocal(rstds[:, 0:1], rstds[:, 0:1]))
            ECH("vector", lambda e: e.tensor_scalar(negmr[:, 0:1], tmp4a[:, 0:1], rstds[:, 0:1],
                                                  -1.0, ALU.mult, ALU.mult))
            E("vector", lambda e, x=xap: e.tensor_scalar(
                scr4k[:, :], x, rstds[:, 0:1], negmr[:, 0:1], ALU.mult, ALU.add))
            ECH("vector", lambda e: e.tensor_reduce(am_sb[:, 0:1], scr4k[:, :], AX.X, ALU.max,
                                                  apply_absolute_value=True))
            ECH("vector", lambda e: e.tensor_scalar(am_sb[:, 0:1], am_sb[:, 0:1], 1e-5, None, ALU.max))
            ECH("vector", lambda e: e.reciprocal(sa_sb[:, 0:1], am_sb[:, 0:1]))
            ECH("vector", lambda e: e.tensor_scalar(sa_sb[:, 0:1], sa_sb[:, 0:1], 127.0, None, ALU.mult))
            E("vector", lambda e: e.tensor_mul(inv_sb[:, 0:1], am_sb[:, 0:1], mwv[:, 0:1]))
            E("vector", lambda e: e.tensor_mul(inv_sb[:, 1:2], am_sb[:, 0:1], mwv[:, 1:2]))
            E("vector", lambda e: e.tensor_scalar(scr4k[:, :], scr4k[:, :], sa_sb[:, 0:1], MAGIC,
                                                  ALU.mult, ALU.add))
            E("vector", lambda e: e.tensor_scalar(qh2_sb[:, :], scr4k[:, :], MAGIC, None,
                                                  ALU.subtract), "vS")
            qh2_done = cnt["vS"]
            W("tensor", "vS", qh2_done)
            if ti > 0:
                W("tensor", "vS", tile_done[ti - 1])
            else:
                W("tensor", "vS", quant_v)
                W("tensor", "gS", quant_g)
            for ck in range(8):
                E("tensor", lambda e, ck=ck: e.transpose(
                    psTrC[:, ck * 128:(ck + 1) * 128], qh2_sb[:, ck * 128:(ck + 1) * 128],
                    ident[:, :]), "tS" if ck == 7 else None)
            W("vector", "tS", cnt["tS"])
            E("vector", lambda e: e.tensor_copy(qh2T[:, :], psTrC[:, :]), "vS")
            qh2T_done = cnt["vS"]
            gu_done = []
            for ib in range(8):
                pG = [psG0, psG1][ib % 2]
                pU = [psU0, psU1][ib % 2]
                W("tensor", "vS", qh2T_done)
                if ib >= 2:
                    W("tensor", "vS", gu_done[ib - 2][1])
                    W("tensor", "scS", gu_done[ib - 2][0])
                for ck in range(8):
                    E("tensor", lambda e, p=pG, ck=ck, ib=ib: e.matmul(
                        p[:, :], qh2T[:, ck * 128:(ck + 1) * 128],
                        wgq[:, ck * INNER + ib * 512:ck * INNER + ib * 512 + 512],
                        start=(ck == 0), stop=(ck == 7)), "tS" if ck == 7 else None)
                for ck in range(8):
                    E("tensor", lambda e, p=pU, ck=ck, ib=ib: e.matmul(
                        p[:, :], qh2T[:, ck * 128:(ck + 1) * 128],
                        wuqq[:, ck * INNER + ib * 512:ck * INNER + ib * 512 + 512],
                        start=(ck == 0), stop=(ck == 7)), "tS" if ck == 7 else None)
                mm_done = cnt["tS"]
                W("scalar", "tS", mm_done)
                W("scalar", "vS", cnt["vS"])
                E("scalar", lambda e, p=pG: e.activation(
                    gsil[:, :], p[:, :], AF.Silu, scale=inv_sb[:, 0:1]), "scS")
                W("vector", "scS", cnt["scS"])
                E("vector", lambda e, p=pU, ib=ib: e.scalar_tensor_tensor(
                    gu_sb[:, ib * 512:(ib + 1) * 512], p[:, :], inv_sb[:, 1:2], gsil[:, :],
                    ALU.mult, ALU.mult), "vS")
                gu_done.append((cnt["scS"], cnt["vS"]))
            ECH("vector", lambda e: e.tensor_reduce(am_sb[:, 1:2], gu_sb[:, :], AX.X, ALU.max,
                                                  apply_absolute_value=True))
            ECH("vector", lambda e: e.tensor_scalar(am_sb[:, 1:2], am_sb[:, 1:2], 1e-5, None, ALU.max))
            ECH("vector", lambda e: e.reciprocal(sa_sb[:, 1:2], am_sb[:, 1:2]))
            ECH("vector", lambda e: e.tensor_scalar(sa_sb[:, 1:2], sa_sb[:, 1:2], 127.0, None, ALU.mult))
            E("vector", lambda e: e.tensor_mul(inv_sb[:, 2:3], am_sb[:, 1:2], mwv[:, 2:3]))
            E("vector", lambda e: e.tensor_scalar(gu_sb[:, :], gu_sb[:, :], sa_sb[:, 1:2], MAGIC,
                                                  ALU.mult, ALU.add))
            E("vector", lambda e: e.tensor_scalar(gu_sb[:, :], gu_sb[:, :], MAGIC, None,
                                                  ALU.subtract), "vS")
            qgu_done = cnt["vS"]
            ev_rounds = []
            for rr in range(8):
                W("tensor", "vS", qgu_done)
                if rr > 0:
                    W("tensor", "vS", ev_rounds[rr - 1])
                for ck in range(4):
                    E("tensor", lambda e, rr=rr, ck=ck: e.transpose(
                        psTrCf[:, ck * 128:(ck + 1) * 128],
                        gu_sb[:, rr * 512 + ck * 128:rr * 512 + (ck + 1) * 128],
                        ident_f[:, :]), "tS" if ck == 3 else None)
                W("vector", "tS", cnt["tS"])
                E("vector", lambda e, rr=rr: e.tensor_copy(
                    qguT[:, rr * 512:(rr + 1) * 512], psTrCf[:, :]), "vS")
                ev_rounds.append(cnt["vS"])
            for nbk in range(2):
                pD = [psD0, psD1][nbk]
                W("tensor", "vS", ev_rounds[7])
                if ti > 0 and nbk == 0:
                    W("tensor", "vS", tile_done[ti - 1])
                for ic in range(32):
                    E("tensor", lambda e, p=pD, ic=ic, nbk=nbk: e.matmul(
                        p[:, :], qguT[:, ic * 128:(ic + 1) * 128],
                        wdqq[:, ic * D + nbk * 512:ic * D + nbk * 512 + 512],
                        start=(ic == 0), stop=(ic == 31)), "tS" if ic == 31 else None)
                W("vector", "tS", cnt["tS"])
                E("vector", lambda e, p=pD, ti=ti, nbk=nbk: e.scalar_tensor_tensor(
                    X[:, ti * D + nbk * 512:ti * D + nbk * 512 + 512], p[:, :], inv_sb[:, 2:3],
                    X[:, ti * D + nbk * 512:ti * D + nbk * 512 + 512], ALU.mult, ALU.add), "vS")
            tile_done[ti] = cnt["vS"]
            W("gpsimd", "vS", tile_done[ti])
            E("gpsimd", lambda e, ti=ti: e.dma_start(
                out=dout[ti * 128:(ti + 1) * 128, :], in_=X[:, ti * D:(ti + 1) * D]), "dS")
            if dbg == "C1" and ti == 0:
                _dump([("qh2", qh2_sb[:, :]), ("qh2T", qh2T[:, :]), ("qguT", qguT[:, :]),
                       ("am", am_sb[:, :]), ("sa", sa_sb[:, :]), ("inv", inv_sb[:, :]),
                       ("X0", X[:, 0:1024]), ("gsil", gsil[:, :]), ("qgu", gu_sb[:, :]),
                       ("rstds", rstds[:, :]), ("negmr", negmr[:, :])])
                break

        W("gpsimd", "dS", cnt["dS"])

    with nc.Block() as block:
        def mk(engname):
            def run(eng):
                for op in OPS[engname]:
                    if op[0] == "w":
                        eng.wait_ge(sems[op[1]], op[2])
                    else:
                        ins = op[1](eng)
                        if op[2] is not None:
                            ins.then_inc(sems[op[2][0]], op[2][1])
            return run
        block.sync(mk("sync"))
        block.tensor(mk("tensor"))
        block.vector(mk("vector"))
        block.scalar(mk("scalar"))
        block.gpsimd(mk("gpsimd"))
    return nc


def _masks_np(half):
    m = np.zeros((128, 2560), dtype=np.float32)
    ii = np.arange(128)[:, None]
    offs03 = [(0, 128), (128, 256), (384, 384), (768, 512)]
    for jp, (o, w) in enumerate(offs03):
        if half == 0:
            m[:, o:o + w] = 1.0
        else:
            jj = np.arange(w)[None, :]
            m[:, o:o + w] = (jj >= 128 * jp + ii).astype(np.float32)
    offs47 = [(1280, 512), (1792, 384), (2176, 256), (2432, 128)]
    for j4, (o, w) in enumerate(offs47):
        if half == 1:
            m[:, o:o + w] = 0.0
        else:
            jj = np.arange(w)[None, :]
            m[:, o:o + w] = (jj >= ii).astype(np.float32)
    return np.asarray(m, dtype=ml_dtypes.bfloat16)


def kernel(L0_attn_x, L0_attn_W_dq, L0_attn_W_uq, L0_attn_W_dkv, L0_attn_W_uk,
           L0_attn_W_uv, L0_attn_W_o, L0_ffn_w_gate, L0_ffn_w_up, L0_ffn_w_down):
    if "nc" not in _CACHE:
        _CACHE["nc"] = _build_nc()
    nc = _CACHE["nc"]

    x = np.ascontiguousarray(np.asarray(L0_attn_x, dtype=np.float32))
    shared = {
        "w_dq": np.ascontiguousarray(np.asarray(L0_attn_W_dq, dtype=np.float32)),
        "w_dkv": np.ascontiguousarray(np.asarray(L0_attn_W_dkv, dtype=np.float32)),
        "w_uq": np.ascontiguousarray(np.asarray(L0_attn_W_uq, dtype=np.float32)),
        "w_uk": np.ascontiguousarray(np.asarray(L0_attn_W_uk, dtype=np.float32)),
        "w_uv": np.ascontiguousarray(np.asarray(L0_attn_W_uv, dtype=np.float32)),
        "w_o": np.ascontiguousarray(np.asarray(L0_attn_W_o, dtype=np.float32)),
        "wg_t": np.ascontiguousarray(np.asarray(L0_ffn_w_gate, dtype=np.float32).T),
        "wu_t": np.ascontiguousarray(np.asarray(L0_ffn_w_up, dtype=np.float32).T),
        "wd_t": np.ascontiguousarray(np.asarray(L0_ffn_w_down, dtype=np.float32).T),
    }
    blocks_of = {0: [1, 3, 5, 7], 1: [0, 2, 4, 6]}
    in_maps = []
    for c in range(8):
        b, h = c >> 1, c & 1
        im = dict(shared)
        im["x_own"] = np.ascontiguousarray(
            np.concatenate([x[b, 512 * k:512 * (k + 1)] for k in blocks_of[h]], axis=0))
        im["x_all"] = np.ascontiguousarray(x[b])
        im["masks"] = _masks_np(h)
        in_maps.append(im)

    res = run_bass_kernel_spmd(nc, in_maps, core_ids=list(range(8)))
    out = np.empty((NBATCH, SEQ, D), dtype=np.float32)
    for c in range(8):
        b, h = c >> 1, c & 1
        r = np.asarray(res.results[c]["out"])
        for j, k in enumerate(blocks_of[h]):
            out[b, 512 * k:512 * (k + 1)] = r[512 * j:512 * (j + 1)]
    return out


# version 10
# speedup vs baseline: 1.4685x; 1.4685x over previous
import numpy as np
import ml_dtypes
import concourse.bass as bass
import concourse.mybir as mybir
from concourse.bass_utils import run_bass_kernel_spmd

F32 = mybir.dt.float32
BF16 = mybir.dt.bfloat16
FP8 = mybir.dt.float8e4
AF = mybir.ActivationFunctionType
ALU = mybir.AluOpType
AX = mybir.AxisListType

D = 1024
SEQ = 4096
NBATCH = 4
DC = 256
INNER = 4096
NT_OWN = 16
NT_ALL = 32
KTU = [8, 16, 24, 32]
MAGIC = 12582912.0
LN_EPS = 1e-5
C_SAFETY = -0.1275

_CACHE = {}


def _build_nc(dbg=None):
    nc = bass.Bass("TRN2", target_bir_lowering=False)

    dx_own = nc.dram_tensor("x_own", [2048, D], F32, kind="ExternalInput").ap()
    dx_all = nc.dram_tensor("x_all", [SEQ, D], F32, kind="ExternalInput").ap()
    dw_dq = nc.dram_tensor("w_dq", [D, DC], F32, kind="ExternalInput").ap()
    dw_dkv = nc.dram_tensor("w_dkv", [D, DC], F32, kind="ExternalInput").ap()
    dw_uq = nc.dram_tensor("w_uq", [DC, D], F32, kind="ExternalInput").ap()
    dw_uk = nc.dram_tensor("w_uk", [DC, D], F32, kind="ExternalInput").ap()
    dw_uv = nc.dram_tensor("w_uv", [DC, D], F32, kind="ExternalInput").ap()
    dw_o = nc.dram_tensor("w_o", [D, D], F32, kind="ExternalInput").ap()
    dwg = nc.dram_tensor("wg_t", [D, INNER], F32, kind="ExternalInput").ap()
    dwu = nc.dram_tensor("wu_t", [D, INNER], F32, kind="ExternalInput").ap()
    dwd = nc.dram_tensor("wd_t", [INNER, D], F32, kind="ExternalInput").ap()
    dmasks = nc.dram_tensor("masks", [128, 2560], BF16, kind="ExternalInput").ap()
    dout = nc.dram_tensor("out", [2048, D], F32, kind="ExternalOutput").ap()

    ident_c = nc.inline_tensor(np.eye(128, dtype=ml_dtypes.bfloat16), "ident_c")
    identf_c = nc.inline_tensor(np.eye(128, dtype=np.float32), "identf_c")
    onescol_c = nc.inline_tensor(np.ones((128, 2), dtype=ml_dtypes.bfloat16), "onescol_c")
    onescol_f_c = nc.inline_tensor(np.ones((128, 1), dtype=np.float32), "onescol_f_c")
    onesrow_f_c = nc.inline_tensor(np.ones((1, 128), dtype=np.float32), "onesrow_f_c")
    bo = np.zeros((128, 2), dtype=np.float32)
    bo[0:64, 0] = 1.0
    bo[64:128, 1] = 1.0
    blockones_c = nc.inline_tensor(bo, "blockones_c")
    sel = np.zeros((2, 256), dtype=np.float32)
    sel[0, 0:128] = 1.0
    sel[1, 128:256] = 1.0
    selAB_c = nc.inline_tensor(sel, "selAB_c")
    selE = np.zeros((16, 8 * 128), dtype=np.float32)
    for hp in range(8):
        selE[2 * hp, hp * 128:hp * 128 + 64] = 1.0
        selE[2 * hp + 1, hp * 128 + 64:hp * 128 + 128] = 1.0
    selE_c = nc.inline_tensor(selE, "selE_c")

    base0 = (nc.sbuf_base + 31) // 32 * 32
    top = nc.sbuf_top // 32 * 32
    cur = [base0]

    sboff = {}

    def sb(name, shape, dtype):
        o = cur[0]
        t = nc.alloc_sbuf_tensor_at(name, shape, dtype, offset=o)
        sboff[name] = o
        nbytes = shape[1] * mybir.dt.size(dtype)
        cur[0] = o + ((nbytes + 31) // 32) * 32
        assert cur[0] <= top, f"sbuf overflow at {name}"
        return t

    X = sb("X", [128, NT_OWN * D], F32)
    ident = sb("ident", [128, 128], BF16)
    ident_f = sb("ident_f", [128, 128], F32)
    onescol = sb("onescol", [128, 2], BF16)
    onescol_f = sb("onescol_f", [128, 1], F32)
    onesrow_f = sb("onesrow_f", [1, 128], F32)
    blockones = sb("blockones", [128, 2], F32)
    selAB = sb("selAB", [2, 256], F32)
    selEs = sb("selEs", [16, 8 * 128], F32)
    masks = sb("masks", [128, 2560], BF16)
    sums = sb("sums", [128, 4], F32)
    sumsq = sb("sumsq", [128, 4], F32)
    rstds = sb("rstds", [128, 4], F32)
    negmr = sb("negmr", [128, 4], F32)
    tmp4a = sb("tmp4a", [128, 4], F32)
    tmp4b = sb("tmp4b", [128, 4], F32)
    biasA = sb("biasA", [128, 1], F32)
    biasB = sb("biasB", [128, 1], F32)
    qk_sb = sb("qk_sb", [2, 2], F32)
    qk2_sb = sb("qk2_sb", [2, 1], F32)
    ltmp = sb("ltmp", [33, 512], F32)
    wst = sb("wst", [128, 192], F32)
    wtot = sb("wtot", [128, 3], F32)
    wsc = sb("wsc", [1, 8], F32)
    swv = sb("swv", [128, 3], F32)
    mwv = sb("mwv", [128, 3], F32)
    am_sb = sb("am_sb", [128, 2], F32)
    sa_sb = sb("sa_sb", [128, 2], F32)
    inv_sb = sb("inv_sb", [128, 3], F32)
    scr4k = sb("scr4k", [128, D], F32)

    R = cur[0]
    avail = top - R
    assert avail >= 126976, f"era region too small: {avail}"

    def at(name, shape, dtype, off, size):
        nbytes = shape[1] * mybir.dt.size(dtype)
        assert nbytes <= size, f"{name}: {nbytes} > {size}"
        assert R + off + nbytes <= top, f"{name} beyond top"
        return nc.alloc_sbuf_tensor_at(name, shape, dtype, offset=R + off)

    xstage = at("xstage", [128, 2 * D], F32, 0, 8192)
    h1tok = at("h1tok", [128, 2 * D], BF16, 8192, 4096)
    h1T = at("h1T", [128, 8 * 512], BF16, 12288, 8192)
    cqT = at("cqT", [128, 2 * 2048], BF16, 20480, 8192)
    ckvT = at("ckvT", [128, 2 * SEQ], BF16, 28672, 16384)
    wdq_sb = at("wdq_sb", [128, 8 * DC], BF16, 45056, 4096)
    wdkv_sb = at("wdkv_sb", [128, 8 * DC], BF16, 49152, 4096)
    fstage = at("fstage", [128, 2 * 512], F32, 53248, 4096)
    wuq_sb = at("wuq_sb", [128, 2 * D], BF16, 57344, 4096)
    wuk_sb = at("wuk_sb", [128, 2 * D], BF16, 61440, 4096)
    wuv_sb = at("wuv_sb", [128, 2 * D], BF16, 65536, 4096)
    wo_sb = at("wo_sb", [128, 8 * D], BF16, 69632, 16384)
    OT = at("OT", [128, 8 * 2048], BF16, 86016, 32768)
    l_all = at("l_all", [16, 2048], F32, 118784, 8192)
    wf_dq = at("wf_dq", [128, 8 * DC], F32, 86016, 8192)
    wf_dkv = at("wf_dkv", [128, 8 * DC], F32, 94208, 8192)
    wf_uq = at("wf_uq", [128, 2 * D], F32, 102400, 8192)
    wf_uk = at("wf_uk", [128, 2 * D], F32, 110592, 8192)
    wf_uv = at("wf_uv", [128, 2 * D], F32, 118784, 8192)
    QT = at("QT", [128, 2048], BF16, 0, 4096)
    PT = at("PT", [128, 4 * 512], BF16, 4096, 4096)
    sqscr = at("sqscr", [128, 2048], BF16, 8192, 4096)
    KT_s = at("KT_s", [128, SEQ], BF16, 12288, 8192)
    V_s = at("V_s", [128, SEQ], BF16, 45056, 8192)
    wgq = at("wgq", [128, 8 * INNER], FP8, 20480, 32768)
    wuqq = at("wuqq", [128, 8 * INNER], FP8, 57344, 32768)
    wdqq = at("wdqq", [128, 32 * D], FP8, 90112, 32768)
    gu_sb = at("gu_sb", [128, INNER], F32, 0, 16384)
    fring = at("fring", [128, 8 * 512], F32, 0, 16384)
    qh2T = at("qh2T", [128, 8 * 128], BF16, 122880, 2048)
    qh2_sb = at("qh2_sb", [128, D], BF16, 124928, 2048)
    qguT = at("qguT", [128, 32 * 128], BF16, 0, 8192)
    gsil = nc.alloc_sbuf_tensor_at("gsil", [128, 512], F32, offset=sboff["scr4k"])
    dbgb = at("dbgb", [128, 1024], F32, 20480, 32768) if dbg else None
    assert R + 126976 <= top

    def ps(name, shape, dtype, bank):
        nc.psum_base = bank
        return nc.alloc_psum_tensor(name, shape, dtype)

    psTrA = ps("psTrA", [128, 1024], BF16, 0)
    psP0 = ps("psP0", [128, 512], F32, 1)
    psP1 = ps("psP1", [128, 512], F32, 2)
    psSA0 = ps("psSA0", [128, 512], F32, 0)
    psSA1 = ps("psSA1", [128, 512], F32, 1)
    psSB0 = ps("psSB0", [128, 512], F32, 2)
    psSB1 = ps("psSB1", [128, 512], F32, 3)
    psLA = ps("psLA", [128, 512], F32, 4)
    psOA = ps("psOA", [128, 512], F32, 5)
    psLB = ps("psLB", [128, 512], F32, 6)
    psOB = ps("psOB", [128, 512], F32, 7)
    psN0 = ps("psN0", [2, 1], F32, 4)
    psN1 = ps("psN1", [2, 1], F32, 5)
    psBC0 = ps("psBC0", [128, 1], F32, 6)
    psBC1 = ps("psBC1", [128, 1], F32, 7)
    psW0 = ps("psW0", [128, 512], F32, 0)
    psW1 = ps("psW1", [128, 512], F32, 1)
    psR = ps("psR", [128, 512], F32, 2)
    psT1 = ps("psT1", [1, 1], F32, 3)
    psTrC = ps("psTrC", [128, 1024], BF16, 7)
    psTrCf = ps("psTrCf", [128, 512], F32, 0)
    psG0 = ps("psG0", [128, 512], F32, 1)
    psG1 = ps("psG1", [128, 512], F32, 2)
    psU0 = ps("psU0", [128, 512], F32, 3)
    psU1 = ps("psU1", [128, 512], F32, 4)
    psD0 = ps("psD0", [128, 512], F32, 5)
    psD1 = ps("psD1", [128, 512], F32, 6)
    nc.psum_base = 8

    DMA_SEMS = ("dL", "dS", "dF", "dQ", "dQ2")
    sem_names = ["dL", "dS", "dF", "dQ", "dQ2", "tS", "vS", "scS", "gS"]
    sems = {n: nc.alloc_semaphore(n) for n in sem_names}
    cnt = {n: 0 for n in sem_names}
    OPS = {"sync": [], "tensor": [], "vector": [], "scalar": [], "gpsimd": []}

    def W(eng, sem, val):
        if val > 0:
            OPS[eng].append(("w", sem, val))

    def E(eng, fn, inc=None):
        n = 16 if inc in DMA_SEMS else 1
        OPS[eng].append(("e", fn, (inc, n) if inc else None))
        if inc:
            cnt[inc] += n

    _SELF = {"vector": "vS", "scalar": "scS", "gpsimd": "gS", "tensor": "tS"}

    def ECH(eng, fn):
        s = _SELF[eng]
        E(eng, fn, s)
        W(eng, s, cnt[s])

    for dst, src in [(ident, ident_c), (ident_f, identf_c), (onescol, onescol_c), (onescol_f, onescol_f_c),
                     (onesrow_f, onesrow_f_c), (blockones, blockones_c),
                     (selAB, selAB_c), (selEs, selE_c)]:
        E("sync", lambda e, d=dst, s=src: e.dma_start(out=d[:, :], in_=s.ap()), "dL")
    E("sync", lambda e: e.dma_start(out=masks[:, :], in_=dmasks), "dL")
    for t in range(NT_OWN):
        E("sync", lambda e, t=t: e.dma_start(out=X[:, t * D:(t + 1) * D],
                                             in_=dx_own[t * 128:(t + 1) * 128, :]), "dL")
    for ck in range(8):
        E("sync", lambda e, ck=ck: e.dma_start(out=wf_dq[:, ck * DC:(ck + 1) * DC],
                                               in_=dw_dq[ck * 128:(ck + 1) * 128, :]), "dL")
        E("sync", lambda e, ck=ck: e.dma_start(out=wf_dkv[:, ck * DC:(ck + 1) * DC],
                                               in_=dw_dkv[ck * 128:(ck + 1) * 128, :]), "dL")
    for cc in range(2):
        E("sync", lambda e, cc=cc: e.dma_start(out=wf_uq[:, cc * D:(cc + 1) * D],
                                               in_=dw_uq[cc * 128:(cc + 1) * 128, :]), "dL")
        E("sync", lambda e, cc=cc: e.dma_start(out=wf_uk[:, cc * D:(cc + 1) * D],
                                               in_=dw_uk[cc * 128:(cc + 1) * 128, :]), "dL")
        E("sync", lambda e, cc=cc: e.dma_start(out=wf_uv[:, cc * D:(cc + 1) * D],
                                               in_=dw_uv[cc * 128:(cc + 1) * 128, :]), "dL")
    dl_stage5 = cnt["dL"]
    W("vector", "dL", dl_stage5)
    E("vector", lambda e: e.tensor_copy(wdq_sb[:, :], wf_dq[:, :]), "vS")
    E("vector", lambda e: e.tensor_copy(wdkv_sb[:, :], wf_dkv[:, :]), "vS")
    E("vector", lambda e: e.tensor_copy(wuq_sb[:, :], wf_uq[:, :]), "vS")
    E("vector", lambda e: e.tensor_copy(wuk_sb[:, :], wf_uk[:, :]), "vS")
    E("vector", lambda e: e.tensor_copy(wuv_sb[:, :], wf_uv[:, :]), "vS")
    wo_cast = []
    for pc in range(16):
        ck, hh = pc // 2, (pc % 2) * 512
        if pc >= 2:
            W("sync", "vS", wo_cast[pc - 2])
        E("sync", lambda e, ck=ck, hh=hh, sl=pc % 2: e.dma_start(
            out=fstage[:, sl * 512:(sl + 1) * 512],
            in_=dw_o[ck * 128:(ck + 1) * 128, hh:hh + 512]), "dL")
        W("vector", "dL", cnt["dL"])
        E("vector", lambda e, ck=ck, hh=hh, sl=pc % 2: e.tensor_copy(
            wo_sb[:, ck * D + hh:ck * D + hh + 512], fstage[:, sl * 512:(sl + 1) * 512]), "vS")
        wo_cast.append(cnt["vS"])
    wcast_all = cnt["vS"]
    dl_head = cnt["dL"]

    h1evac = {}
    projevac = []
    xload = {}
    h1done = {}
    sqdone = {}
    tr_done = {}
    chain_done = {}
    npr = 0
    for g in range(12):
        own = g >= 8
        for j in range(4):
            ti = g * 4 + j
            if not own:
                prev = ti - 2
                if prev >= 0 and prev in h1done:
                    W("sync", "vS", h1done[prev])
                    W("sync", "scS", sqdone[prev])
                E("sync", lambda e, ti=ti, sl=ti % 2: e.dma_start(
                    out=xstage[:, sl * D:(sl + 1) * D],
                    in_=dx_all[ti * 128:(ti + 1) * 128, :]), "dL")
                xload[ti] = cnt["dL"]
                xap = xstage[:, (ti % 2) * D:((ti % 2) + 1) * D]
            else:
                xload[ti] = dl_head
                xap = X[:, (ti - 32) * D:(ti - 31) * D]
            W("vector", "dL", xload[ti])
            ECH("vector", lambda e, x=xap, j=j: e.reduce_sum(sums[:, j:j + 1], x, AX.X))
            W("scalar", "dL", xload[ti])
            if ti >= 1:
                W("scalar", "vS", h1done.get(ti - 1, 0))
            E("scalar", lambda e, x=xap, j=j: e.activation(
                scr4k[:, :], x, AF.Square, accum_out=sumsq[:, j:j + 1]), "scS")
            sqdone[ti] = cnt["scS"]
            W("vector", "scS", sqdone[ti])
            ECH("vector", lambda e, j=j: e.tensor_scalar(
                tmp4a[:, j:j + 1], sums[:, j:j + 1], 1.0 / D, None, ALU.mult))
            ECH("vector", lambda e, j=j: e.tensor_scalar(
                tmp4b[:, j:j + 1], tmp4a[:, j:j + 1], tmp4a[:, j:j + 1], -1.0,
                ALU.mult, ALU.mult))
            ECH("vector", lambda e, j=j: e.scalar_tensor_tensor(
                tmp4b[:, j:j + 1], sumsq[:, j:j + 1], 1.0 / D,
                tmp4b[:, j:j + 1], ALU.mult, ALU.add))
            E("vector", lambda e, j=j: e.tensor_scalar(
                rstds[:, j:j + 1], tmp4b[:, j:j + 1], LN_EPS, None, ALU.add), "vS")
            W("scalar", "vS", cnt["vS"])
            E("scalar", lambda e, j=j: e.activation(
                rstds[:, j:j + 1], rstds[:, j:j + 1], AF.Sqrt), "scS")
            W("vector", "scS", cnt["scS"])
            ECH("vector", lambda e, j=j: e.reciprocal(rstds[:, j:j + 1], rstds[:, j:j + 1]))
            ECH("vector", lambda e, j=j: e.tensor_scalar(
                negmr[:, j:j + 1], tmp4a[:, j:j + 1], rstds[:, j:j + 1], -1.0,
                ALU.mult, ALU.mult))
            sl = j % 2
            slot_prev = (g, j - 2) if j >= 2 else (g - 1, j + 2)
            if slot_prev in tr_done:
                W("vector", "tS", tr_done[slot_prev])
            E("vector", lambda e, x=xap, j=j, sl=sl: e.tensor_scalar(
                h1tok[:, sl * D:(sl + 1) * D], x, rstds[:, j:j + 1], negmr[:, j:j + 1],
                ALU.mult, ALU.add), "vS")
            h1done[ti] = cnt["vS"]
            W("tensor", "vS", h1done[ti])
            tr_prev = (g, j - 1) if j >= 1 else (g - 1, 3)
            if tr_prev in h1evac:
                W("tensor", "vS", h1evac[tr_prev])
            for ck in range(8):
                E("tensor", lambda e, sl=sl, ck=ck: e.transpose(
                    psTrA[:, ck * 128:(ck + 1) * 128],
                    h1tok[:, sl * D + ck * 128:sl * D + (ck + 1) * 128], ident[:, :]),
                    "tS" if ck == 7 else None)
            tr_done[(g, j)] = cnt["tS"]
            W("vector", "tS", tr_done[(g, j)])
            E("vector", lambda e, j=j: e.tensor_copy(
                h1T[:, :].rearrange("p (c t) -> p c t", c=8)[:, :, j * 128:(j + 1) * 128],
                psTrA[:, :].rearrange("p (c t) -> p c t", c=8)), "vS")
            h1evac[(g, j)] = cnt["vS"]
        wtile = wdq_sb if own else wdkv_sb
        dstT = cqT if own else ckvT
        gcol = (g - 8) * 512 if own else g * 512
        span = 2048 if own else SEQ
        for m in range(2):
            bank = [psP0, psP1][npr % 2]
            if npr >= 2:
                W("tensor", "vS", projevac[npr - 2])
            W("tensor", "vS", h1evac[(g, 3)])
            W("tensor", "vS", wcast_all)
            for ck in range(8):
                E("tensor", lambda e, b=bank, wt=wtile, ck=ck, m=m: e.matmul(
                    b[:, :], wt[:, ck * DC + m * 128:ck * DC + (m + 1) * 128],
                    h1T[:, ck * 512:(ck + 1) * 512],
                    start=(ck == 0), stop=(ck == 7)), "tS" if ck == 7 else None)
            W("vector", "tS", cnt["tS"])
            E("vector", lambda e, b=bank, d=dstT, m=m, gc=gcol, sp=span: e.tensor_copy(
                d[:, m * sp + gc:m * sp + gc + 512], b[:, :]), "vS")
            projevac.append(cnt["vS"])
            npr += 1
    phaseA_tS = cnt["tS"]
    phaseA_vS = cnt["vS"]

    def _dump(pairs):
        for nm, ap in pairs:
            dt_ = ap.dtype
            dten = nc.dram_tensor("dump_" + nm, list(ap.shape), dt_,
                                  kind="ExternalOutput").ap()
            W("gpsimd", "vS", cnt["vS"])
            W("gpsimd", "scS", cnt["scS"])
            W("gpsimd", "tS", cnt["tS"])
            E("gpsimd", lambda e, d=dten, s=ap: e.dma_start(out=d, in_=s), "dS")
        W("gpsimd", "dS", cnt["dS"])

    if dbg == "A":
        _dump([("ckvT", ckvT[:, :]), ("cqT", cqT[:, :]), ("wdq", wdq_sb[:, :]),
               ("wuk", wuk_sb[:, :]), ("wo", wo_sb[:, :]), ("X", X[:, 0:2048]),
               ("h1T", h1T[:, :]), ("h1tok", h1tok[:, :]), ("sums", sums[:, :]),
               ("sumsq", sumsq[:, :]), ("rstds", rstds[:, :]), ("negmr", negmr[:, :]),
               ("tmp4a", tmp4a[:, :]), ("tmp4b", tmp4b[:, :])])

    ffn_srcs = []
    for wi, dw in enumerate([dwg, dwu, dwd]):
        nrow = D if wi < 2 else INNER
        ncolw = INNER if wi < 2 else D
        idx = 0
        for ck in range(nrow // 128):
            for hh in range(0, ncolw, 512):
                ffn_srcs.append((wi, idx, dw[ck * 128:(ck + 1) * 128, hh:hh + 512]))
                idx += 1
    assert len(ffn_srcs) == 192
    ffn_ptr = [0]
    ffn_gdone = []

    def emit_ffn_mean(nsteps):
        for _ in range(nsteps):
            i = ffn_ptr[0]
            if i >= len(ffn_srcs):
                return
            wi, idx, src = ffn_srcs[i]
            slot = i % 2
            if i < 2:
                W("gpsimd", "vS", wcast_all)
            if i >= 2:
                W("gpsimd", "scS", ffn_gdone[i - 2])
            E("gpsimd", lambda e, s=src, sl=slot: e.dma_start(
                out=fstage[:, sl * 512:(sl + 1) * 512], in_=s), "dF")
            W("scalar", "dF", cnt["dF"])
            col = wi * 64 + idx
            E("scalar", lambda e, sl=slot, col=col: e.activation(
                fstage[:, sl * 512:(sl + 1) * 512], fstage[:, sl * 512:(sl + 1) * 512], AF.Abs,
                accum_out=wst[:, col:col + 1]), "scS")
            ffn_gdone.append(cnt["scS"])
            ffn_ptr[0] += 1

    if dbg != "A":
        u_evac = {}
        exp_cnt = {}
        mask_cnt = {}
        lav_cnt = {}
        s_cnt = {}
        last_exp_prev_hp = 0
        last_lav_prev = 0

        for hp in range(8):
            W("tensor", "scS", last_exp_prev_hp)
            if hp == 0:
                W("tensor", "vS", phaseA_vS)
            else:
                W("tensor", "vS", u_evac[(hp - 1, 3)])
            bev = []
            for nb in range(8):
                bank = [psSA0, psSA1][nb % 2]
                if nb >= 2:
                    W("tensor", "vS", bev[nb - 2])
                for cc in range(2):
                    E("tensor", lambda e, b=bank, cc=cc, hp=hp, nb=nb: e.matmul(
                        b[:, :], wuk_sb[:, cc * D + hp * 128:cc * D + (hp + 1) * 128],
                        ckvT[:, cc * SEQ + nb * 512:cc * SEQ + nb * 512 + 512],
                        start=(cc == 0), stop=(cc == 1)), "tS" if cc == 1 else None)
                W("vector", "tS", cnt["tS"])
                E("vector", lambda e, b=bank, nb=nb: e.tensor_copy(
                    KT_s[:, nb * 512:(nb + 1) * 512], b[:, :]), "vS")
                bev.append(cnt["vS"])
            for nb in range(4):
                bank = [psSB0, psSB1][nb % 2]
                if nb >= 2:
                    W("tensor", "vS", bev[8 + nb - 2])
                for cc in range(2):
                    E("tensor", lambda e, b=bank, cc=cc, hp=hp, nb=nb: e.matmul(
                        b[:, :], wuq_sb[:, cc * D + hp * 128:cc * D + (hp + 1) * 128],
                        cqT[:, cc * 2048 + nb * 512:cc * 2048 + nb * 512 + 512],
                        start=(cc == 0), stop=(cc == 1)), "tS" if cc == 1 else None)
                W("vector", "tS", cnt["tS"])
                E("vector", lambda e, b=bank, nb=nb: e.tensor_copy(
                    QT[:, nb * 512:(nb + 1) * 512], b[:, :]), "vS")
                bev.append(cnt["vS"])
            for kt in range(32):
                bank = [psLA, psOA][kt % 2]
                if kt >= 2:
                    W("tensor", "vS", bev[12 + kt - 2])
                for cc in range(2):
                    E("tensor", lambda e, b=bank, cc=cc, hp=hp, kt=kt: e.matmul(
                        b[:, 0:128], ckvT[:, cc * SEQ + kt * 128:cc * SEQ + (kt + 1) * 128],
                        wuv_sb[:, cc * D + hp * 128:cc * D + (hp + 1) * 128],
                        start=(cc == 0), stop=(cc == 1)), "tS" if cc == 1 else None)
                W("vector", "tS", cnt["tS"])
                E("vector", lambda e, b=bank, kt=kt: e.tensor_copy(
                    V_s[:, kt * 128:(kt + 1) * 128], b[:, 0:128]), "vS")
                bev.append(cnt["vS"])
            kv_built = cnt["vS"]

            W("scalar", "vS", kv_built)
            E("scalar", lambda e: e.activation(sqscr[:, :], QT[:, :], AF.Square), "scS")
            W("vector", "scS", cnt["scS"])
            E("vector", lambda e: e.reduce_max(tmp4a[:, 0:1], sqscr[:, :], AX.X), "vS")
            E("scalar", lambda e: e.activation(sqscr[:, :], KT_s[:, 0:2048], AF.Square), "scS")
            W("vector", "scS", cnt["scS"])
            E("vector", lambda e: e.reduce_max(tmp4a[:, 1:2], sqscr[:, :], AX.X), "vS")
            W("scalar", "vS", cnt["vS"])
            E("scalar", lambda e: e.activation(sqscr[:, :], KT_s[:, 2048:4096], AF.Square), "scS")
            W("vector", "scS", cnt["scS"])
            ECH("vector", lambda e: e.reduce_max(tmp4b[:, 0:1], sqscr[:, :], AX.X))
            E("vector", lambda e: e.tensor_max(tmp4a[:, 1:2], tmp4a[:, 1:2], tmp4b[:, 0:1]), "vS")
            W("tensor", "vS", cnt["vS"])
            E("tensor", lambda e: e.matmul(psN0[:, :], blockones[:, :], tmp4a[:, 0:1],
                                           start=True, stop=True), "tS")
            E("tensor", lambda e: e.matmul(psN1[:, :], blockones[:, :], tmp4a[:, 1:2],
                                           start=True, stop=True), "tS")
            W("vector", "tS", cnt["tS"])
            E("vector", lambda e: e.tensor_copy(qk_sb[:, 0:1], psN0[:, :]), "vS")
            E("vector", lambda e: e.tensor_copy(qk_sb[:, 1:2], psN1[:, :]), "vS")
            W("vector", "vS", cnt["vS"])
            W("scalar", "vS", cnt["vS"])
            E("scalar", lambda e: e.activation(qk_sb[:, :], qk_sb[:, :], AF.Sqrt), "scS")
            W("vector", "scS", cnt["scS"])
            ECH("vector", lambda e: e.tensor_scalar(qk2_sb[:, :], qk_sb[:, 0:1], qk_sb[:, 1:2],
                                                    C_SAFETY, ALU.mult, ALU.mult))
            W("tensor", "vS", cnt["vS"])
            E("tensor", lambda e: e.matmul(psBC0[:, :], selAB[:, 0:128], qk2_sb[:, :],
                                           start=True, stop=True), "tS")
            E("tensor", lambda e: e.matmul(psBC1[:, :], selAB[:, 128:256], qk2_sb[:, :],
                                           start=True, stop=True), "tS")
            W("vector", "tS", cnt["tS"])
            E("vector", lambda e: e.tensor_copy(biasA[:, :], psBC0[:, :]), "vS")
            E("vector", lambda e: e.tensor_copy(biasB[:, :], psBC1[:, :]), "vS")
            bias_ready = cnt["vS"]

            emit_ffn_mean(24)

            for u in range(4):
                KT = KTU[u]
                if u == 0:
                    W("tensor", "vS", bias_ready)
                else:
                    W("tensor", "vS", u_evac[(hp, u - 1)])

                def emit_S(kt, hp=hp, u=u, KT=KT, bias_ready=bias_ready,
                           last_lav_prev=last_lav_prev):
                    o_ = 0 if kt < KT - 4 else 128 * (kt - (KT - 4))
                    cols = 512 - o_
                    slot = kt % 2
                    pA = [psSA0, psSA1][kt % 2]
                    pB = [psSB0, psSB1][kt % 2]
                    if kt >= 1:
                        W("tensor", "scS", exp_cnt[(hp, u, kt - 1)])
                        if (hp, u, kt - 1) in mask_cnt:
                            W("tensor", "vS", mask_cnt[(hp, u, kt - 1)])
                    elif u > 0:
                        W("tensor", "scS", exp_cnt[(hp, u - 1, KTU[u - 1] - 2 + kt)])
                    elif u == 0:
                        W("tensor", "vS", bias_ready)
                    E("tensor", lambda e, p=pA, o_=o_, kt=kt, u=u: e.matmul(
                        p[:, o_:512], KT_s[0:64, kt * 128:(kt + 1) * 128],
                        QT[0:64, u * 512 + o_:(u + 1) * 512],
                        start=True, stop=True, tile_position=(0, 0)), "tS")
                    E("tensor", lambda e, p=pB, o_=o_, kt=kt, u=u: e.matmul(
                        p[:, o_:512], KT_s[64:128, kt * 128:(kt + 1) * 128],
                        QT[64:128, u * 512 + o_:(u + 1) * 512],
                        start=True, stop=True, tile_position=(64, 0)), "tS")
                    s_cnt[(hp, u, kt)] = cnt["tS"]
                    W("scalar", "tS", s_cnt[(hp, u, kt)])
                    if kt >= 2:
                        W("scalar", "tS", lav_cnt[(hp, u, kt - 2)])
                    else:
                        W("scalar", "tS", last_lav_prev)
                    W("scalar", "vS", bias_ready)
                    E("scalar", lambda e, p=pA, o_=o_, c=cols, sl=slot: e.activation(
                        PT[:, sl * 1024:sl * 1024 + c], p[:, o_:512], AF.Exp,
                        bias=biasA[:, :], scale=0.125), "scS")
                    E("scalar", lambda e, p=pB, o_=o_, c=cols, sl=slot: e.activation(
                        PT[:, sl * 1024 + 512:sl * 1024 + 512 + c], p[:, o_:512], AF.Exp,
                        bias=biasB[:, :], scale=0.125), "scS")
                    exp_cnt[(hp, u, kt)] = cnt["scS"]
                    if kt >= KT - 8:
                        jp = kt - (KT - 8)
                        if jp < 4:
                            mc0, mw, moff = 0, 128 * (jp + 1), [0, 128, 384, 768][jp]
                        else:
                            mc0, mw = 0, 512 - o_
                            moff = [1280, 1792, 2176, 2432][jp - 4]
                        W("vector", "scS", exp_cnt[(hp, u, kt)])
                        E("vector", lambda e, sl=slot, mc0=mc0, mw=mw, moff=moff: e.tensor_mul(
                            PT[:, sl * 1024 + mc0:sl * 1024 + mc0 + mw],
                            PT[:, sl * 1024 + mc0:sl * 1024 + mc0 + mw],
                            masks[:, moff:moff + mw]), "vS")
                        E("vector", lambda e, sl=slot, mc0=mc0, mw=mw, moff=moff: e.tensor_mul(
                            PT[:, sl * 1024 + 512 + mc0:sl * 1024 + 512 + mc0 + mw],
                            PT[:, sl * 1024 + 512 + mc0:sl * 1024 + 512 + mc0 + mw],
                            masks[:, moff:moff + mw]), "vS")
                        mask_cnt[(hp, u, kt)] = cnt["vS"]

                def emit_lav(kt, hp=hp, u=u, KT=KT, waits=True):
                    o_ = 0 if kt < KT - 4 else 128 * (kt - (KT - 4))
                    cols = 512 - o_
                    slot = kt % 2
                    if waits:
                        W("tensor", "scS", exp_cnt[(hp, u, kt)])
                        if (hp, u, kt) in mask_cnt:
                            W("tensor", "vS", mask_cnt[(hp, u, kt)])
                    st_, sp_ = (kt == 0), (kt == KT - 1)
                    E("tensor", lambda e, sl=slot, o_=o_, c=cols, st_=st_, sp_=sp_: e.matmul(
                        psLA[0:1, o_:512], onescol[:, 0:1], PT[:, sl * 1024:sl * 1024 + c],
                        start=st_, stop=sp_, tile_position=(0, 0)), "tS")
                    E("tensor", lambda e, sl=slot, o_=o_, c=cols, st_=st_, sp_=sp_: e.matmul(
                        psLB[32:33, o_:512], onescol[:, 1:2],
                        PT[:, sl * 1024 + 512:sl * 1024 + 512 + c],
                        start=st_, stop=sp_, tile_position=(0, 32)), "tS")
                    E("tensor", lambda e, sl=slot, o_=o_, kt=kt, c=cols, st_=st_, sp_=sp_: e.matmul(
                        psOA[0:64, o_:512], V_s[:, kt * 128:kt * 128 + 64],
                        PT[:, sl * 1024:sl * 1024 + c],
                        start=st_, stop=sp_, tile_position=(0, 0)), "tS")
                    E("tensor", lambda e, sl=slot, o_=o_, kt=kt, c=cols, st_=st_, sp_=sp_: e.matmul(
                        psOB[64:128, o_:512], V_s[:, kt * 128 + 64:(kt + 1) * 128],
                        PT[:, sl * 1024 + 512:sl * 1024 + 512 + c],
                        start=st_, stop=sp_, tile_position=(0, 64)), "tS")
                    lav_cnt[(hp, u, kt)] = cnt["tS"]

                for kt in range(KT):
                    emit_S(kt)
                    if kt >= 1:
                        emit_lav(kt - 1, waits=False)
                emit_lav(KT - 1)

                last_lav_prev = lav_cnt[(hp, u, KT - 1)]
                W("vector", "tS", last_lav_prev)
                W("vector", "dS", cnt["dS"])
                E("vector", lambda e, hp=hp, u=u: e.tensor_copy(
                    OT[0:64, hp * 2048 + u * 512:hp * 2048 + (u + 1) * 512], psOA[0:64, :]), "vS")
                E("vector", lambda e, hp=hp, u=u: e.tensor_copy(
                    OT[64:128, hp * 2048 + u * 512:hp * 2048 + (u + 1) * 512], psOB[64:128, :]), "vS")
                E("vector", lambda e: e.tensor_copy(ltmp[0:1, :], psLA[0:1, :]), "vS")
                E("vector", lambda e: e.tensor_copy(ltmp[32:33, :], psLB[32:33, :]), "vS")
                if dbg == "B" and hp == 7 and u == 3:
                    E("vector", lambda e: e.tensor_copy(dbgb[:, 0:512], psLA[:, :]), "vS")
                    E("vector", lambda e: e.tensor_copy(dbgb[:, 512:1024], psOA[:, :]), "vS")
                u_evac[(hp, u)] = cnt["vS"]
                W("gpsimd", "vS", u_evac[(hp, u)])
                E("gpsimd", lambda e, hp=hp, u=u: e.dma_start(
                    out=l_all[2 * hp:2 * hp + 1, u * 512:(u + 1) * 512], in_=ltmp[0:1, :]), "dS")
                E("gpsimd", lambda e, hp=hp, u=u: e.dma_start(
                    out=l_all[2 * hp + 1:2 * hp + 2, u * 512:(u + 1) * 512],
                    in_=ltmp[32:33, :]), "dS")
            last_exp_prev_hp = exp_cnt[(hp, 3, KTU[3] - 1)]

        W("vector", "dS", cnt["dS"])
        E("vector", lambda e: e.reciprocal(l_all[:, :], l_all[:, :]), "vS")
        W("tensor", "scS", cnt["scS"])
        W("tensor", "vS", cnt["vS"])
        wo_ev = {}
        for u in range(4):
            for hp in range(8):
                if (u, hp) != (0, 0):
                    W("tensor", "vS", cnt["vS"])
                E("tensor", lambda e, hp=hp, u=u: e.matmul(
                    psR[:, :], selEs[:, hp * 128:(hp + 1) * 128],
                    l_all[:, u * 512:(u + 1) * 512], start=True, stop=True), "tS")
                W("vector", "tS", cnt["tS"])
                E("vector", lambda e, hp=hp, u=u: e.tensor_mul(
                    OT[:, hp * 2048 + u * 512:hp * 2048 + (u + 1) * 512],
                    OT[:, hp * 2048 + u * 512:hp * 2048 + (u + 1) * 512], psR[:, :]), "vS")
            norm_u = cnt["vS"]
            for tt in range(4):
                ti = u * 4 + tt
                for nbk in range(2):
                    bank = [psW0, psW1][nbk]
                    pk = u * 8 + tt * 2 + nbk - 2
                    if pk >= 0:
                        W("tensor", "vS", wo_ev[pk])
                    W("tensor", "vS", norm_u)
                    for hp in range(8):
                        E("tensor", lambda e, b=bank, hp=hp, u=u, tt=tt, nbk=nbk: e.matmul(
                            b[:, :],
                            OT[:, hp * 2048 + u * 512 + tt * 128:hp * 2048 + u * 512 + (tt + 1) * 128],
                            wo_sb[:, hp * D + nbk * 512:hp * D + nbk * 512 + 512],
                            start=(hp == 0), stop=(hp == 7)), "tS" if hp == 7 else None)
                    W("vector", "tS", cnt["tS"])
                    E("vector", lambda e, b=bank, ti=ti, nbk=nbk: e.scalar_tensor_tensor(
                        X[:, ti * D + nbk * 512:ti * D + nbk * 512 + 512], b[:, :], 1.0,
                        X[:, ti * D + nbk * 512:ti * D + nbk * 512 + 512],
                        ALU.mult, ALU.add), "vS")
                    wo_ev[u * 8 + tt * 2 + nbk] = cnt["vS"]
        tB3 = cnt["tS"]
        vB3 = cnt["vS"]


        if dbg == "B":
            _dump([("X", X[:, :]), ("l_all", l_all[:, :]), ("OT", OT[:, :]),
                   ("biasA", biasA[:, :]), ("biasB", biasB[:, :]),
                   ("KT", KT_s[:, :]), ("QT", QT[:, :]), ("V", V_s[:, :]),
                   ("PT", PT[:, :]), ("qk", qk_sb[:, :]), ("qk2", qk2_sb[:, :]),
                   ("dbgb", dbgb[:, :])])
    if dbg not in ("A", "B"):
        emit_ffn_mean(len(ffn_srcs))
        gMean = cnt["scS"]
        W("vector", "scS", gMean)
        W("vector", "tS", tB3)
        for wi in range(3):
            E("vector", lambda e, wi=wi: e.reduce_sum(
                wtot[:, wi:wi + 1], wst[:, wi * 64:(wi + 1) * 64], AX.X), "vS")
        W("tensor", "scS", cnt["scS"])
        W("tensor", "vS", cnt["vS"])
        for wi in range(3):
            if wi > 0:
                W("tensor", "vS", cnt["vS"])
            E("tensor", lambda e, wi=wi: e.matmul(
                psT1[:, :], onescol_f[:, :], wtot[:, wi:wi + 1], start=True, stop=True), "tS")
            W("vector", "tS", cnt["tS"])
            E("vector", lambda e, wi=wi: e.tensor_copy(wsc[:, wi:wi + 1], psT1[:, :]), "vS")
        W("vector", "vS", cnt["vS"])
        ECH("vector", lambda e: e.tensor_scalar(wsc[:, 3:6], wsc[:, 0:3], 1.0 / (D * INNER), 1e-5,
                                                ALU.mult, ALU.max))
        ECH("vector", lambda e: e.reciprocal(wsc[:, 0:3], wsc[:, 3:6]))
        ECH("vector", lambda e: e.tensor_scalar(wsc[:, 6:8], wsc[:, 3:5], 1.0 / 127, None, ALU.mult))
        ECH("vector", lambda e: e.tensor_scalar(wsc[:, 5:6], wsc[:, 5:6], 1.0 / 127, None, ALU.mult))
        W("tensor", "vS", cnt["vS"])
        for wi in range(3):
            if wi > 0:
                W("tensor", "vS", cnt["vS"])
            E("tensor", lambda e, wi=wi: e.matmul(psBC0[:, :], onesrow_f[:, :],
                                                  wsc[0:1, wi:wi + 1], start=True, stop=True), "tS")
            W("vector", "tS", cnt["tS"])
            E("vector", lambda e, wi=wi: e.tensor_copy(swv[:, wi:wi + 1], psBC0[:, :]), "vS")
        for wi, scol in enumerate([6, 7, 5]):
            if wi > 0:
                W("tensor", "vS", cnt["vS"])
            E("tensor", lambda e, scol=scol: e.matmul(psBC1[:, :], onesrow_f[:, :],
                                                      wsc[0:1, scol:scol + 1],
                                                      start=True, stop=True), "tS")
            W("vector", "tS", cnt["tS"])
            E("vector", lambda e, wi=wi: e.tensor_copy(mwv[:, wi:wi + 1], psBC1[:, :]), "vS")
        wsc_ready = cnt["vS"]

        qdst = {0: wgq, 1: wuqq, 2: wdqq}
        qdone = []
        W("gpsimd", "tS", tB3)
        for i, (wi, idx, src) in enumerate(ffn_srcs):
            slot = i % 8
            lq = "sync" if i % 2 == 0 else "scalar"
            dsem = "dQ" if i % 2 == 0 else "dQ2"
            if i < 8:
                W(lq, "scS", gMean)
                W(lq, "vS", wsc_ready)
                W(lq, "tS", tB3)
            else:
                W(lq, "vS", qdone[i - 8])
            E(lq, lambda e, s=src, sl=slot: e.dma_start(
                out=fring[:, sl * 512:(sl + 1) * 512], in_=s), dsem)
            eng = "vector"
            sem = "vS"
            W(eng, dsem, cnt[dsem])
            W(eng, "vS", wsc_ready)
            if wi < 2:
                dcol = (idx // 8) * INNER + (idx % 8) * 512
            else:
                dcol = (idx // 2) * D + (idx % 2) * 512
            E(eng, lambda e, sl=slot, wi=wi: e.tensor_scalar(
                fring[:, sl * 512:(sl + 1) * 512], fring[:, sl * 512:(sl + 1) * 512],
                swv[:, wi:wi + 1], 1.0, ALU.mult, ALU.min))
            E(eng, lambda e, sl=slot: e.tensor_scalar(
                fring[:, sl * 512:(sl + 1) * 512], fring[:, sl * 512:(sl + 1) * 512],
                -1.0, MAGIC, ALU.max, ALU.add))
            E(eng, lambda e, sl=slot, wi=wi, dcol=dcol: e.tensor_scalar(
                qdst[wi][:, dcol:dcol + 512], fring[:, sl * 512:(sl + 1) * 512],
                MAGIC, None, ALU.subtract), sem)
            qdone.append(cnt[sem])
        quant_v = cnt["vS"]
        quant_g = cnt["gS"]


        if dbg == "C0":
            _dump([("wgq", wgq[:, :]), ("wuqq", wuqq[:, :]), ("wdqq", wdqq[:, :]),
                   ("wsc", wsc[:, :]), ("swv", swv[:, :]), ("mwv", mwv[:, :]),
                   ("wst", wst[:, :]), ("wtot", wtot[:, :])])
    if dbg not in ("A", "B", "C0"):
        tile_done = {}
        tstate = {}
        for ti in range(NT_OWN):
            xap = X[:, ti * D:(ti + 1) * D]
            if ti == 0:
                W("vector", "gS", quant_g)
                W("vector", "tS", tB3)
                W("scalar", "tS", tB3)
            E("vector", lambda e, x=xap: e.reduce_sum(sums[:, 0:1], x, AX.X), "vS")
            W("scalar", "vS", cnt["vS"])
            E("scalar", lambda e, x=xap: e.activation(
                scr4k[:, :], x, AF.Square, accum_out=sumsq[:, 0:1]), "scS")
            W("vector", "scS", cnt["scS"])
            ECH("vector", lambda e: e.tensor_scalar(tmp4a[:, 0:1], sums[:, 0:1], 1.0 / D, None, ALU.mult))
            ECH("vector", lambda e: e.tensor_scalar(tmp4b[:, 0:1], tmp4a[:, 0:1], tmp4a[:, 0:1], -1.0, ALU.mult, ALU.mult))
            ECH("vector", lambda e: e.scalar_tensor_tensor(tmp4b[:, 0:1], sumsq[:, 0:1], 1.0 / D,
                                                         tmp4b[:, 0:1], ALU.mult, ALU.add))
            E("vector", lambda e: e.tensor_scalar(
                rstds[:, 0:1], tmp4b[:, 0:1], LN_EPS, None, ALU.add), "vS")
            W("scalar", "vS", cnt["vS"])
            E("scalar", lambda e: e.activation(rstds[:, 0:1], rstds[:, 0:1], AF.Sqrt), "scS")
            W("vector", "scS", cnt["scS"])
            ECH("vector", lambda e: e.reciprocal(rstds[:, 0:1], rstds[:, 0:1]))
            ECH("vector", lambda e: e.tensor_scalar(negmr[:, 0:1], tmp4a[:, 0:1], rstds[:, 0:1],
                                                  -1.0, ALU.mult, ALU.mult))
            E("vector", lambda e, x=xap: e.tensor_scalar(
                scr4k[:, :], x, rstds[:, 0:1], negmr[:, 0:1], ALU.mult, ALU.add))
            ECH("vector", lambda e: e.tensor_reduce(am_sb[:, 0:1], scr4k[:, :], AX.X, ALU.max,
                                                  apply_absolute_value=True))
            ECH("vector", lambda e: e.tensor_scalar(am_sb[:, 0:1], am_sb[:, 0:1], 1e-5, None, ALU.max))
            ECH("vector", lambda e: e.reciprocal(sa_sb[:, 0:1], am_sb[:, 0:1]))
            ECH("vector", lambda e: e.tensor_scalar(sa_sb[:, 0:1], sa_sb[:, 0:1], 127.0, None, ALU.mult))
            if "gu_prev" in tstate:
                W("vector", "scS", tstate["gu_prev"][7][0])
            E("vector", lambda e: e.tensor_mul(inv_sb[:, 0:1], am_sb[:, 0:1], mwv[:, 0:1]))
            E("vector", lambda e: e.tensor_mul(inv_sb[:, 1:2], am_sb[:, 0:1], mwv[:, 1:2]))
            E("vector", lambda e: e.tensor_scalar(scr4k[:, :], scr4k[:, :], sa_sb[:, 0:1], MAGIC,
                                                  ALU.mult, ALU.add))
            if "qh2_tr" in tstate:
                W("vector", "tS", tstate["qh2_tr"])
            E("vector", lambda e: e.tensor_scalar(qh2_sb[:, :], scr4k[:, :], MAGIC, None,
                                                  ALU.subtract), "vS")
            qh2_done = cnt["vS"]
            W("tensor", "vS", qh2_done)
            if ti == 0:
                W("tensor", "vS", quant_v)
                W("tensor", "gS", quant_g)
            elif "qh2T_evac" in tstate:
                W("tensor", "vS", tstate["qh2T_evac"])
            for ck in range(8):
                E("tensor", lambda e, ck=ck: e.transpose(
                    psTrC[:, ck * 128:(ck + 1) * 128], qh2_sb[:, ck * 128:(ck + 1) * 128],
                    ident[:, :]), "tS" if ck == 7 else None)
            tstate["qh2_tr"] = cnt["tS"]
            W("vector", "tS", cnt["tS"])
            if "gate_last" in tstate:
                W("vector", "tS", tstate["gate_last"])
            E("vector", lambda e: e.tensor_copy(qh2T[:, :], psTrC[:, :]), "vS")
            qh2T_done = cnt["vS"]
            tstate["qh2T_evac"] = qh2T_done
            gu_done = []
            for ib in range(8):
                pG = [psG0, psG1][ib % 2]
                pU = [psU0, psU1][ib % 2]
                W("tensor", "vS", qh2T_done)
                if ib >= 2:
                    W("tensor", "vS", gu_done[ib - 2][1])
                    W("tensor", "scS", gu_done[ib - 2][0])
                elif "gu_prev" in tstate:
                    W("tensor", "vS", tstate["gu_prev"][6 + ib][1])
                    W("tensor", "scS", tstate["gu_prev"][6 + ib][0])
                for ck in range(8):
                    E("tensor", lambda e, p=pG, ck=ck, ib=ib: e.matmul(
                        p[:, :], qh2T[:, ck * 128:(ck + 1) * 128],
                        wgq[:, ck * INNER + ib * 512:ck * INNER + ib * 512 + 512],
                        start=(ck == 0), stop=(ck == 7)), "tS" if ck == 7 else None)
                for ck in range(8):
                    E("tensor", lambda e, p=pU, ck=ck, ib=ib: e.matmul(
                        p[:, :], qh2T[:, ck * 128:(ck + 1) * 128],
                        wuqq[:, ck * INNER + ib * 512:ck * INNER + ib * 512 + 512],
                        start=(ck == 0), stop=(ck == 7)), "tS" if ck == 7 else None)
                mm_done = cnt["tS"]
                W("scalar", "tS", mm_done)
                W("scalar", "vS", cnt["vS"])
                E("scalar", lambda e, p=pG: e.activation(
                    gsil[:, :], p[:, :], AF.Silu, scale=inv_sb[:, 0:1]), "scS")
                W("vector", "scS", cnt["scS"])
                if ib == 0 and "qgu_tr_last" in tstate:
                    W("vector", "tS", tstate["qgu_tr_last"])
                E("vector", lambda e, p=pU, ib=ib: e.scalar_tensor_tensor(
                    gu_sb[:, ib * 512:(ib + 1) * 512], p[:, :], inv_sb[:, 1:2], gsil[:, :],
                    ALU.mult, ALU.mult), "vS")
                gu_done.append((cnt["scS"], cnt["vS"]))
                if ib == 7:
                    tstate["gate_last"] = cnt["tS"]
                    tstate["gu_prev"] = list(gu_done)
            ECH("vector", lambda e: e.tensor_reduce(am_sb[:, 1:2], gu_sb[:, :], AX.X, ALU.max,
                                                  apply_absolute_value=True))
            ECH("vector", lambda e: e.tensor_scalar(am_sb[:, 1:2], am_sb[:, 1:2], 1e-5, None, ALU.max))
            ECH("vector", lambda e: e.reciprocal(sa_sb[:, 1:2], am_sb[:, 1:2]))
            ECH("vector", lambda e: e.tensor_scalar(sa_sb[:, 1:2], sa_sb[:, 1:2], 127.0, None, ALU.mult))
            E("vector", lambda e: e.tensor_mul(inv_sb[:, 2:3], am_sb[:, 1:2], mwv[:, 2:3]))
            E("vector", lambda e: e.tensor_scalar(gu_sb[:, :], gu_sb[:, :], sa_sb[:, 1:2], MAGIC,
                                                  ALU.mult, ALU.add))
            E("vector", lambda e: e.tensor_scalar(gu_sb[:, :], gu_sb[:, :], MAGIC, None,
                                                  ALU.subtract), "vS")
            qgu_done = cnt["vS"]
            ev_rounds = []
            for rr in range(8):
                W("tensor", "vS", qgu_done)
                if rr > 0:
                    W("tensor", "vS", ev_rounds[rr - 1])
                elif "ev7_prev" in tstate:
                    W("tensor", "vS", tstate["ev7_prev"])
                for ck in range(4):
                    E("tensor", lambda e, rr=rr, ck=ck: e.transpose(
                        psTrCf[:, ck * 128:(ck + 1) * 128],
                        gu_sb[:, rr * 512 + ck * 128:rr * 512 + (ck + 1) * 128],
                        ident_f[:, :]), "tS" if ck == 3 else None)
                W("vector", "tS", cnt["tS"])
                if rr == 0 and "down_mm_last" in tstate:
                    W("vector", "tS", tstate["down_mm_last"])
                E("vector", lambda e, rr=rr: e.tensor_copy(
                    qguT[:, rr * 512:(rr + 1) * 512], psTrCf[:, :]), "vS")
                ev_rounds.append(cnt["vS"])
                if rr == 7:
                    tstate["qgu_tr_last"] = cnt["tS"]
                    tstate["ev7_prev"] = cnt["vS"]
            for nbk in range(2):
                pD = [psD0, psD1][nbk]
                W("tensor", "vS", ev_rounds[7])
                if ti > 0 and nbk == 0:
                    W("tensor", "vS", tstate["dn_prev"])
                for ic in range(32):
                    E("tensor", lambda e, p=pD, ic=ic, nbk=nbk: e.matmul(
                        p[:, :], qguT[:, ic * 128:(ic + 1) * 128],
                        wdqq[:, ic * D + nbk * 512:ic * D + nbk * 512 + 512],
                        start=(ic == 0), stop=(ic == 31)), "tS" if ic == 31 else None)
                W("vector", "tS", cnt["tS"])
                E("vector", lambda e, p=pD, ti=ti, nbk=nbk: e.scalar_tensor_tensor(
                    X[:, ti * D + nbk * 512:ti * D + nbk * 512 + 512], p[:, :], inv_sb[:, 2:3],
                    X[:, ti * D + nbk * 512:ti * D + nbk * 512 + 512], ALU.mult, ALU.add), "vS")
            tile_done[ti] = cnt["vS"]
            tstate["dn_prev"] = cnt["vS"]
            tstate["down_mm_last"] = cnt["tS"]
            W("gpsimd", "vS", tile_done[ti])
            E("gpsimd", lambda e, ti=ti: e.dma_start(
                out=dout[ti * 128:(ti + 1) * 128, :], in_=X[:, ti * D:(ti + 1) * D]), "dS")
            if dbg == "C1" and ti == 0:
                _dump([("qh2", qh2_sb[:, :]), ("qh2T", qh2T[:, :]), ("qguT", qguT[:, :]),
                       ("am", am_sb[:, :]), ("sa", sa_sb[:, :]), ("inv", inv_sb[:, :]),
                       ("X0", X[:, 0:1024]), ("gsil", gsil[:, :]), ("qgu", gu_sb[:, :]),
                       ("rstds", rstds[:, :]), ("negmr", negmr[:, :])])
                break

        W("gpsimd", "dS", cnt["dS"])

    with nc.Block() as block:
        def mk(engname):
            def run(eng):
                for op in OPS[engname]:
                    if op[0] == "w":
                        eng.wait_ge(sems[op[1]], op[2])
                    else:
                        ins = op[1](eng)
                        if op[2] is not None:
                            ins.then_inc(sems[op[2][0]], op[2][1])
            return run
        block.sync(mk("sync"))
        block.tensor(mk("tensor"))
        block.vector(mk("vector"))
        block.scalar(mk("scalar"))
        block.gpsimd(mk("gpsimd"))
    return nc


def _masks_np(half):
    m = np.zeros((128, 2560), dtype=np.float32)
    ii = np.arange(128)[:, None]
    offs03 = [(0, 128), (128, 256), (384, 384), (768, 512)]
    for jp, (o, w) in enumerate(offs03):
        if half == 0:
            m[:, o:o + w] = 1.0
        else:
            jj = np.arange(w)[None, :]
            m[:, o:o + w] = (jj >= 128 * jp + ii).astype(np.float32)
    offs47 = [(1280, 512), (1792, 384), (2176, 256), (2432, 128)]
    for j4, (o, w) in enumerate(offs47):
        if half == 1:
            m[:, o:o + w] = 0.0
        else:
            jj = np.arange(w)[None, :]
            m[:, o:o + w] = (jj >= ii).astype(np.float32)
    return np.asarray(m, dtype=ml_dtypes.bfloat16)


def kernel(L0_attn_x, L0_attn_W_dq, L0_attn_W_uq, L0_attn_W_dkv, L0_attn_W_uk,
           L0_attn_W_uv, L0_attn_W_o, L0_ffn_w_gate, L0_ffn_w_up, L0_ffn_w_down):
    if "nc" not in _CACHE:
        _CACHE["nc"] = _build_nc()
    nc = _CACHE["nc"]

    x = np.ascontiguousarray(np.asarray(L0_attn_x, dtype=np.float32))
    shared = {
        "w_dq": np.ascontiguousarray(np.asarray(L0_attn_W_dq, dtype=np.float32)),
        "w_dkv": np.ascontiguousarray(np.asarray(L0_attn_W_dkv, dtype=np.float32)),
        "w_uq": np.ascontiguousarray(np.asarray(L0_attn_W_uq, dtype=np.float32)),
        "w_uk": np.ascontiguousarray(np.asarray(L0_attn_W_uk, dtype=np.float32)),
        "w_uv": np.ascontiguousarray(np.asarray(L0_attn_W_uv, dtype=np.float32)),
        "w_o": np.ascontiguousarray(np.asarray(L0_attn_W_o, dtype=np.float32)),
        "wg_t": np.ascontiguousarray(np.asarray(L0_ffn_w_gate, dtype=np.float32).T),
        "wu_t": np.ascontiguousarray(np.asarray(L0_ffn_w_up, dtype=np.float32).T),
        "wd_t": np.ascontiguousarray(np.asarray(L0_ffn_w_down, dtype=np.float32).T),
    }
    blocks_of = {0: [1, 3, 5, 7], 1: [0, 2, 4, 6]}
    in_maps = []
    for c in range(8):
        b, h = c >> 1, c & 1
        im = dict(shared)
        im["x_own"] = np.ascontiguousarray(
            np.concatenate([x[b, 512 * k:512 * (k + 1)] for k in blocks_of[h]], axis=0))
        im["x_all"] = np.ascontiguousarray(x[b])
        im["masks"] = _masks_np(h)
        in_maps.append(im)

    res = run_bass_kernel_spmd(nc, in_maps, core_ids=list(range(8)))
    out = np.empty((NBATCH, SEQ, D), dtype=np.float32)
    for c in range(8):
        b, h = c >> 1, c & 1
        r = np.asarray(res.results[c]["out"])
        for j, k in enumerate(blocks_of[h]):
            out[b, 512 * k:512 * (k + 1)] = r[512 * j:512 * (j + 1)]
    return out


# version 16
# speedup vs baseline: 1.9149x; 1.3040x over previous
import numpy as np
import ml_dtypes
import concourse.bass as bass
import concourse.mybir as mybir
from concourse.bass_utils import run_bass_kernel_spmd

F32 = mybir.dt.float32
BF16 = mybir.dt.bfloat16
FP8 = mybir.dt.float8e4
AF = mybir.ActivationFunctionType
ALU = mybir.AluOpType
AX = mybir.AxisListType

D = 1024
SEQ = 4096
NBATCH = 4
DC = 256
INNER = 4096
NT_OWN = 16
NT_ALL = 32
KTU = [8, 16, 24, 32]
MAGIC = 12582912.0
LN_EPS = 1e-5
C_SAFETY = -0.1275

_CACHE = {}


def _build_nc(dbg=None):
    nc = bass.Bass("TRN2", target_bir_lowering=False)

    dx_own = nc.dram_tensor("x_own", [2048, D], F32, kind="ExternalInput").ap()
    dx_all = nc.dram_tensor("x_all", [SEQ, D], F32, kind="ExternalInput").ap()
    dw_dq = nc.dram_tensor("w_dq", [D, DC], F32, kind="ExternalInput").ap()
    dw_dkv = nc.dram_tensor("w_dkv", [D, DC], F32, kind="ExternalInput").ap()
    dw_uq = nc.dram_tensor("w_uq", [DC, D], F32, kind="ExternalInput").ap()
    dw_uk = nc.dram_tensor("w_uk", [DC, D], F32, kind="ExternalInput").ap()
    dw_uv = nc.dram_tensor("w_uv", [DC, D], F32, kind="ExternalInput").ap()
    dw_o = nc.dram_tensor("w_o", [D, D], F32, kind="ExternalInput").ap()
    dwg = nc.dram_tensor("wg_t", [D, INNER], F32, kind="ExternalInput").ap()
    dwu = nc.dram_tensor("wu_t", [D, INNER], F32, kind="ExternalInput").ap()
    dwd = nc.dram_tensor("wd_t", [INNER, D], F32, kind="ExternalInput").ap()
    dmasks = nc.dram_tensor("masks", [128, 2560], BF16, kind="ExternalInput").ap()
    dout = nc.dram_tensor("out", [2048, D], F32, kind="ExternalOutput").ap()

    ident_c = nc.inline_tensor(np.eye(128, dtype=ml_dtypes.bfloat16), "ident_c")
    identf_c = nc.inline_tensor(np.eye(128, dtype=np.float32), "identf_c")
    onescol_c = nc.inline_tensor(np.ones((128, 2), dtype=ml_dtypes.bfloat16), "onescol_c")
    onescol_f_c = nc.inline_tensor(np.ones((128, 1), dtype=np.float32), "onescol_f_c")
    onesrow_f_c = nc.inline_tensor(np.ones((1, 128), dtype=np.float32), "onesrow_f_c")
    bo = np.zeros((128, 2), dtype=np.float32)
    bo[0:64, 0] = 1.0
    bo[64:128, 1] = 1.0
    blockones_c = nc.inline_tensor(bo, "blockones_c")
    sel = np.zeros((2, 256), dtype=np.float32)
    sel[0, 0:128] = 1.0
    sel[1, 128:256] = 1.0
    selAB_c = nc.inline_tensor(sel, "selAB_c")
    selE = np.zeros((16, 8 * 128), dtype=np.float32)
    for hp in range(8):
        selE[2 * hp, hp * 128:hp * 128 + 64] = 1.0
        selE[2 * hp + 1, hp * 128 + 64:hp * 128 + 128] = 1.0
    selE_c = nc.inline_tensor(selE, "selE_c")

    base0 = (nc.sbuf_base + 31) // 32 * 32
    top = nc.sbuf_top // 32 * 32
    cur = [base0]

    sboff = {}

    def sb(name, shape, dtype):
        o = cur[0]
        t = nc.alloc_sbuf_tensor_at(name, shape, dtype, offset=o)
        sboff[name] = o
        nbytes = shape[1] * mybir.dt.size(dtype)
        cur[0] = o + ((nbytes + 31) // 32) * 32
        assert cur[0] <= top, f"sbuf overflow at {name}"
        return t

    X = sb("X", [128, NT_OWN * D], F32)
    ident = sb("ident", [128, 128], BF16)
    ident_f = sb("ident_f", [128, 128], F32)
    onescol = sb("onescol", [128, 2], BF16)
    onescol_f = sb("onescol_f", [128, 1], F32)
    onesrow_f = sb("onesrow_f", [1, 128], F32)
    blockones = sb("blockones", [128, 2], F32)
    selAB = sb("selAB", [2, 256], F32)
    selEs = sb("selEs", [16, 8 * 128], F32)
    masks = sb("masks", [128, 2560], BF16)
    sums = sb("sums", [128, 4], F32)
    sumsq = sb("sumsq", [128, 4], F32)
    rstds = sb("rstds", [128, 4], F32)
    negmr = sb("negmr", [128, 4], F32)
    tmp4a = sb("tmp4a", [128, 4], F32)
    tmp4b = sb("tmp4b", [128, 4], F32)
    biasA = sb("biasA", [128, 1], F32)
    biasB = sb("biasB", [128, 1], F32)
    qk_sb = sb("qk_sb", [2, 2], F32)
    qk2_sb = sb("qk2_sb", [2, 1], F32)
    ltmp = sb("ltmp", [33, 512], F32)
    wst = sb("wst", [128, 192], F32)
    wtot = sb("wtot", [128, 3], F32)
    wsc = sb("wsc", [1, 8], F32)
    swv = sb("swv", [128, 3], F32)
    mwv = sb("mwv", [128, 3], F32)
    am_sb = sb("am_sb", [128, 2], F32)
    sa_sb = sb("sa_sb", [128, 2], F32)
    inv_sb = sb("inv_sb", [128, 3], F32)
    scr4k = sb("scr4k", [128, D], F32)

    R = cur[0]
    avail = top - R
    assert avail >= 126976, f"era region too small: {avail}"

    def at(name, shape, dtype, off, size):
        nbytes = shape[1] * mybir.dt.size(dtype)
        assert nbytes <= size, f"{name}: {nbytes} > {size}"
        assert R + off + nbytes <= top, f"{name} beyond top"
        return nc.alloc_sbuf_tensor_at(name, shape, dtype, offset=R + off)

    xstage = at("xstage", [128, 2 * D], F32, 0, 8192)
    h1tok = at("h1tok", [128, 2 * D], BF16, 8192, 4096)
    h1T = at("h1T", [128, 8 * 512], BF16, 12288, 8192)
    cqT = at("cqT", [128, 2 * 2048], BF16, 20480, 8192)
    ckvT = at("ckvT", [128, 2 * SEQ], BF16, 28672, 16384)
    wdq_sb = at("wdq_sb", [128, 8 * DC], BF16, 45056, 4096)
    wdkv_sb = at("wdkv_sb", [128, 8 * DC], BF16, 49152, 4096)
    fstage = at("fstage", [128, 2 * 512], F32, 53248, 4096)
    wuq_sb = at("wuq_sb", [128, 2 * D], BF16, 57344, 4096)
    wuk_sb = at("wuk_sb", [128, 2 * D], BF16, 61440, 4096)
    wuv_sb = at("wuv_sb", [128, 2 * D], BF16, 65536, 4096)
    wo_sb = at("wo_sb", [128, 8 * D], BF16, 69632, 16384)
    OT = at("OT", [128, 8 * 2048], BF16, 86016, 32768)
    l_all = at("l_all", [16, 2048], F32, 118784, 8192)
    wf_dq = at("wf_dq", [128, 8 * DC], F32, 86016, 8192)
    wf_dkv = at("wf_dkv", [128, 8 * DC], F32, 94208, 8192)
    wf_uq = at("wf_uq", [128, 2 * D], F32, 102400, 8192)
    wf_uk = at("wf_uk", [128, 2 * D], F32, 110592, 8192)
    wf_uv = at("wf_uv", [128, 2 * D], F32, 118784, 8192)
    QT = at("QT", [128, 2048], BF16, 0, 4096)
    PT = at("PT", [128, 8 * 512], BF16, 4096, 8192)
    sqscr = at("sqscr", [128, 2048], BF16, 8192, 4096)
    KT_s = at("KT_s", [128, SEQ], BF16, 12288, 8192)
    V_s = at("V_s", [128, SEQ], BF16, 45056, 8192)
    wgq = at("wgq", [128, 8 * INNER], FP8, 20480, 32768)
    wuqq = at("wuqq", [128, 8 * INNER], FP8, 57344, 32768)
    wdqq = at("wdqq", [128, 32 * D], FP8, 90112, 32768)
    gu_sb = at("gu_sb", [128, INNER], F32, 0, 16384)
    fring = at("fring", [128, 8 * 512], F32, 0, 16384)
    qh2T = at("qh2T", [128, 8 * 128], BF16, 122880, 2048)
    qh2_sb = at("qh2_sb", [128, D], BF16, 124928, 2048)
    qguT = at("qguT", [128, 32 * 128], BF16, 0, 8192)
    gsil = nc.alloc_sbuf_tensor_at("gsil", [128, 512], F32, offset=sboff["scr4k"])
    dbgb = at("dbgb", [128, 1024], F32, 20480, 32768) if dbg else None
    assert R + 126976 <= top

    def ps(name, shape, dtype, bank):
        nc.psum_base = bank
        return nc.alloc_psum_tensor(name, shape, dtype)

    psTrA = ps("psTrA", [128, 1024], BF16, 0)
    psP0 = ps("psP0", [128, 512], F32, 1)
    psP1 = ps("psP1", [128, 512], F32, 2)
    psSA = ps("psSA", [128, 1024], F32, 0)
    psSB = ps("psSB", [128, 1024], F32, 2)
    psSA0 = ps("psSA0", [128, 512], F32, 0)
    psSA1 = ps("psSA1", [128, 512], F32, 1)
    psSB0 = ps("psSB0", [128, 512], F32, 2)
    psSB1 = ps("psSB1", [128, 512], F32, 3)
    psLA = ps("psLA", [128, 512], F32, 4)
    psOA = ps("psOA", [128, 512], F32, 5)
    psLB = ps("psLB", [128, 512], F32, 6)
    psOB = ps("psOB", [128, 512], F32, 7)
    psN0 = ps("psN0", [2, 1], F32, 4)
    psN1 = ps("psN1", [2, 1], F32, 5)
    psBC0 = ps("psBC0", [128, 1], F32, 6)
    psBC1 = ps("psBC1", [128, 1], F32, 7)
    psW0 = ps("psW0", [128, 512], F32, 0)
    psW1 = ps("psW1", [128, 512], F32, 1)
    psR = ps("psR", [128, 512], F32, 2)
    psT1 = ps("psT1", [1, 1], F32, 3)
    psTrC = ps("psTrC", [128, 1024], BF16, 7)
    psTrCf = ps("psTrCf", [128, 512], F32, 0)
    psTrCf2 = ps("psTrCf2", [128, 512], F32, 7)
    psG0 = ps("psG0", [128, 512], F32, 1)
    psG1 = ps("psG1", [128, 512], F32, 2)
    psU0 = ps("psU0", [128, 512], F32, 3)
    psU1 = ps("psU1", [128, 512], F32, 4)
    psD0 = ps("psD0", [128, 512], F32, 5)
    psD1 = ps("psD1", [128, 512], F32, 6)
    nc.psum_base = 8

    DMA_SEMS = ("dL", "dS", "dF", "dQ", "dQ2")
    sem_names = ["dL", "dS", "dF", "dQ", "dQ2", "tS", "vS", "scS", "gS"]
    sems = {n: nc.alloc_semaphore(n) for n in sem_names}
    cnt = {n: 0 for n in sem_names}
    OPS = {"sync": [], "tensor": [], "vector": [], "scalar": [], "gpsimd": []}

    def W(eng, sem, val):
        if val > 0:
            OPS[eng].append(("w", sem, val))

    def E(eng, fn, inc=None):
        n = 16 if inc in DMA_SEMS else 1
        OPS[eng].append(("e", fn, (inc, n) if inc else None))
        if inc:
            cnt[inc] += n

    _SELF = {"vector": "vS", "scalar": "scS", "gpsimd": "gS", "tensor": "tS"}

    def ECH(eng, fn):
        s = _SELF[eng]
        E(eng, fn, s)
        W(eng, s, cnt[s])

    for dst, src in [(ident, ident_c), (ident_f, identf_c), (onescol, onescol_c), (onescol_f, onescol_f_c),
                     (onesrow_f, onesrow_f_c), (blockones, blockones_c),
                     (selAB, selAB_c), (selEs, selE_c)]:
        E("sync", lambda e, d=dst, s=src: e.dma_start(out=d[:, :], in_=s.ap()), "dL")
    E("sync", lambda e: e.dma_start(out=masks[:, :], in_=dmasks), "dL")
    for t in range(NT_OWN):
        E("sync", lambda e, t=t: e.dma_start(out=X[:, t * D:(t + 1) * D],
                                             in_=dx_own[t * 128:(t + 1) * 128, :]), "dL")
    for ck in range(8):
        E("sync", lambda e, ck=ck: e.dma_start(out=wf_dq[:, ck * DC:(ck + 1) * DC],
                                               in_=dw_dq[ck * 128:(ck + 1) * 128, :]), "dL")
        E("sync", lambda e, ck=ck: e.dma_start(out=wf_dkv[:, ck * DC:(ck + 1) * DC],
                                               in_=dw_dkv[ck * 128:(ck + 1) * 128, :]), "dL")
    for cc in range(2):
        E("sync", lambda e, cc=cc: e.dma_start(out=wf_uq[:, cc * D:(cc + 1) * D],
                                               in_=dw_uq[cc * 128:(cc + 1) * 128, :]), "dL")
        E("sync", lambda e, cc=cc: e.dma_start(out=wf_uk[:, cc * D:(cc + 1) * D],
                                               in_=dw_uk[cc * 128:(cc + 1) * 128, :]), "dL")
        E("sync", lambda e, cc=cc: e.dma_start(out=wf_uv[:, cc * D:(cc + 1) * D],
                                               in_=dw_uv[cc * 128:(cc + 1) * 128, :]), "dL")
    dl_stage5 = cnt["dL"]
    W("vector", "dL", dl_stage5)
    E("vector", lambda e: e.tensor_copy(wdq_sb[:, :], wf_dq[:, :]), "vS")
    E("vector", lambda e: e.tensor_copy(wdkv_sb[:, :], wf_dkv[:, :]), "vS")
    E("vector", lambda e: e.tensor_copy(wuq_sb[:, :], wf_uq[:, :]), "vS")
    E("vector", lambda e: e.tensor_copy(wuk_sb[:, :], wf_uk[:, :]), "vS")
    E("vector", lambda e: e.tensor_copy(wuv_sb[:, :], wf_uv[:, :]), "vS")
    wo_cast = []
    for pc in range(16):
        ck, hh = pc // 2, (pc % 2) * 512
        if pc >= 2:
            W("sync", "vS", wo_cast[pc - 2])
        E("sync", lambda e, ck=ck, hh=hh, sl=pc % 2: e.dma_start(
            out=fstage[:, sl * 512:(sl + 1) * 512],
            in_=dw_o[ck * 128:(ck + 1) * 128, hh:hh + 512]), "dL")
        W("vector", "dL", cnt["dL"])
        E("vector", lambda e, ck=ck, hh=hh, sl=pc % 2: e.tensor_copy(
            wo_sb[:, ck * D + hh:ck * D + hh + 512], fstage[:, sl * 512:(sl + 1) * 512]), "vS")
        wo_cast.append(cnt["vS"])
    wcast_all = cnt["vS"]
    dl_head = cnt["dL"]

    h1evac = {}
    projevac = []
    xload = {}
    h1done = {}
    sqdone = {}
    tr_done = {}
    chain_done = {}
    npr = 0
    for g in range(12):
        own = g >= 8
        for jp in range(2):
            j0, j1 = 2 * jp, 2 * jp + 1
            tis, xaps = [], []
            for j in (j0, j1):
                ti = g * 4 + j
                if not own:
                    prev = ti - 2
                    if prev >= 0 and prev in h1done:
                        W("sync", "vS", h1done[prev])
                        W("sync", "scS", sqdone[prev])
                    E("sync", lambda e, ti=ti, sl=ti % 2: e.dma_start(
                        out=xstage[:, sl * D:(sl + 1) * D],
                        in_=dx_all[ti * 128:(ti + 1) * 128, :]), "dL")
                    xload[ti] = cnt["dL"]
                    xap = xstage[:, (ti % 2) * D:((ti % 2) + 1) * D]
                else:
                    xload[ti] = dl_head
                    xap = X[:, (ti - 32) * D:(ti - 31) * D]
                tis.append(ti)
                xaps.append(xap)
            W("vector", "dL", xload[tis[0]])
            E("vector", lambda e, x=xaps[0], j=j0: e.reduce_sum(sums[:, j:j + 1], x, AX.X), "vS")
            W("vector", "dL", xload[tis[1]])
            ECH("vector", lambda e, x=xaps[1], j=j1: e.reduce_sum(sums[:, j:j + 1], x, AX.X))
            W("scalar", "dL", xload[tis[0]])
            if tis[0] >= 1:
                W("scalar", "vS", h1done.get(tis[0] - 1, 0))
            E("scalar", lambda e, x=xaps[0], j=j0: e.activation(
                scr4k[:, :], x, AF.Square, accum_out=sumsq[:, j:j + 1]), "scS")
            W("scalar", "dL", xload[tis[1]])
            E("scalar", lambda e, x=xaps[1], j=j1: e.activation(
                scr4k[:, :], x, AF.Square, accum_out=sumsq[:, j:j + 1]), "scS")
            sqdone[tis[0]] = sqdone[tis[1]] = cnt["scS"]
            W("vector", "scS", sqdone[tis[1]])
            ECH("vector", lambda e, j=j0: e.tensor_scalar(
                tmp4a[:, j:j + 2], sums[:, j:j + 2], 1.0 / D, None, ALU.mult))
            ECH("vector", lambda e, j=j0: e.tensor_mul(
                tmp4b[:, j:j + 2], tmp4a[:, j:j + 2], tmp4a[:, j:j + 2]))
            ECH("vector", lambda e, j=j0: e.tensor_scalar(
                tmp4b[:, j:j + 2], tmp4b[:, j:j + 2], -1.0, None, ALU.mult))
            ECH("vector", lambda e, j=j0: e.scalar_tensor_tensor(
                tmp4b[:, j:j + 2], sumsq[:, j:j + 2], 1.0 / D,
                tmp4b[:, j:j + 2], ALU.mult, ALU.add))
            E("vector", lambda e, j=j0: e.tensor_scalar(
                rstds[:, j:j + 2], tmp4b[:, j:j + 2], LN_EPS, None, ALU.add), "vS")
            W("scalar", "vS", cnt["vS"])
            E("scalar", lambda e, j=j0: e.activation(
                rstds[:, j:j + 2], rstds[:, j:j + 2], AF.Sqrt), "scS")
            W("vector", "scS", cnt["scS"])
            ECH("vector", lambda e, j=j0: e.reciprocal(rstds[:, j:j + 2], rstds[:, j:j + 2]))
            ECH("vector", lambda e, j=j0: e.scalar_tensor_tensor(
                negmr[:, j:j + 2], tmp4a[:, j:j + 2], -1.0, rstds[:, j:j + 2],
                ALU.mult, ALU.mult))
            for j, ti, xap in ((j0, tis[0], xaps[0]), (j1, tis[1], xaps[1])):
                sl = j % 2
                slot_prev = (g, j - 2) if j >= 2 else (g - 1, j + 2)
                if slot_prev in tr_done:
                    W("vector", "tS", tr_done[slot_prev])
                E("vector", lambda e, x=xap, j=j, sl=sl: e.tensor_scalar(
                    h1tok[:, sl * D:(sl + 1) * D], x, rstds[:, j:j + 1], negmr[:, j:j + 1],
                    ALU.mult, ALU.add), "vS")
                h1done[ti] = cnt["vS"]
                W("tensor", "vS", h1done[ti])
                tr_prev = (g, j - 1) if j >= 1 else (g - 1, 3)
                if tr_prev in h1evac:
                    W("tensor", "vS", h1evac[tr_prev])
                for ck in range(8):
                    E("tensor", lambda e, sl=sl, ck=ck: e.transpose(
                        psTrA[:, ck * 128:(ck + 1) * 128],
                        h1tok[:, sl * D + ck * 128:sl * D + (ck + 1) * 128], ident[:, :]),
                        "tS" if ck == 7 else None)
                tr_done[(g, j)] = cnt["tS"]
                W("vector", "tS", tr_done[(g, j)])
                E("vector", lambda e, j=j: e.tensor_copy(
                    h1T[:, :].rearrange("p (c t) -> p c t", c=8)[:, :, j * 128:(j + 1) * 128],
                    psTrA[:, :].rearrange("p (c t) -> p c t", c=8)), "vS")
                h1evac[(g, j)] = cnt["vS"]
        wtile = wdq_sb if own else wdkv_sb
        dstT = cqT if own else ckvT
        gcol = (g - 8) * 512 if own else g * 512
        span = 2048 if own else SEQ
        for m in range(2):
            bank = [psP0, psP1][npr % 2]
            if npr >= 2:
                W("tensor", "vS", projevac[npr - 2])
            W("tensor", "vS", h1evac[(g, 3)])
            W("tensor", "vS", wcast_all)
            for ck in range(8):
                E("tensor", lambda e, b=bank, wt=wtile, ck=ck, m=m: e.matmul(
                    b[:, :], wt[:, ck * DC + m * 128:ck * DC + (m + 1) * 128],
                    h1T[:, ck * 512:(ck + 1) * 512],
                    start=(ck == 0), stop=(ck == 7)), "tS" if ck == 7 else None)
            W("vector", "tS", cnt["tS"])
            E("vector", lambda e, b=bank, d=dstT, m=m, gc=gcol, sp=span: e.tensor_copy(
                d[:, m * sp + gc:m * sp + gc + 512], b[:, :]), "vS")
            projevac.append(cnt["vS"])
            npr += 1
    phaseA_tS = cnt["tS"]
    phaseA_vS = cnt["vS"]

    def _dump(pairs):
        for nm, ap in pairs:
            dt_ = ap.dtype
            dten = nc.dram_tensor("dump_" + nm, list(ap.shape), dt_,
                                  kind="ExternalOutput").ap()
            W("gpsimd", "vS", cnt["vS"])
            W("gpsimd", "scS", cnt["scS"])
            W("gpsimd", "tS", cnt["tS"])
            E("gpsimd", lambda e, d=dten, s=ap: e.dma_start(out=d, in_=s), "dS")
        W("gpsimd", "dS", cnt["dS"])

    if dbg == "A":
        _dump([("ckvT", ckvT[:, :]), ("cqT", cqT[:, :]), ("wdq", wdq_sb[:, :]),
               ("wuk", wuk_sb[:, :]), ("wo", wo_sb[:, :]), ("X", X[:, 0:2048]),
               ("h1T", h1T[:, :]), ("h1tok", h1tok[:, :]), ("sums", sums[:, :]),
               ("sumsq", sumsq[:, :]), ("rstds", rstds[:, :]), ("negmr", negmr[:, :]),
               ("tmp4a", tmp4a[:, :]), ("tmp4b", tmp4b[:, :])])

    ffn_srcs = []
    for wi, dw in enumerate([dwg, dwu, dwd]):
        nrow = D if wi < 2 else INNER
        ncolw = INNER if wi < 2 else D
        idx = 0
        for ck in range(nrow // 128):
            for hh in range(0, ncolw, 512):
                ffn_srcs.append((wi, idx, dw[ck * 128:(ck + 1) * 128, hh:hh + 512]))
                idx += 1
    assert len(ffn_srcs) == 192
    ffn_ptr = [0]
    ffn_gdone = []

    def emit_ffn_mean(nsteps):
        for _ in range(nsteps):
            i = ffn_ptr[0]
            if i >= len(ffn_srcs):
                return
            wi, idx, src = ffn_srcs[i]
            slot = i % 2
            if i < 2:
                W("gpsimd", "vS", wcast_all)
            if i >= 2:
                W("gpsimd", "scS", ffn_gdone[i - 2])
            E("gpsimd", lambda e, s=src, sl=slot: e.dma_start(
                out=fstage[:, sl * 512:(sl + 1) * 512], in_=s), "dF")
            W("scalar", "dF", cnt["dF"])
            col = wi * 64 + idx
            E("scalar", lambda e, sl=slot, col=col: e.activation(
                fstage[:, sl * 512:(sl + 1) * 512], fstage[:, sl * 512:(sl + 1) * 512], AF.Abs,
                accum_out=wst[:, col:col + 1]), "scS")
            ffn_gdone.append(cnt["scS"])
            ffn_ptr[0] += 1

    if dbg != "A":
        u_evac = {}
        exp_cnt = {}
        expA_cnt = {}
        expB_cnt = {}
        mask_cnt = {}
        lav_cnt = {}
        s_cnt = {}
        last_exp_prev_hp = 0
        last_lav_prev = 0

        for hp in range(8):
            W("tensor", "scS", last_exp_prev_hp)
            if hp == 0:
                W("tensor", "vS", phaseA_vS)
            else:
                W("tensor", "vS", u_evac[(hp - 1, 3)])
            bev = []
            for nb in range(8):
                bank = [psSA0, psSA1][nb % 2]
                if nb >= 2:
                    W("tensor", "vS", bev[nb - 2])
                for cc in range(2):
                    E("tensor", lambda e, b=bank, cc=cc, hp=hp, nb=nb: e.matmul(
                        b[:, :], wuk_sb[:, cc * D + hp * 128:cc * D + (hp + 1) * 128],
                        ckvT[:, cc * SEQ + nb * 512:cc * SEQ + nb * 512 + 512],
                        start=(cc == 0), stop=(cc == 1)), "tS" if cc == 1 else None)
                W("vector", "tS", cnt["tS"])
                E("vector", lambda e, b=bank, nb=nb: e.tensor_copy(
                    KT_s[:, nb * 512:(nb + 1) * 512], b[:, :]), "vS")
                bev.append(cnt["vS"])
            for nb in range(4):
                bank = [psSB0, psSB1][nb % 2]
                if nb >= 2:
                    W("tensor", "vS", bev[8 + nb - 2])
                for cc in range(2):
                    E("tensor", lambda e, b=bank, cc=cc, hp=hp, nb=nb: e.matmul(
                        b[:, :], wuq_sb[:, cc * D + hp * 128:cc * D + (hp + 1) * 128],
                        cqT[:, cc * 2048 + nb * 512:cc * 2048 + nb * 512 + 512],
                        start=(cc == 0), stop=(cc == 1)), "tS" if cc == 1 else None)
                W("vector", "tS", cnt["tS"])
                E("vector", lambda e, b=bank, nb=nb: e.tensor_copy(
                    QT[:, nb * 512:(nb + 1) * 512], b[:, :]), "vS")
                bev.append(cnt["vS"])
            for kt in range(32):
                bank = [psLA, psOA][kt % 2]
                if kt >= 2:
                    W("tensor", "vS", bev[12 + kt - 2])
                for cc in range(2):
                    E("tensor", lambda e, b=bank, cc=cc, hp=hp, kt=kt: e.matmul(
                        b[:, 0:128], ckvT[:, cc * SEQ + kt * 128:cc * SEQ + (kt + 1) * 128],
                        wuv_sb[:, cc * D + hp * 128:cc * D + (hp + 1) * 128],
                        start=(cc == 0), stop=(cc == 1)), "tS" if cc == 1 else None)
                W("vector", "tS", cnt["tS"])
                E("vector", lambda e, b=bank, kt=kt: e.tensor_copy(
                    V_s[:, kt * 128:(kt + 1) * 128], b[:, 0:128]), "vS")
                bev.append(cnt["vS"])
            kv_built = cnt["vS"]

            W("scalar", "vS", kv_built)
            W("scalar", "tS", cnt["tS"])
            E("scalar", lambda e: e.activation(sqscr[:, :], QT[:, :], AF.Square), "scS")
            W("vector", "scS", cnt["scS"])
            E("vector", lambda e: e.reduce_max(tmp4a[:, 0:1], sqscr[:, :], AX.X), "vS")
            E("scalar", lambda e: e.activation(sqscr[:, :], KT_s[:, 0:2048], AF.Square), "scS")
            W("vector", "scS", cnt["scS"])
            E("vector", lambda e: e.reduce_max(tmp4a[:, 1:2], sqscr[:, :], AX.X), "vS")
            W("scalar", "vS", cnt["vS"])
            E("scalar", lambda e: e.activation(sqscr[:, :], KT_s[:, 2048:4096], AF.Square), "scS")
            W("vector", "scS", cnt["scS"])
            ECH("vector", lambda e: e.reduce_max(tmp4b[:, 0:1], sqscr[:, :], AX.X))
            E("vector", lambda e: e.tensor_max(tmp4a[:, 1:2], tmp4a[:, 1:2], tmp4b[:, 0:1]), "vS")
            W("tensor", "vS", cnt["vS"])
            E("tensor", lambda e: e.matmul(psN0[:, :], blockones[:, :], tmp4a[:, 0:1],
                                           start=True, stop=True), "tS")
            E("tensor", lambda e: e.matmul(psN1[:, :], blockones[:, :], tmp4a[:, 1:2],
                                           start=True, stop=True), "tS")
            W("vector", "tS", cnt["tS"])
            E("vector", lambda e: e.tensor_copy(qk_sb[:, 0:1], psN0[:, :]), "vS")
            E("vector", lambda e: e.tensor_copy(qk_sb[:, 1:2], psN1[:, :]), "vS")
            W("vector", "vS", cnt["vS"])
            W("scalar", "vS", cnt["vS"])
            E("scalar", lambda e: e.activation(qk_sb[:, :], qk_sb[:, :], AF.Sqrt), "scS")
            W("vector", "scS", cnt["scS"])
            ECH("vector", lambda e: e.tensor_scalar(qk2_sb[:, :], qk_sb[:, 0:1], qk_sb[:, 1:2],
                                                    C_SAFETY, ALU.mult, ALU.mult))
            W("tensor", "vS", cnt["vS"])
            E("tensor", lambda e: e.matmul(psBC0[:, :], selAB[:, 0:128], qk2_sb[:, :],
                                           start=True, stop=True), "tS")
            E("tensor", lambda e: e.matmul(psBC1[:, :], selAB[:, 128:256], qk2_sb[:, :],
                                           start=True, stop=True), "tS")
            W("vector", "tS", cnt["tS"])
            E("vector", lambda e: e.tensor_copy(biasA[:, :], psBC0[:, :]), "vS")
            E("vector", lambda e: e.tensor_copy(biasB[:, :], psBC1[:, :]), "vS")
            bias_ready = cnt["vS"]

            emit_ffn_mean(24)

            for u in range(4):
                KT = KTU[u]
                if u == 0:
                    W("tensor", "vS", bias_ready)
                else:
                    W("tensor", "vS", u_evac[(hp, u - 1)])

                def slotbase(kt):
                    return ((kt // 2) % 2) * 2048

                def emit_Spair(p, hp=hp, u=u, KT=KT, bias_ready=bias_ready,
                               last_lav_prev=last_lav_prev):
                    if p >= 1:
                        W("tensor", "scS", expA_cnt[(hp, u, p - 1)])
                    elif u > 0:
                        W("tensor", "scS", exp_cnt[(hp, u - 1, KTU[u - 1] - 1)])
                    else:
                        W("tensor", "vS", bias_ready)
                    for kt in (2 * p, 2 * p + 1):
                        o_ = 0 if kt < KT - 4 else 128 * (kt - (KT - 4))
                        sl = (kt % 2) * 512
                        E("tensor", lambda e, o_=o_, kt=kt, sl=sl: e.matmul(
                            psSA[:, sl + o_:sl + 512], KT_s[0:64, kt * 128:(kt + 1) * 128],
                            QT[0:64, u * 512 + o_:(u + 1) * 512],
                            start=True, stop=True, tile_position=(0, 0)), "tS")
                    s_cntA = cnt["tS"]
                    if p >= 1:
                        W("tensor", "scS", expB_cnt[(hp, u, p - 1)])
                    for kt in (2 * p, 2 * p + 1):
                        o_ = 0 if kt < KT - 4 else 128 * (kt - (KT - 4))
                        sl = (kt % 2) * 512
                        E("tensor", lambda e, o_=o_, kt=kt, sl=sl: e.matmul(
                            psSB[:, sl + o_:sl + 512], KT_s[64:128, kt * 128:(kt + 1) * 128],
                            QT[64:128, u * 512 + o_:(u + 1) * 512],
                            start=True, stop=True, tile_position=(64, 0)), "tS")
                    s_cnt[(hp, u, p)] = cnt["tS"]
                    s_cntB = cnt["tS"]
                    base = slotbase(2 * p)
                    W("scalar", "tS", s_cntA)
                    if p >= 2:
                        W("scalar", "tS", lav_cnt[(hp, u, 2 * p - 3)])
                    else:
                        W("scalar", "tS", last_lav_prev)
                    if p == 0:
                        W("scalar", "vS", bias_ready)
                    if 2 * p >= KT - 4:
                        for kt in (2 * p, 2 * p + 1):
                            o_ = 128 * (kt - (KT - 4))
                            c = 512 - o_
                            sl = (kt % 2) * 512
                            E("scalar", lambda e, o_=o_, c=c, sl=sl, base=base: e.activation(
                                PT[:, base + sl:base + sl + c], psSA[:, sl + o_:sl + 512],
                                AF.Exp, bias=biasA[:, :], scale=0.125), "scS")
                        expA_cnt[(hp, u, p)] = cnt["scS"]
                        W("scalar", "tS", s_cntB)
                        for kt in (2 * p, 2 * p + 1):
                            o_ = 128 * (kt - (KT - 4))
                            c = 512 - o_
                            sl = (kt % 2) * 512
                            E("scalar", lambda e, o_=o_, c=c, sl=sl, base=base: e.activation(
                                PT[:, base + 1024 + sl:base + 1024 + sl + c],
                                psSB[:, sl + o_:sl + 512],
                                AF.Exp, bias=biasB[:, :], scale=0.125), "scS")
                    else:
                        E("scalar", lambda e, base=base: e.activation(
                            PT[:, base:base + 1024], psSA[:, :], AF.Exp,
                            bias=biasA[:, :], scale=0.125), "scS")
                        expA_cnt[(hp, u, p)] = cnt["scS"]
                        W("scalar", "tS", s_cntB)
                        E("scalar", lambda e, base=base: e.activation(
                            PT[:, base + 1024:base + 2048], psSB[:, :], AF.Exp,
                            bias=biasB[:, :], scale=0.125), "scS")
                    expB_cnt[(hp, u, p)] = cnt["scS"]
                    exp_cnt[(hp, u, 2 * p)] = cnt["scS"]
                    exp_cnt[(hp, u, 2 * p + 1)] = cnt["scS"]
                    mkts = [kt for kt in (2 * p, 2 * p + 1) if kt >= KT - 8]
                    if mkts:
                        def mgeom(kt):
                            o_ = 0 if kt < KT - 4 else 128 * (kt - (KT - 4))
                            sl = (kt % 2) * 512
                            jp = kt - (KT - 8)
                            if jp < 4:
                                mw, moff = 128 * (jp + 1), [0, 128, 384, 768][jp]
                            else:
                                mw = 512 - o_
                                moff = [1280, 1792, 2176, 2432][jp - 4]
                            return sl, mw, moff
                        W("vector", "scS", expA_cnt[(hp, u, p)])
                        for kt in mkts:
                            sl, mw, moff = mgeom(kt)
                            E("vector", lambda e, sl=sl, base=base, mw=mw, moff=moff: e.tensor_mul(
                                PT[:, base + sl:base + sl + mw],
                                PT[:, base + sl:base + sl + mw],
                                masks[:, moff:moff + mw]), "vS")
                        W("vector", "scS", expB_cnt[(hp, u, p)])
                        for kt in mkts:
                            sl, mw, moff = mgeom(kt)
                            E("vector", lambda e, sl=sl, base=base, mw=mw, moff=moff: e.tensor_mul(
                                PT[:, base + 1024 + sl:base + 1024 + sl + mw],
                                PT[:, base + 1024 + sl:base + 1024 + sl + mw],
                                masks[:, moff:moff + mw]), "vS")
                        for kt in mkts:
                            mask_cnt[(hp, u, kt)] = cnt["vS"]

                def emit_lav(kt, hp=hp, u=u, KT=KT, waits=True):
                    o_ = 0 if kt < KT - 4 else 128 * (kt - (KT - 4))
                    cols = 512 - o_
                    base = slotbase(kt)
                    sl = (kt % 2) * 512
                    if waits:
                        W("tensor", "scS", exp_cnt[(hp, u, kt)])
                    if (hp, u, kt) in mask_cnt:
                        W("tensor", "vS", mask_cnt[(hp, u, kt)])
                    st_, sp_ = (kt == 0), (kt == KT - 1)
                    E("tensor", lambda e, sl=sl, base=base, o_=o_, c=cols, st_=st_, sp_=sp_: e.matmul(
                        psLA[0:1, o_:512], onescol[:, 0:1], PT[:, base + sl:base + sl + c],
                        start=st_, stop=sp_, tile_position=(0, 0)), "tS")
                    E("tensor", lambda e, sl=sl, base=base, o_=o_, c=cols, st_=st_, sp_=sp_: e.matmul(
                        psLB[32:33, o_:512], onescol[:, 1:2],
                        PT[:, base + 1024 + sl:base + 1024 + sl + c],
                        start=st_, stop=sp_, tile_position=(0, 32)), "tS")
                    E("tensor", lambda e, sl=sl, base=base, o_=o_, kt=kt, c=cols, st_=st_, sp_=sp_: e.matmul(
                        psOA[0:64, o_:512], V_s[:, kt * 128:kt * 128 + 64],
                        PT[:, base + sl:base + sl + c],
                        start=st_, stop=sp_, tile_position=(0, 0)), "tS")
                    E("tensor", lambda e, sl=sl, base=base, o_=o_, kt=kt, c=cols, st_=st_, sp_=sp_: e.matmul(
                        psOB[64:128, o_:512], V_s[:, kt * 128 + 64:(kt + 1) * 128],
                        PT[:, base + 1024 + sl:base + 1024 + sl + c],
                        start=st_, stop=sp_, tile_position=(0, 64)), "tS")
                    lav_cnt[(hp, u, kt)] = cnt["tS"]

                for p in range(KT // 2):
                    emit_Spair(p)
                    if p >= 1:
                        emit_lav(2 * p - 2, waits=False)
                        emit_lav(2 * p - 1, waits=False)
                emit_lav(KT - 2)
                emit_lav(KT - 1)

                last_lav_prev = lav_cnt[(hp, u, KT - 1)]
                W("vector", "tS", last_lav_prev)
                W("vector", "dS", cnt["dS"])
                E("vector", lambda e, hp=hp, u=u: e.tensor_copy(
                    OT[0:64, hp * 2048 + u * 512:hp * 2048 + (u + 1) * 512], psOA[0:64, :]), "vS")
                E("vector", lambda e, hp=hp, u=u: e.tensor_copy(
                    OT[64:128, hp * 2048 + u * 512:hp * 2048 + (u + 1) * 512], psOB[64:128, :]), "vS")
                E("vector", lambda e: e.tensor_copy(ltmp[0:1, :], psLA[0:1, :]), "vS")
                E("vector", lambda e: e.tensor_copy(ltmp[32:33, :], psLB[32:33, :]), "vS")
                if dbg == "B" and hp == 7 and u == 3:
                    E("vector", lambda e: e.tensor_copy(dbgb[:, 0:512], psLA[:, :]), "vS")
                    E("vector", lambda e: e.tensor_copy(dbgb[:, 512:1024], psOA[:, :]), "vS")
                u_evac[(hp, u)] = cnt["vS"]
                W("gpsimd", "vS", u_evac[(hp, u)])
                E("gpsimd", lambda e, hp=hp, u=u: e.dma_start(
                    out=l_all[2 * hp:2 * hp + 1, u * 512:(u + 1) * 512], in_=ltmp[0:1, :]), "dS")
                E("gpsimd", lambda e, hp=hp, u=u: e.dma_start(
                    out=l_all[2 * hp + 1:2 * hp + 2, u * 512:(u + 1) * 512],
                    in_=ltmp[32:33, :]), "dS")
            last_exp_prev_hp = exp_cnt[(hp, 3, KTU[3] - 1)]

        W("vector", "dS", cnt["dS"])
        E("vector", lambda e: e.reciprocal(l_all[:, :], l_all[:, :]), "vS")
        W("tensor", "scS", cnt["scS"])
        W("tensor", "vS", cnt["vS"])
        wo_ev = {}
        for u in range(4):
            for hp in range(8):
                if (u, hp) != (0, 0):
                    W("tensor", "vS", cnt["vS"])
                E("tensor", lambda e, hp=hp, u=u: e.matmul(
                    psR[:, :], selEs[:, hp * 128:(hp + 1) * 128],
                    l_all[:, u * 512:(u + 1) * 512], start=True, stop=True), "tS")
                W("vector", "tS", cnt["tS"])
                E("vector", lambda e, hp=hp, u=u: e.tensor_mul(
                    OT[:, hp * 2048 + u * 512:hp * 2048 + (u + 1) * 512],
                    OT[:, hp * 2048 + u * 512:hp * 2048 + (u + 1) * 512], psR[:, :]), "vS")
            norm_u = cnt["vS"]
            for tt in range(4):
                ti = u * 4 + tt
                for nbk in range(2):
                    bank = [psW0, psW1][nbk]
                    pk = u * 8 + tt * 2 + nbk - 2
                    if pk >= 0:
                        W("tensor", "vS", wo_ev[pk])
                    W("tensor", "vS", norm_u)
                    for hp in range(8):
                        E("tensor", lambda e, b=bank, hp=hp, u=u, tt=tt, nbk=nbk: e.matmul(
                            b[:, :],
                            OT[:, hp * 2048 + u * 512 + tt * 128:hp * 2048 + u * 512 + (tt + 1) * 128],
                            wo_sb[:, hp * D + nbk * 512:hp * D + nbk * 512 + 512],
                            start=(hp == 0), stop=(hp == 7)), "tS" if hp == 7 else None)
                    W("vector", "tS", cnt["tS"])
                    E("vector", lambda e, b=bank, ti=ti, nbk=nbk: e.scalar_tensor_tensor(
                        X[:, ti * D + nbk * 512:ti * D + nbk * 512 + 512], b[:, :], 1.0,
                        X[:, ti * D + nbk * 512:ti * D + nbk * 512 + 512],
                        ALU.mult, ALU.add), "vS")
                    wo_ev[u * 8 + tt * 2 + nbk] = cnt["vS"]
        tB3 = cnt["tS"]
        vB3 = cnt["vS"]


        if dbg == "B":
            _dump([("X", X[:, :]), ("l_all", l_all[:, :]), ("OT", OT[:, :]),
                   ("biasA", biasA[:, :]), ("biasB", biasB[:, :]),
                   ("KT", KT_s[:, :]), ("QT", QT[:, :]), ("V", V_s[:, :]),
                   ("PT", PT[:, :]), ("qk", qk_sb[:, :]), ("qk2", qk2_sb[:, :]),
                   ("dbgb", dbgb[:, :])])
    if dbg not in ("A", "B"):
        emit_ffn_mean(len(ffn_srcs))
        gMean = cnt["scS"]
        W("vector", "scS", gMean)
        W("vector", "tS", tB3)
        for wi in range(3):
            E("vector", lambda e, wi=wi: e.reduce_sum(
                wtot[:, wi:wi + 1], wst[:, wi * 64:(wi + 1) * 64], AX.X), "vS")
        W("tensor", "scS", cnt["scS"])
        W("tensor", "vS", cnt["vS"])
        for wi in range(3):
            if wi > 0:
                W("tensor", "vS", cnt["vS"])
            E("tensor", lambda e, wi=wi: e.matmul(
                psT1[:, :], onescol_f[:, :], wtot[:, wi:wi + 1], start=True, stop=True), "tS")
            W("vector", "tS", cnt["tS"])
            E("vector", lambda e, wi=wi: e.tensor_copy(wsc[:, wi:wi + 1], psT1[:, :]), "vS")
        W("vector", "vS", cnt["vS"])
        ECH("vector", lambda e: e.tensor_scalar(wsc[:, 3:6], wsc[:, 0:3], 1.0 / (D * INNER), 1e-5,
                                                ALU.mult, ALU.max))
        ECH("vector", lambda e: e.reciprocal(wsc[:, 0:3], wsc[:, 3:6]))
        ECH("vector", lambda e: e.tensor_scalar(wsc[:, 6:8], wsc[:, 3:5], 1.0 / 127, None, ALU.mult))
        ECH("vector", lambda e: e.tensor_scalar(wsc[:, 5:6], wsc[:, 5:6], 1.0 / 127, None, ALU.mult))
        W("tensor", "vS", cnt["vS"])
        for wi in range(3):
            if wi > 0:
                W("tensor", "vS", cnt["vS"])
            E("tensor", lambda e, wi=wi: e.matmul(psBC0[:, :], onesrow_f[:, :],
                                                  wsc[0:1, wi:wi + 1], start=True, stop=True), "tS")
            W("vector", "tS", cnt["tS"])
            E("vector", lambda e, wi=wi: e.tensor_copy(swv[:, wi:wi + 1], psBC0[:, :]), "vS")
        for wi, scol in enumerate([6, 7, 5]):
            if wi > 0:
                W("tensor", "vS", cnt["vS"])
            E("tensor", lambda e, scol=scol: e.matmul(psBC1[:, :], onesrow_f[:, :],
                                                      wsc[0:1, scol:scol + 1],
                                                      start=True, stop=True), "tS")
            W("vector", "tS", cnt["tS"])
            E("vector", lambda e, wi=wi: e.tensor_copy(mwv[:, wi:wi + 1], psBC1[:, :]), "vS")
        wsc_ready = cnt["vS"]

        qdst = {0: wgq, 1: wuqq, 2: wdqq}
        qdone = []
        W("gpsimd", "tS", tB3)
        for i, (wi, idx, src) in enumerate(ffn_srcs):
            slot = i % 8
            lq = "sync" if i % 2 == 0 else "scalar"
            dsem = "dQ" if i % 2 == 0 else "dQ2"
            if i < 8:
                W(lq, "scS", gMean)
                W(lq, "vS", wsc_ready)
                W(lq, "tS", tB3)
            else:
                W(lq, "vS", qdone[i - 8])
            E(lq, lambda e, s=src, sl=slot: e.dma_start(
                out=fring[:, sl * 512:(sl + 1) * 512], in_=s), dsem)
            eng = "vector"
            sem = "vS"
            W(eng, dsem, cnt[dsem])
            W(eng, "vS", wsc_ready)
            if wi < 2:
                dcol = (idx // 8) * INNER + (idx % 8) * 512
            else:
                dcol = (idx // 2) * D + (idx % 2) * 512
            E(eng, lambda e, sl=slot, wi=wi: e.tensor_scalar(
                fring[:, sl * 512:(sl + 1) * 512], fring[:, sl * 512:(sl + 1) * 512],
                swv[:, wi:wi + 1], 1.0, ALU.mult, ALU.min))
            E(eng, lambda e, sl=slot: e.tensor_scalar(
                fring[:, sl * 512:(sl + 1) * 512], fring[:, sl * 512:(sl + 1) * 512],
                -1.0, MAGIC, ALU.max, ALU.add))
            E(eng, lambda e, sl=slot, wi=wi, dcol=dcol: e.tensor_scalar(
                qdst[wi][:, dcol:dcol + 512], fring[:, sl * 512:(sl + 1) * 512],
                MAGIC, None, ALU.subtract), sem)
            qdone.append(cnt[sem])
        quant_v = cnt["vS"]
        quant_g = cnt["gS"]


        if dbg == "C0":
            _dump([("wgq", wgq[:, :]), ("wuqq", wuqq[:, :]), ("wdqq", wdqq[:, :]),
                   ("wsc", wsc[:, :]), ("swv", swv[:, :]), ("mwv", mwv[:, :]),
                   ("wst", wst[:, :]), ("wtot", wtot[:, :])])
    if dbg not in ("A", "B", "C0"):
        tile_done = {}
        tstate = {}
        qh2ds = {}

        def emit_stageA(ti):
            xap = X[:, ti * D:(ti + 1) * D]
            if ti == 0:
                W("vector", "gS", quant_g)
                W("vector", "tS", tB3)
                W("scalar", "tS", tB3)
            E("vector", lambda e, x=xap: e.reduce_sum(sums[:, 0:1], x, AX.X), "vS")
            W("scalar", "vS", cnt["vS"])
            if "gu_prev" in tstate:
                W("scalar", "vS", tstate["gu_prev"][7][1])
            E("scalar", lambda e, x=xap: e.activation(
                scr4k[:, :], x, AF.Square, accum_out=sumsq[:, 0:1]), "scS")
            W("vector", "scS", cnt["scS"])
            ECH("vector", lambda e: e.tensor_scalar(tmp4a[:, 0:1], sums[:, 0:1], 1.0 / D, None, ALU.mult))
            ECH("vector", lambda e: e.tensor_scalar(tmp4b[:, 0:1], tmp4a[:, 0:1], tmp4a[:, 0:1], -1.0, ALU.mult, ALU.mult))
            ECH("vector", lambda e: e.scalar_tensor_tensor(tmp4b[:, 0:1], sumsq[:, 0:1], 1.0 / D,
                                                         tmp4b[:, 0:1], ALU.mult, ALU.add))
            E("vector", lambda e: e.tensor_scalar(
                rstds[:, 0:1], tmp4b[:, 0:1], LN_EPS, None, ALU.add), "vS")
            W("scalar", "vS", cnt["vS"])
            E("scalar", lambda e: e.activation(rstds[:, 0:1], rstds[:, 0:1], AF.Sqrt), "scS")
            W("vector", "scS", cnt["scS"])
            ECH("vector", lambda e: e.reciprocal(rstds[:, 0:1], rstds[:, 0:1]))
            ECH("vector", lambda e: e.tensor_scalar(negmr[:, 0:1], tmp4a[:, 0:1], rstds[:, 0:1],
                                                  -1.0, ALU.mult, ALU.mult))
            E("vector", lambda e, x=xap: e.tensor_scalar(
                scr4k[:, :], x, rstds[:, 0:1], negmr[:, 0:1], ALU.mult, ALU.add))
            ECH("vector", lambda e: e.tensor_reduce(am_sb[:, 0:1], scr4k[:, :], AX.X, ALU.max,
                                                  apply_absolute_value=True))
            ECH("vector", lambda e: e.tensor_scalar(am_sb[:, 0:1], am_sb[:, 0:1], 1e-5, None, ALU.max))
            ECH("vector", lambda e: e.reciprocal(sa_sb[:, 0:1], am_sb[:, 0:1]))
            ECH("vector", lambda e: e.tensor_scalar(sa_sb[:, 0:1], sa_sb[:, 0:1], 127.0, None, ALU.mult))
            if "gu_prev" in tstate:
                W("vector", "scS", tstate["gu_prev"][7][0])
            E("vector", lambda e: e.tensor_mul(inv_sb[:, 0:1], am_sb[:, 0:1], mwv[:, 0:1]))
            E("vector", lambda e: e.tensor_mul(inv_sb[:, 1:2], am_sb[:, 0:1], mwv[:, 1:2]))
            E("vector", lambda e: e.tensor_scalar(scr4k[:, :], scr4k[:, :], sa_sb[:, 0:1], MAGIC,
                                                  ALU.mult, ALU.add))
            if "qh2_tr" in tstate:
                W("vector", "tS", tstate["qh2_tr"])
            E("vector", lambda e: e.tensor_scalar(qh2_sb[:, :], scr4k[:, :], MAGIC, None,
                                                  ALU.subtract), "vS")
            qh2ds[ti] = cnt["vS"]

        emit_stageA(0)
        for ti in range(NT_OWN):
            W("tensor", "vS", qh2ds[ti])
            if ti == 0:
                W("tensor", "vS", quant_v)
                W("tensor", "gS", quant_g)
            else:
                W("tensor", "vS", tstate["qh2T_evac"])
                W("tensor", "vS", tstate["ev7_prev"])
            for ck in range(8):
                E("tensor", lambda e, ck=ck: e.transpose(
                    psTrC[:, ck * 128:(ck + 1) * 128], qh2_sb[:, ck * 128:(ck + 1) * 128],
                    ident[:, :]), "tS" if ck == 7 else None)
            tstate["qh2_tr"] = cnt["tS"]
            W("vector", "tS", cnt["tS"])
            if "gate_last" in tstate:
                W("vector", "tS", tstate["gate_last"])
            E("vector", lambda e: e.tensor_copy(qh2T[:, :], psTrC[:, :]), "vS")
            qh2T_done = cnt["vS"]
            tstate["qh2T_evac"] = qh2T_done
            gu_done = []
            for ib in range(8):
                pG = [psG0, psG1][ib % 2]
                pU = [psU0, psU1][ib % 2]
                W("tensor", "vS", qh2T_done)
                if ib >= 2:
                    W("tensor", "vS", gu_done[ib - 2][1])
                    W("tensor", "scS", gu_done[ib - 2][0])
                elif "gu_prev" in tstate:
                    W("tensor", "vS", tstate["gu_prev"][6 + ib][1])
                    W("tensor", "scS", tstate["gu_prev"][6 + ib][0])
                for ck in range(8):
                    E("tensor", lambda e, p=pG, ck=ck, ib=ib: e.matmul(
                        p[:, :], qh2T[:, ck * 128:(ck + 1) * 128],
                        wgq[:, ck * INNER + ib * 512:ck * INNER + ib * 512 + 512],
                        start=(ck == 0), stop=(ck == 7)), "tS" if ck == 7 else None)
                for ck in range(8):
                    E("tensor", lambda e, p=pU, ck=ck, ib=ib: e.matmul(
                        p[:, :], qh2T[:, ck * 128:(ck + 1) * 128],
                        wuqq[:, ck * INNER + ib * 512:ck * INNER + ib * 512 + 512],
                        start=(ck == 0), stop=(ck == 7)), "tS" if ck == 7 else None)
                mm_done = cnt["tS"]
                W("scalar", "tS", mm_done)
                W("scalar", "vS", cnt["vS"])
                E("scalar", lambda e, p=pG: e.activation(
                    gsil[:, :], p[:, :], AF.Silu, scale=inv_sb[:, 0:1]), "scS")
                W("vector", "scS", cnt["scS"])
                if ib == 0 and "qgu_tr_last" in tstate:
                    W("vector", "tS", tstate["qgu_tr_last"])
                E("vector", lambda e, p=pU, ib=ib: e.scalar_tensor_tensor(
                    gu_sb[:, ib * 512:(ib + 1) * 512], p[:, :], inv_sb[:, 1:2], gsil[:, :],
                    ALU.mult, ALU.mult), "vS")
                gu_done.append((cnt["scS"], cnt["vS"]))
                if ib == 7:
                    tstate["gate_last"] = cnt["tS"]
                    tstate["gu_prev"] = list(gu_done)
            ECH("vector", lambda e: e.tensor_reduce(am_sb[:, 1:2], gu_sb[:, :], AX.X, ALU.max,
                                                  apply_absolute_value=True))
            ECH("vector", lambda e: e.tensor_scalar(am_sb[:, 1:2], am_sb[:, 1:2], 1e-5, None, ALU.max))
            ECH("vector", lambda e: e.reciprocal(sa_sb[:, 1:2], am_sb[:, 1:2]))
            ECH("vector", lambda e: e.tensor_scalar(sa_sb[:, 1:2], sa_sb[:, 1:2], 127.0, None, ALU.mult))
            E("vector", lambda e: e.tensor_mul(inv_sb[:, 2:3], am_sb[:, 1:2], mwv[:, 2:3]))
            ev_rounds = []
            tr_cnts = []
            for rr in range(8):
                E("vector", lambda e, rr=rr: e.tensor_scalar(
                    gu_sb[:, rr * 512:(rr + 1) * 512], gu_sb[:, rr * 512:(rr + 1) * 512],
                    sa_sb[:, 1:2], MAGIC, ALU.mult, ALU.add))
                E("vector", lambda e, rr=rr: e.tensor_scalar(
                    gu_sb[:, rr * 512:(rr + 1) * 512], gu_sb[:, rr * 512:(rr + 1) * 512],
                    MAGIC, None, ALU.subtract), "vS")
                q_cnt = cnt["vS"]
                pT = [psTrCf, psTrCf2][rr % 2]
                W("tensor", "vS", q_cnt)
                if rr >= 2:
                    W("tensor", "vS", ev_rounds[rr - 2])
                elif "ev7_prev" in tstate and ti > 0:
                    W("tensor", "vS", tstate["ev7_prev"])
                for ck in range(4):
                    E("tensor", lambda e, pT=pT, rr=rr, ck=ck: e.transpose(
                        pT[:, ck * 128:(ck + 1) * 128],
                        gu_sb[:, rr * 512 + ck * 128:rr * 512 + (ck + 1) * 128],
                        ident_f[:, :]), "tS" if ck == 3 else None)
                tr_cnts.append(cnt["tS"])
                if rr >= 1:
                    prr = rr - 1
                    W("vector", "tS", tr_cnts[prr])
                    if prr == 0 and "down_mm_last" in tstate:
                        W("vector", "tS", tstate["down_mm_last"])
                    E("vector", lambda e, prr=prr: e.tensor_copy(
                        qguT[:, prr * 512:(prr + 1) * 512],
                        [psTrCf, psTrCf2][prr % 2][:, :]), "vS")
                    ev_rounds.append(cnt["vS"])
            W("vector", "tS", tr_cnts[7])
            E("vector", lambda e: e.tensor_copy(qguT[:, 7 * 512:8 * 512], psTrCf2[:, :]), "vS")
            ev_rounds.append(cnt["vS"])
            tstate["qgu_tr_last"] = tr_cnts[7]
            tstate["ev7_prev"] = ev_rounds[7]
            if ti + 1 < NT_OWN:
                emit_stageA(ti + 1)
            for nbk in range(2):
                pD = [psD0, psD1][nbk]
                W("tensor", "vS", ev_rounds[7])
                if ti > 0 and nbk == 0:
                    W("tensor", "vS", tstate["dn_prev"])
                for ic in range(32):
                    E("tensor", lambda e, p=pD, ic=ic, nbk=nbk: e.matmul(
                        p[:, :], qguT[:, ic * 128:(ic + 1) * 128],
                        wdqq[:, ic * D + nbk * 512:ic * D + nbk * 512 + 512],
                        start=(ic == 0), stop=(ic == 31)), "tS" if ic == 31 else None)
                W("vector", "tS", cnt["tS"])
                E("vector", lambda e, p=pD, ti=ti, nbk=nbk: e.scalar_tensor_tensor(
                    X[:, ti * D + nbk * 512:ti * D + nbk * 512 + 512], p[:, :], inv_sb[:, 2:3],
                    X[:, ti * D + nbk * 512:ti * D + nbk * 512 + 512], ALU.mult, ALU.add), "vS")
            tile_done[ti] = cnt["vS"]
            tstate["dn_prev"] = cnt["vS"]
            tstate["down_mm_last"] = cnt["tS"]
            W("gpsimd", "vS", tile_done[ti])
            E("gpsimd", lambda e, ti=ti: e.dma_start(
                out=dout[ti * 128:(ti + 1) * 128, :], in_=X[:, ti * D:(ti + 1) * D]), "dS")
            if dbg == "C1" and ti == 0:
                _dump([("qh2", qh2_sb[:, :]), ("qh2T", qh2T[:, :]), ("qguT", qguT[:, :]),
                       ("am", am_sb[:, :]), ("sa", sa_sb[:, :]), ("inv", inv_sb[:, :]),
                       ("X0", X[:, 0:1024]), ("gsil", gsil[:, :]), ("qgu", gu_sb[:, :]),
                       ("rstds", rstds[:, :]), ("negmr", negmr[:, :])])
                break

        W("gpsimd", "dS", cnt["dS"])

    with nc.Block() as block:
        def mk(engname):
            def run(eng):
                for op in OPS[engname]:
                    if op[0] == "w":
                        eng.wait_ge(sems[op[1]], op[2])
                    else:
                        ins = op[1](eng)
                        if op[2] is not None:
                            ins.then_inc(sems[op[2][0]], op[2][1])
            return run
        block.sync(mk("sync"))
        block.tensor(mk("tensor"))
        block.vector(mk("vector"))
        block.scalar(mk("scalar"))
        block.gpsimd(mk("gpsimd"))
    return nc


def _masks_np(half):
    m = np.zeros((128, 2560), dtype=np.float32)
    ii = np.arange(128)[:, None]
    offs03 = [(0, 128), (128, 256), (384, 384), (768, 512)]
    for jp, (o, w) in enumerate(offs03):
        if half == 0:
            m[:, o:o + w] = 1.0
        else:
            jj = np.arange(w)[None, :]
            m[:, o:o + w] = (jj >= 128 * jp + ii).astype(np.float32)
    offs47 = [(1280, 512), (1792, 384), (2176, 256), (2432, 128)]
    for j4, (o, w) in enumerate(offs47):
        if half == 1:
            m[:, o:o + w] = 0.0
        else:
            jj = np.arange(w)[None, :]
            m[:, o:o + w] = (jj >= ii).astype(np.float32)
    return np.asarray(m, dtype=ml_dtypes.bfloat16)


def kernel(L0_attn_x, L0_attn_W_dq, L0_attn_W_uq, L0_attn_W_dkv, L0_attn_W_uk,
           L0_attn_W_uv, L0_attn_W_o, L0_ffn_w_gate, L0_ffn_w_up, L0_ffn_w_down):
    if "nc" not in _CACHE:
        _CACHE["nc"] = _build_nc()
    nc = _CACHE["nc"]

    x = np.ascontiguousarray(np.asarray(L0_attn_x, dtype=np.float32))
    shared = {
        "w_dq": np.ascontiguousarray(np.asarray(L0_attn_W_dq, dtype=np.float32)),
        "w_dkv": np.ascontiguousarray(np.asarray(L0_attn_W_dkv, dtype=np.float32)),
        "w_uq": np.ascontiguousarray(np.asarray(L0_attn_W_uq, dtype=np.float32)),
        "w_uk": np.ascontiguousarray(np.asarray(L0_attn_W_uk, dtype=np.float32)),
        "w_uv": np.ascontiguousarray(np.asarray(L0_attn_W_uv, dtype=np.float32)),
        "w_o": np.ascontiguousarray(np.asarray(L0_attn_W_o, dtype=np.float32)),
        "wg_t": np.ascontiguousarray(np.asarray(L0_ffn_w_gate, dtype=np.float32).T),
        "wu_t": np.ascontiguousarray(np.asarray(L0_ffn_w_up, dtype=np.float32).T),
        "wd_t": np.ascontiguousarray(np.asarray(L0_ffn_w_down, dtype=np.float32).T),
    }
    blocks_of = {0: [1, 3, 5, 7], 1: [0, 2, 4, 6]}
    in_maps = []
    for c in range(8):
        b, h = c >> 1, c & 1
        im = dict(shared)
        im["x_own"] = np.ascontiguousarray(
            np.concatenate([x[b, 512 * k:512 * (k + 1)] for k in blocks_of[h]], axis=0))
        im["x_all"] = np.ascontiguousarray(x[b])
        im["masks"] = _masks_np(h)
        in_maps.append(im)

    res = run_bass_kernel_spmd(nc, in_maps, core_ids=list(range(8)))
    out = np.empty((NBATCH, SEQ, D), dtype=np.float32)
    for c in range(8):
        b, h = c >> 1, c & 1
        r = np.asarray(res.results[c]["out"])
        for j, k in enumerate(blocks_of[h]):
            out[b, 512 * k:512 * (k + 1)] = r[512 * j:512 * (j + 1)]
    return out


# version 19
# speedup vs baseline: 2.0693x; 1.0806x over previous
import numpy as np
import ml_dtypes
import concourse.bass as bass
import concourse.mybir as mybir
from concourse.bass_utils import run_bass_kernel_spmd

F32 = mybir.dt.float32
BF16 = mybir.dt.bfloat16
FP8 = mybir.dt.float8e4
AF = mybir.ActivationFunctionType
ALU = mybir.AluOpType
AX = mybir.AxisListType

D = 1024
SEQ = 4096
NBATCH = 4
DC = 256
INNER = 4096
NT_OWN = 16
NT_ALL = 32
KTU = [8, 16, 24, 32]
MAGIC = 12582912.0
LN_EPS = 1e-5
C_SAFETY = -0.1275

_CACHE = {}


def _build_nc(dbg=None):
    nc = bass.Bass("TRN2", target_bir_lowering=False)

    dx_own = nc.dram_tensor("x_own", [2048, D], F32, kind="ExternalInput").ap()
    dx_all = nc.dram_tensor("x_all", [SEQ, D], F32, kind="ExternalInput").ap()
    dw_dq = nc.dram_tensor("w_dq", [D, DC], F32, kind="ExternalInput").ap()
    dw_dkv = nc.dram_tensor("w_dkv", [D, DC], F32, kind="ExternalInput").ap()
    dw_uq = nc.dram_tensor("w_uq", [DC, D], F32, kind="ExternalInput").ap()
    dw_uk = nc.dram_tensor("w_uk", [DC, D], F32, kind="ExternalInput").ap()
    dw_uv = nc.dram_tensor("w_uv", [DC, D], F32, kind="ExternalInput").ap()
    dw_o = nc.dram_tensor("w_o", [D, D], F32, kind="ExternalInput").ap()
    dwg = nc.dram_tensor("wg_t", [D, INNER], F32, kind="ExternalInput").ap()
    dwu = nc.dram_tensor("wu_t", [D, INNER], F32, kind="ExternalInput").ap()
    dwd = nc.dram_tensor("wd_t", [INNER, D], F32, kind="ExternalInput").ap()
    dmasks = nc.dram_tensor("masks", [128, 2560], BF16, kind="ExternalInput").ap()
    dout = nc.dram_tensor("out", [2048, D], F32, kind="ExternalOutput").ap()

    ident_c = nc.inline_tensor(np.eye(128, dtype=ml_dtypes.bfloat16), "ident_c")
    identf_c = nc.inline_tensor(np.eye(128, dtype=np.float32), "identf_c")
    onescol_c = nc.inline_tensor(np.ones((128, 2), dtype=ml_dtypes.bfloat16), "onescol_c")
    onescol_f_c = nc.inline_tensor(np.ones((128, 1), dtype=np.float32), "onescol_f_c")
    onesrow_f_c = nc.inline_tensor(np.ones((1, 128), dtype=np.float32), "onesrow_f_c")
    bo = np.zeros((128, 2), dtype=np.float32)
    bo[0:64, 0] = 1.0
    bo[64:128, 1] = 1.0
    blockones_c = nc.inline_tensor(bo, "blockones_c")
    sel = np.zeros((2, 256), dtype=np.float32)
    sel[0, 0:128] = 1.0
    sel[1, 128:256] = 1.0
    selAB_c = nc.inline_tensor(sel, "selAB_c")
    selE = np.zeros((16, 8 * 128), dtype=np.float32)
    for hp in range(8):
        selE[2 * hp, hp * 128:hp * 128 + 64] = 1.0
        selE[2 * hp + 1, hp * 128 + 64:hp * 128 + 128] = 1.0
    selE_c = nc.inline_tensor(selE, "selE_c")

    base0 = (nc.sbuf_base + 31) // 32 * 32
    top = nc.sbuf_top // 32 * 32
    cur = [base0]

    sboff = {}

    def sb(name, shape, dtype):
        o = cur[0]
        t = nc.alloc_sbuf_tensor_at(name, shape, dtype, offset=o)
        sboff[name] = o
        nbytes = shape[1] * mybir.dt.size(dtype)
        cur[0] = o + ((nbytes + 31) // 32) * 32
        assert cur[0] <= top, f"sbuf overflow at {name}"
        return t

    X = sb("X", [128, NT_OWN * D], F32)
    ident = sb("ident", [128, 128], BF16)
    ident_f = sb("ident_f", [128, 128], F32)
    onescol = sb("onescol", [128, 2], BF16)
    onescol_f = sb("onescol_f", [128, 1], F32)
    onesrow_f = sb("onesrow_f", [1, 128], F32)
    blockones = sb("blockones", [128, 2], F32)
    selAB = sb("selAB", [2, 256], F32)
    selEs = sb("selEs", [16, 8 * 128], F32)
    masks = sb("masks", [128, 2560], BF16)
    sums = sb("sums", [128, 4], F32)
    sumsq = sb("sumsq", [128, 4], F32)
    rstds = sb("rstds", [128, 4], F32)
    negmr = sb("negmr", [128, 4], F32)
    tmp4a = sb("tmp4a", [128, 4], F32)
    tmp4b = sb("tmp4b", [128, 4], F32)
    biasA = sb("biasA", [128, 1], F32)
    biasB = sb("biasB", [128, 1], F32)
    qk_sb = sb("qk_sb", [2, 2], F32)
    qk2_sb = sb("qk2_sb", [2, 1], F32)
    ltmp = sb("ltmp", [33, 512], F32)
    wst = sb("wst", [128, 192], F32)
    wtot = sb("wtot", [128, 3], F32)
    wsc = sb("wsc", [1, 8], F32)
    swv = sb("swv", [128, 3], F32)
    mwv = sb("mwv", [128, 3], F32)
    am_sb = sb("am_sb", [128, 2], F32)
    sa_sb = sb("sa_sb", [128, 2], F32)
    inv_sb = sb("inv_sb", [128, 3], F32)
    scr4k = sb("scr4k", [128, D], F32)

    R = cur[0]
    avail = top - R
    assert avail >= 126976, f"era region too small: {avail}"

    def at(name, shape, dtype, off, size):
        nbytes = shape[1] * mybir.dt.size(dtype)
        assert nbytes <= size, f"{name}: {nbytes} > {size}"
        assert R + off + nbytes <= top, f"{name} beyond top"
        return nc.alloc_sbuf_tensor_at(name, shape, dtype, offset=R + off)

    xstage = at("xstage", [128, 2 * D], F32, 0, 8192)
    h1tok = at("h1tok", [128, 2 * D], BF16, 8192, 4096)
    h1T = at("h1T", [128, 8 * 512], BF16, 12288, 8192)
    cqT = at("cqT", [128, 2 * 2048], BF16, 20480, 8192)
    ckvT = at("ckvT", [128, 2 * SEQ], BF16, 28672, 16384)
    wdq_sb = at("wdq_sb", [128, 8 * DC], BF16, 45056, 4096)
    wdkv_sb = at("wdkv_sb", [128, 8 * DC], BF16, 49152, 4096)
    fstage = at("fstage", [128, 2 * 512], F32, 53248, 4096)
    wuq_sb = at("wuq_sb", [128, 2 * D], BF16, 57344, 4096)
    wuk_sb = at("wuk_sb", [128, 2 * D], BF16, 61440, 4096)
    wuv_sb = at("wuv_sb", [128, 2 * D], BF16, 65536, 4096)
    wo_sb = at("wo_sb", [128, 8 * D], BF16, 69632, 16384)
    OT = at("OT", [128, 8 * 2048], BF16, 86016, 32768)
    l_all = at("l_all", [16, 2048], F32, 118784, 8192)
    wf_dq = at("wf_dq", [128, 8 * DC], F32, 86016, 8192)
    wf_dkv = at("wf_dkv", [128, 8 * DC], F32, 94208, 8192)
    wf_uq = at("wf_uq", [128, 2 * D], F32, 102400, 8192)
    wf_uk = at("wf_uk", [128, 2 * D], F32, 110592, 8192)
    wf_uv = at("wf_uv", [128, 2 * D], F32, 118784, 8192)
    QT = at("QT", [128, 2048], BF16, 0, 4096)
    PT = at("PT", [128, 8 * 512], BF16, 4096, 8192)
    sqscr = at("sqscr", [128, 2048], BF16, 8192, 4096)
    KT_s = at("KT_s", [128, SEQ], BF16, 12288, 8192)
    V_s = at("V_s", [128, SEQ], BF16, 45056, 8192)
    wgq = at("wgq", [128, 8 * INNER], FP8, 20480, 32768)
    wuqq = at("wuqq", [128, 8 * INNER], FP8, 57344, 32768)
    wdqq = at("wdqq", [128, 32 * D], FP8, 90112, 32768)
    gu_sb = at("gu_sb", [128, INNER], F32, 0, 16384)
    fring = at("fring", [128, 8 * 512], F32, 0, 16384)
    qh2T = at("qh2T", [128, 8 * 128], BF16, 122880, 2048)
    qh2_sb = at("qh2_sb", [128, D], BF16, 124928, 2048)
    qguT = at("qguT", [128, 32 * 128], BF16, 0, 8192)
    gsil = nc.alloc_sbuf_tensor_at("gsil", [128, 512], F32, offset=sboff["scr4k"])
    dbgb = at("dbgb", [128, 1024], F32, 20480, 32768) if dbg else None
    assert R + 126976 <= top

    def ps(name, shape, dtype, bank):
        nc.psum_base = bank
        return nc.alloc_psum_tensor(name, shape, dtype)

    psTrA = ps("psTrA", [128, 1024], BF16, 0)
    psP0 = ps("psP0", [128, 512], F32, 1)
    psP1 = ps("psP1", [128, 512], F32, 2)
    psSA = ps("psSA", [128, 1024], F32, 0)
    psSB = ps("psSB", [128, 1024], F32, 2)
    psSA0 = ps("psSA0", [128, 512], F32, 0)
    psSA1 = ps("psSA1", [128, 512], F32, 1)
    psSB0 = ps("psSB0", [128, 512], F32, 2)
    psSB1 = ps("psSB1", [128, 512], F32, 3)
    psLA = ps("psLA", [128, 512], F32, 4)
    psOA = ps("psOA", [128, 512], F32, 5)
    psLB = ps("psLB", [128, 512], F32, 6)
    psOB = ps("psOB", [128, 512], F32, 7)
    psN0 = ps("psN0", [2, 1], F32, 4)
    psN1 = ps("psN1", [2, 1], F32, 5)
    psBC0 = ps("psBC0", [128, 1], F32, 6)
    psBC1 = ps("psBC1", [128, 1], F32, 7)
    psW0 = ps("psW0", [128, 512], F32, 0)
    psW1 = ps("psW1", [128, 512], F32, 1)
    psW2 = ps("psW2", [128, 512], F32, 4)
    psW3 = ps("psW3", [128, 512], F32, 5)
    psR = ps("psR", [128, 512], F32, 2)
    psT1 = ps("psT1", [1, 1], F32, 3)
    psTrC = ps("psTrC", [128, 1024], BF16, 7)
    psTrCf = ps("psTrCf", [128, 512], F32, 0)
    psTrCf2 = ps("psTrCf2", [128, 512], F32, 7)
    psG0 = ps("psG0", [128, 512], F32, 1)
    psG1 = ps("psG1", [128, 512], F32, 2)
    psU0 = ps("psU0", [128, 512], F32, 3)
    psU1 = ps("psU1", [128, 512], F32, 4)
    psD0 = ps("psD0", [128, 512], F32, 5)
    psD1 = ps("psD1", [128, 512], F32, 6)
    nc.psum_base = 8

    DMA_SEMS = ("dL", "dS", "dF", "dQ", "dQ2")
    sem_names = ["dL", "dS", "dF", "dQ", "dQ2", "tS", "vS", "scS", "gS"]
    sems = {n: nc.alloc_semaphore(n) for n in sem_names}
    cnt = {n: 0 for n in sem_names}
    OPS = {"sync": [], "tensor": [], "vector": [], "scalar": [], "gpsimd": []}

    def W(eng, sem, val):
        if val > 0:
            OPS[eng].append(("w", sem, val))

    def E(eng, fn, inc=None):
        n = 16 if inc in DMA_SEMS else 1
        OPS[eng].append(("e", fn, (inc, n) if inc else None))
        if inc:
            cnt[inc] += n

    _SELF = {"vector": "vS", "scalar": "scS", "gpsimd": "gS", "tensor": "tS"}

    def ECH(eng, fn):
        s = _SELF[eng]
        E(eng, fn, s)
        W(eng, s, cnt[s])

    for dst, src in [(ident, ident_c), (ident_f, identf_c), (onescol, onescol_c), (onescol_f, onescol_f_c),
                     (onesrow_f, onesrow_f_c), (blockones, blockones_c),
                     (selAB, selAB_c), (selEs, selE_c)]:
        E("sync", lambda e, d=dst, s=src: e.dma_start(out=d[:, :], in_=s.ap()), "dL")
    E("sync", lambda e: e.dma_start(out=masks[:, :], in_=dmasks), "dL")
    for t in range(NT_OWN):
        E("sync", lambda e, t=t: e.dma_start(out=X[:, t * D:(t + 1) * D],
                                             in_=dx_own[t * 128:(t + 1) * 128, :]), "dL")
    for ck in range(8):
        E("sync", lambda e, ck=ck: e.dma_start(out=wf_dq[:, ck * DC:(ck + 1) * DC],
                                               in_=dw_dq[ck * 128:(ck + 1) * 128, :]), "dL")
        E("sync", lambda e, ck=ck: e.dma_start(out=wf_dkv[:, ck * DC:(ck + 1) * DC],
                                               in_=dw_dkv[ck * 128:(ck + 1) * 128, :]), "dL")
    for cc in range(2):
        E("sync", lambda e, cc=cc: e.dma_start(out=wf_uq[:, cc * D:(cc + 1) * D],
                                               in_=dw_uq[cc * 128:(cc + 1) * 128, :]), "dL")
        E("sync", lambda e, cc=cc: e.dma_start(out=wf_uk[:, cc * D:(cc + 1) * D],
                                               in_=dw_uk[cc * 128:(cc + 1) * 128, :]), "dL")
        E("sync", lambda e, cc=cc: e.dma_start(out=wf_uv[:, cc * D:(cc + 1) * D],
                                               in_=dw_uv[cc * 128:(cc + 1) * 128, :]), "dL")
    dl_stage5 = cnt["dL"]
    W("vector", "dL", dl_stage5)
    E("vector", lambda e: e.tensor_copy(wdq_sb[:, :], wf_dq[:, :]), "vS")
    E("vector", lambda e: e.tensor_copy(wdkv_sb[:, :], wf_dkv[:, :]), "vS")
    E("vector", lambda e: e.tensor_copy(wuq_sb[:, :], wf_uq[:, :]), "vS")
    E("vector", lambda e: e.tensor_copy(wuk_sb[:, :], wf_uk[:, :]), "vS")
    E("vector", lambda e: e.tensor_copy(wuv_sb[:, :], wf_uv[:, :]), "vS")
    wo_cast = []
    for pc in range(16):
        ck, hh = pc // 2, (pc % 2) * 512
        if pc >= 2:
            W("sync", "vS", wo_cast[pc - 2])
        E("sync", lambda e, ck=ck, hh=hh, sl=pc % 2: e.dma_start(
            out=fstage[:, sl * 512:(sl + 1) * 512],
            in_=dw_o[ck * 128:(ck + 1) * 128, hh:hh + 512]), "dL")
        W("vector", "dL", cnt["dL"])
        E("vector", lambda e, ck=ck, hh=hh, sl=pc % 2: e.tensor_copy(
            wo_sb[:, ck * D + hh:ck * D + hh + 512], fstage[:, sl * 512:(sl + 1) * 512]), "vS")
        wo_cast.append(cnt["vS"])
    wcast_all = cnt["vS"]
    dl_head = cnt["dL"]

    h1evac = {}
    projevac = []
    xload = {}
    h1done = {}
    sqdone = {}
    tr_done = {}
    chain_done = {}
    npr = 0
    for g in range(12):
        own = g >= 8
        for jp in range(2):
            j0, j1 = 2 * jp, 2 * jp + 1
            tis, xaps = [], []
            for j in (j0, j1):
                ti = g * 4 + j
                if not own:
                    prev = ti - 2
                    if prev >= 0 and prev in h1done:
                        W("sync", "vS", h1done[prev])
                        W("sync", "scS", sqdone[prev])
                    E("sync", lambda e, ti=ti, sl=ti % 2: e.dma_start(
                        out=xstage[:, sl * D:(sl + 1) * D],
                        in_=dx_all[ti * 128:(ti + 1) * 128, :]), "dL")
                    xload[ti] = cnt["dL"]
                    xap = xstage[:, (ti % 2) * D:((ti % 2) + 1) * D]
                else:
                    xload[ti] = dl_head
                    xap = X[:, (ti - 32) * D:(ti - 31) * D]
                tis.append(ti)
                xaps.append(xap)
            W("vector", "dL", xload[tis[0]])
            E("vector", lambda e, x=xaps[0], j=j0: e.reduce_sum(sums[:, j:j + 1], x, AX.X), "vS")
            W("vector", "dL", xload[tis[1]])
            ECH("vector", lambda e, x=xaps[1], j=j1: e.reduce_sum(sums[:, j:j + 1], x, AX.X))
            W("scalar", "dL", xload[tis[0]])
            if tis[0] >= 1:
                W("scalar", "vS", h1done.get(tis[0] - 1, 0))
            E("scalar", lambda e, x=xaps[0], j=j0: e.activation(
                scr4k[:, :], x, AF.Square, accum_out=sumsq[:, j:j + 1]), "scS")
            W("scalar", "dL", xload[tis[1]])
            E("scalar", lambda e, x=xaps[1], j=j1: e.activation(
                scr4k[:, :], x, AF.Square, accum_out=sumsq[:, j:j + 1]), "scS")
            sqdone[tis[0]] = sqdone[tis[1]] = cnt["scS"]
            W("vector", "scS", sqdone[tis[1]])
            ECH("vector", lambda e, j=j0: e.tensor_scalar(
                tmp4a[:, j:j + 2], sums[:, j:j + 2], 1.0 / D, None, ALU.mult))
            ECH("vector", lambda e, j=j0: e.tensor_mul(
                tmp4b[:, j:j + 2], tmp4a[:, j:j + 2], tmp4a[:, j:j + 2]))
            ECH("vector", lambda e, j=j0: e.tensor_scalar(
                tmp4b[:, j:j + 2], tmp4b[:, j:j + 2], -1.0, None, ALU.mult))
            ECH("vector", lambda e, j=j0: e.scalar_tensor_tensor(
                tmp4b[:, j:j + 2], sumsq[:, j:j + 2], 1.0 / D,
                tmp4b[:, j:j + 2], ALU.mult, ALU.add))
            E("vector", lambda e, j=j0: e.tensor_scalar(
                rstds[:, j:j + 2], tmp4b[:, j:j + 2], LN_EPS, None, ALU.add), "vS")
            W("scalar", "vS", cnt["vS"])
            E("scalar", lambda e, j=j0: e.activation(
                rstds[:, j:j + 2], rstds[:, j:j + 2], AF.Sqrt), "scS")
            W("vector", "scS", cnt["scS"])
            ECH("vector", lambda e, j=j0: e.reciprocal(rstds[:, j:j + 2], rstds[:, j:j + 2]))
            ECH("vector", lambda e, j=j0: e.scalar_tensor_tensor(
                negmr[:, j:j + 2], tmp4a[:, j:j + 2], -1.0, rstds[:, j:j + 2],
                ALU.mult, ALU.mult))
            for j, ti, xap in ((j0, tis[0], xaps[0]), (j1, tis[1], xaps[1])):
                sl = j % 2
                slot_prev = (g, j - 2) if j >= 2 else (g - 1, j + 2)
                if slot_prev in tr_done:
                    W("vector", "tS", tr_done[slot_prev])
                E("vector", lambda e, x=xap, j=j, sl=sl: e.tensor_scalar(
                    h1tok[:, sl * D:(sl + 1) * D], x, rstds[:, j:j + 1], negmr[:, j:j + 1],
                    ALU.mult, ALU.add), "vS")
                h1done[ti] = cnt["vS"]
                W("tensor", "vS", h1done[ti])
                tr_prev = (g, j - 1) if j >= 1 else (g - 1, 3)
                if tr_prev in h1evac:
                    W("tensor", "vS", h1evac[tr_prev])
                for ck in range(8):
                    E("tensor", lambda e, sl=sl, ck=ck: e.transpose(
                        psTrA[:, ck * 128:(ck + 1) * 128],
                        h1tok[:, sl * D + ck * 128:sl * D + (ck + 1) * 128], ident[:, :]),
                        "tS" if ck == 7 else None)
                tr_done[(g, j)] = cnt["tS"]
                W("vector", "tS", tr_done[(g, j)])
                E("vector", lambda e, j=j: e.tensor_copy(
                    h1T[:, :].rearrange("p (c t) -> p c t", c=8)[:, :, j * 128:(j + 1) * 128],
                    psTrA[:, :].rearrange("p (c t) -> p c t", c=8)), "vS")
                h1evac[(g, j)] = cnt["vS"]
        wtile = wdq_sb if own else wdkv_sb
        dstT = cqT if own else ckvT
        gcol = (g - 8) * 512 if own else g * 512
        span = 2048 if own else SEQ
        for m in range(2):
            bank = [psP0, psP1][npr % 2]
            if npr >= 2:
                W("tensor", "vS", projevac[npr - 2])
            W("tensor", "vS", h1evac[(g, 3)])
            W("tensor", "vS", wcast_all)
            for ck in range(8):
                E("tensor", lambda e, b=bank, wt=wtile, ck=ck, m=m: e.matmul(
                    b[:, :], wt[:, ck * DC + m * 128:ck * DC + (m + 1) * 128],
                    h1T[:, ck * 512:(ck + 1) * 512],
                    start=(ck == 0), stop=(ck == 7)), "tS" if ck == 7 else None)
            W("vector", "tS", cnt["tS"])
            E("vector", lambda e, b=bank, d=dstT, m=m, gc=gcol, sp=span: e.tensor_copy(
                d[:, m * sp + gc:m * sp + gc + 512], b[:, :]), "vS")
            projevac.append(cnt["vS"])
            npr += 1
    phaseA_tS = cnt["tS"]
    phaseA_vS = cnt["vS"]

    def _dump(pairs):
        for nm, ap in pairs:
            dt_ = ap.dtype
            dten = nc.dram_tensor("dump_" + nm, list(ap.shape), dt_,
                                  kind="ExternalOutput").ap()
            W("gpsimd", "vS", cnt["vS"])
            W("gpsimd", "scS", cnt["scS"])
            W("gpsimd", "tS", cnt["tS"])
            E("gpsimd", lambda e, d=dten, s=ap: e.dma_start(out=d, in_=s), "dS")
        W("gpsimd", "dS", cnt["dS"])

    if dbg == "A":
        _dump([("ckvT", ckvT[:, :]), ("cqT", cqT[:, :]), ("wdq", wdq_sb[:, :]),
               ("wuk", wuk_sb[:, :]), ("wo", wo_sb[:, :]), ("X", X[:, 0:2048]),
               ("h1T", h1T[:, :]), ("h1tok", h1tok[:, :]), ("sums", sums[:, :]),
               ("sumsq", sumsq[:, :]), ("rstds", rstds[:, :]), ("negmr", negmr[:, :]),
               ("tmp4a", tmp4a[:, :]), ("tmp4b", tmp4b[:, :])])

    ffn_srcs = []
    for wi, dw in enumerate([dwg, dwu, dwd]):
        nrow = D if wi < 2 else INNER
        ncolw = INNER if wi < 2 else D
        idx = 0
        for ck in range(nrow // 128):
            for hh in range(0, ncolw, 512):
                ffn_srcs.append((wi, idx, dw[ck * 128:(ck + 1) * 128, hh:hh + 512]))
                idx += 1
    assert len(ffn_srcs) == 192
    ffn_ptr = [0]
    ffn_gdone = []

    def emit_ffn_mean(nsteps):
        for _ in range(nsteps):
            i = ffn_ptr[0]
            if i >= len(ffn_srcs):
                return
            wi, idx, src = ffn_srcs[i]
            slot = i % 2
            if i < 2:
                W("gpsimd", "vS", wcast_all)
            if i >= 2:
                W("gpsimd", "scS", ffn_gdone[i - 2])
            E("gpsimd", lambda e, s=src, sl=slot: e.dma_start(
                out=fstage[:, sl * 512:(sl + 1) * 512], in_=s), "dF")
            W("scalar", "dF", cnt["dF"])
            col = wi * 64 + idx
            E("scalar", lambda e, sl=slot, col=col: e.activation(
                fstage[:, sl * 512:(sl + 1) * 512], fstage[:, sl * 512:(sl + 1) * 512], AF.Abs,
                accum_out=wst[:, col:col + 1]), "scS")
            ffn_gdone.append(cnt["scS"])
            ffn_ptr[0] += 1

    if dbg != "A":
        u_evac = {}
        exp_cnt = {}
        expA_cnt = {}
        expB_cnt = {}
        mask_cnt = {}
        lav_cnt = {}
        s_cnt = {}
        last_exp_prev_hp = 0
        last_lav_prev = 0

        for hp in range(8):
            W("tensor", "scS", last_exp_prev_hp)
            if hp == 0:
                W("tensor", "vS", phaseA_vS)
            else:
                W("tensor", "vS", u_evac[(hp - 1, 3)])
            bev = []
            for nb in range(8):
                bank = [psSA0, psSA1][nb % 2]
                if nb >= 2:
                    W("tensor", "vS", bev[nb - 2])
                for cc in range(2):
                    E("tensor", lambda e, b=bank, cc=cc, hp=hp, nb=nb: e.matmul(
                        b[:, :], wuk_sb[:, cc * D + hp * 128:cc * D + (hp + 1) * 128],
                        ckvT[:, cc * SEQ + nb * 512:cc * SEQ + nb * 512 + 512],
                        start=(cc == 0), stop=(cc == 1)), "tS" if cc == 1 else None)
                W("vector", "tS", cnt["tS"])
                E("vector", lambda e, b=bank, nb=nb: e.tensor_copy(
                    KT_s[:, nb * 512:(nb + 1) * 512], b[:, :]), "vS")
                bev.append(cnt["vS"])
            for nb in range(4):
                bank = [psSB0, psSB1][nb % 2]
                if nb >= 2:
                    W("tensor", "vS", bev[8 + nb - 2])
                for cc in range(2):
                    E("tensor", lambda e, b=bank, cc=cc, hp=hp, nb=nb: e.matmul(
                        b[:, :], wuq_sb[:, cc * D + hp * 128:cc * D + (hp + 1) * 128],
                        cqT[:, cc * 2048 + nb * 512:cc * 2048 + nb * 512 + 512],
                        start=(cc == 0), stop=(cc == 1)), "tS" if cc == 1 else None)
                W("vector", "tS", cnt["tS"])
                E("vector", lambda e, b=bank, nb=nb: e.tensor_copy(
                    QT[:, nb * 512:(nb + 1) * 512], b[:, :]), "vS")
                bev.append(cnt["vS"])
            for kt in range(32):
                bank = [psLA, psOA][kt % 2]
                if kt >= 2:
                    W("tensor", "vS", bev[12 + kt - 2])
                for cc in range(2):
                    E("tensor", lambda e, b=bank, cc=cc, hp=hp, kt=kt: e.matmul(
                        b[:, 0:128], ckvT[:, cc * SEQ + kt * 128:cc * SEQ + (kt + 1) * 128],
                        wuv_sb[:, cc * D + hp * 128:cc * D + (hp + 1) * 128],
                        start=(cc == 0), stop=(cc == 1)), "tS" if cc == 1 else None)
                W("vector", "tS", cnt["tS"])
                E("vector", lambda e, b=bank, kt=kt: e.tensor_copy(
                    V_s[:, kt * 128:(kt + 1) * 128], b[:, 0:128]), "vS")
                bev.append(cnt["vS"])
            kv_built = cnt["vS"]

            emit_ffn_mean(24)

            W("scalar", "vS", kv_built)
            W("scalar", "tS", cnt["tS"])
            E("scalar", lambda e: e.activation(sqscr[:, :], QT[:, :], AF.Square), "scS")
            W("vector", "scS", cnt["scS"])
            E("vector", lambda e: e.reduce_max(tmp4a[:, 0:1], sqscr[:, :], AX.X), "vS")
            E("scalar", lambda e: e.activation(sqscr[:, :], KT_s[:, 0:2048], AF.Square), "scS")
            W("vector", "scS", cnt["scS"])
            E("vector", lambda e: e.reduce_max(tmp4a[:, 1:2], sqscr[:, :], AX.X), "vS")
            W("scalar", "vS", cnt["vS"])
            E("scalar", lambda e: e.activation(sqscr[:, :], KT_s[:, 2048:4096], AF.Square), "scS")
            W("vector", "scS", cnt["scS"])
            ECH("vector", lambda e: e.reduce_max(tmp4b[:, 0:1], sqscr[:, :], AX.X))
            E("vector", lambda e: e.tensor_max(tmp4a[:, 1:2], tmp4a[:, 1:2], tmp4b[:, 0:1]), "vS")
            W("tensor", "vS", cnt["vS"])
            E("tensor", lambda e: e.matmul(psN0[:, :], blockones[:, :], tmp4a[:, 0:1],
                                           start=True, stop=True), "tS")
            E("tensor", lambda e: e.matmul(psN1[:, :], blockones[:, :], tmp4a[:, 1:2],
                                           start=True, stop=True), "tS")
            W("vector", "tS", cnt["tS"])
            E("vector", lambda e: e.tensor_copy(qk_sb[:, 0:1], psN0[:, :]), "vS")
            E("vector", lambda e: e.tensor_copy(qk_sb[:, 1:2], psN1[:, :]), "vS")
            W("vector", "vS", cnt["vS"])
            W("scalar", "vS", cnt["vS"])
            E("scalar", lambda e: e.activation(qk_sb[:, :], qk_sb[:, :], AF.Sqrt), "scS")
            W("vector", "scS", cnt["scS"])
            ECH("vector", lambda e: e.tensor_scalar(qk2_sb[:, :], qk_sb[:, 0:1], qk_sb[:, 1:2],
                                                    C_SAFETY, ALU.mult, ALU.mult))
            W("tensor", "vS", cnt["vS"])
            E("tensor", lambda e: e.matmul(psBC0[:, :], selAB[:, 0:128], qk2_sb[:, :],
                                           start=True, stop=True), "tS")
            E("tensor", lambda e: e.matmul(psBC1[:, :], selAB[:, 128:256], qk2_sb[:, :],
                                           start=True, stop=True), "tS")
            W("vector", "tS", cnt["tS"])
            E("vector", lambda e: e.tensor_copy(biasA[:, :], psBC0[:, :]), "vS")
            E("vector", lambda e: e.tensor_copy(biasB[:, :], psBC1[:, :]), "vS")
            bias_ready = cnt["vS"]

            for u in range(4):
                KT = KTU[u]
                if u == 0:
                    W("tensor", "vS", bias_ready)
                    evac_wait = None
                else:
                    evac_wait = u_evac[(hp, u - 1)]

                def slotbase(kt):
                    return ((kt // 2) % 2) * 2048

                def emit_Spair(p, hp=hp, u=u, KT=KT, bias_ready=bias_ready,
                               last_lav_prev=last_lav_prev):
                    if p >= 1:
                        W("tensor", "scS", expA_cnt[(hp, u, p - 1)])
                    elif u > 0:
                        W("tensor", "scS", exp_cnt[(hp, u - 1, KTU[u - 1] - 1)])
                    else:
                        W("tensor", "vS", bias_ready)
                    for kt in (2 * p, 2 * p + 1):
                        o_ = 0 if kt < KT - 4 else 128 * (kt - (KT - 4))
                        sl = (kt % 2) * 512
                        E("tensor", lambda e, o_=o_, kt=kt, sl=sl: e.matmul(
                            psSA[:, sl + o_:sl + 512], KT_s[0:64, kt * 128:(kt + 1) * 128],
                            QT[0:64, u * 512 + o_:(u + 1) * 512],
                            start=True, stop=True, tile_position=(0, 0)), "tS")
                    s_cntA = cnt["tS"]
                    if p >= 1:
                        W("tensor", "scS", expB_cnt[(hp, u, p - 1)])
                    for kt in (2 * p, 2 * p + 1):
                        o_ = 0 if kt < KT - 4 else 128 * (kt - (KT - 4))
                        sl = (kt % 2) * 512
                        E("tensor", lambda e, o_=o_, kt=kt, sl=sl: e.matmul(
                            psSB[:, sl + o_:sl + 512], KT_s[64:128, kt * 128:(kt + 1) * 128],
                            QT[64:128, u * 512 + o_:(u + 1) * 512],
                            start=True, stop=True, tile_position=(64, 0)), "tS")
                    s_cnt[(hp, u, p)] = cnt["tS"]
                    s_cntB = cnt["tS"]
                    base = slotbase(2 * p)
                    W("scalar", "tS", s_cntA)
                    if p >= 2:
                        W("scalar", "tS", lav_cnt[(hp, u, 2 * p - 3)])
                    else:
                        W("scalar", "tS", last_lav_prev)
                    if p == 0:
                        W("scalar", "vS", bias_ready)
                    if 2 * p >= KT - 4:
                        for kt in (2 * p, 2 * p + 1):
                            o_ = 128 * (kt - (KT - 4))
                            c = 512 - o_
                            sl = (kt % 2) * 512
                            E("scalar", lambda e, o_=o_, c=c, sl=sl, base=base: e.activation(
                                PT[:, base + sl:base + sl + c], psSA[:, sl + o_:sl + 512],
                                AF.Exp, bias=biasA[:, :], scale=0.125), "scS")
                        expA_cnt[(hp, u, p)] = cnt["scS"]
                        W("scalar", "tS", s_cntB)
                        for kt in (2 * p, 2 * p + 1):
                            o_ = 128 * (kt - (KT - 4))
                            c = 512 - o_
                            sl = (kt % 2) * 512
                            E("scalar", lambda e, o_=o_, c=c, sl=sl, base=base: e.activation(
                                PT[:, base + 1024 + sl:base + 1024 + sl + c],
                                psSB[:, sl + o_:sl + 512],
                                AF.Exp, bias=biasB[:, :], scale=0.125), "scS")
                    else:
                        E("scalar", lambda e, base=base: e.activation(
                            PT[:, base:base + 1024], psSA[:, :], AF.Exp,
                            bias=biasA[:, :], scale=0.125), "scS")
                        expA_cnt[(hp, u, p)] = cnt["scS"]
                        W("scalar", "tS", s_cntB)
                        E("scalar", lambda e, base=base: e.activation(
                            PT[:, base + 1024:base + 2048], psSB[:, :], AF.Exp,
                            bias=biasB[:, :], scale=0.125), "scS")
                    expB_cnt[(hp, u, p)] = cnt["scS"]
                    exp_cnt[(hp, u, 2 * p)] = cnt["scS"]
                    exp_cnt[(hp, u, 2 * p + 1)] = cnt["scS"]
                    mkts = [kt for kt in (2 * p, 2 * p + 1) if kt >= KT - 8]
                    if mkts:
                        def mgeom(kt):
                            o_ = 0 if kt < KT - 4 else 128 * (kt - (KT - 4))
                            sl = (kt % 2) * 512
                            jp = kt - (KT - 8)
                            if jp < 4:
                                mw, moff = 128 * (jp + 1), [0, 128, 384, 768][jp]
                            else:
                                mw = 512 - o_
                                moff = [1280, 1792, 2176, 2432][jp - 4]
                            return sl, mw, moff
                        W("vector", "scS", expA_cnt[(hp, u, p)])
                        for kt in mkts:
                            sl, mw, moff = mgeom(kt)
                            E("vector", lambda e, sl=sl, base=base, mw=mw, moff=moff: e.tensor_mul(
                                PT[:, base + sl:base + sl + mw],
                                PT[:, base + sl:base + sl + mw],
                                masks[:, moff:moff + mw]), "vS")
                        W("vector", "scS", expB_cnt[(hp, u, p)])
                        for kt in mkts:
                            sl, mw, moff = mgeom(kt)
                            E("vector", lambda e, sl=sl, base=base, mw=mw, moff=moff: e.tensor_mul(
                                PT[:, base + 1024 + sl:base + 1024 + sl + mw],
                                PT[:, base + 1024 + sl:base + 1024 + sl + mw],
                                masks[:, moff:moff + mw]), "vS")
                        for kt in mkts:
                            mask_cnt[(hp, u, kt)] = cnt["vS"]

                def emit_lav(kt, hp=hp, u=u, KT=KT, waits=True):
                    o_ = 0 if kt < KT - 4 else 128 * (kt - (KT - 4))
                    cols = 512 - o_
                    base = slotbase(kt)
                    sl = (kt % 2) * 512
                    if waits:
                        W("tensor", "scS", exp_cnt[(hp, u, kt)])
                    if (hp, u, kt) in mask_cnt:
                        W("tensor", "vS", mask_cnt[(hp, u, kt)])
                    st_, sp_ = (kt == 0), (kt == KT - 1)
                    E("tensor", lambda e, sl=sl, base=base, o_=o_, c=cols, st_=st_, sp_=sp_: e.matmul(
                        psLA[0:1, o_:512], onescol[:, 0:1], PT[:, base + sl:base + sl + c],
                        start=st_, stop=sp_, tile_position=(0, 0)), "tS")
                    E("tensor", lambda e, sl=sl, base=base, o_=o_, c=cols, st_=st_, sp_=sp_: e.matmul(
                        psLB[32:33, o_:512], onescol[:, 1:2],
                        PT[:, base + 1024 + sl:base + 1024 + sl + c],
                        start=st_, stop=sp_, tile_position=(0, 32)), "tS")
                    E("tensor", lambda e, sl=sl, base=base, o_=o_, kt=kt, c=cols, st_=st_, sp_=sp_: e.matmul(
                        psOA[0:64, o_:512], V_s[:, kt * 128:kt * 128 + 64],
                        PT[:, base + sl:base + sl + c],
                        start=st_, stop=sp_, tile_position=(0, 0)), "tS")
                    E("tensor", lambda e, sl=sl, base=base, o_=o_, kt=kt, c=cols, st_=st_, sp_=sp_: e.matmul(
                        psOB[64:128, o_:512], V_s[:, kt * 128 + 64:(kt + 1) * 128],
                        PT[:, base + 1024 + sl:base + 1024 + sl + c],
                        start=st_, stop=sp_, tile_position=(0, 64)), "tS")
                    lav_cnt[(hp, u, kt)] = cnt["tS"]

                for p in range(KT // 2):
                    emit_Spair(p)
                    if p >= 1:
                        if p == 1 and evac_wait is not None:
                            W("tensor", "vS", evac_wait)
                        emit_lav(2 * p - 2, waits=False)
                        emit_lav(2 * p - 1, waits=False)
                emit_lav(KT - 2)
                emit_lav(KT - 1)

                last_lav_prev = lav_cnt[(hp, u, KT - 1)]
                W("vector", "tS", last_lav_prev)
                W("vector", "dS", cnt["dS"])
                E("vector", lambda e, hp=hp, u=u: e.tensor_copy(
                    OT[0:64, hp * 2048 + u * 512:hp * 2048 + (u + 1) * 512], psOA[0:64, :]), "vS")
                E("vector", lambda e, hp=hp, u=u: e.tensor_copy(
                    OT[64:128, hp * 2048 + u * 512:hp * 2048 + (u + 1) * 512], psOB[64:128, :]), "vS")
                E("vector", lambda e: e.tensor_copy(ltmp[0:1, :], psLA[0:1, :]), "vS")
                E("vector", lambda e: e.tensor_copy(ltmp[32:33, :], psLB[32:33, :]), "vS")
                if dbg == "B" and hp == 7 and u == 3:
                    E("vector", lambda e: e.tensor_copy(dbgb[:, 0:512], psLA[:, :]), "vS")
                    E("vector", lambda e: e.tensor_copy(dbgb[:, 512:1024], psOA[:, :]), "vS")
                u_evac[(hp, u)] = cnt["vS"]
                W("gpsimd", "vS", u_evac[(hp, u)])
                E("gpsimd", lambda e, hp=hp, u=u: e.dma_start(
                    out=l_all[2 * hp:2 * hp + 1, u * 512:(u + 1) * 512], in_=ltmp[0:1, :]), "dS")
                E("gpsimd", lambda e, hp=hp, u=u: e.dma_start(
                    out=l_all[2 * hp + 1:2 * hp + 2, u * 512:(u + 1) * 512],
                    in_=ltmp[32:33, :]), "dS")
            last_exp_prev_hp = exp_cnt[(hp, 3, KTU[3] - 1)]

        tAtt = cnt["tS"]
        vAtt = u_evac[(7, 3)]

        if dbg not in ("A", "B"):
            emit_ffn_mean(len(ffn_srcs))
            gMean = cnt["scS"]
            W("vector", "scS", gMean)
            W("vector", "tS", tAtt)
            for wi in range(3):
                E("vector", lambda e, wi=wi: e.reduce_sum(
                    wtot[:, wi:wi + 1], wst[:, wi * 64:(wi + 1) * 64], AX.X), "vS")
            W("tensor", "scS", cnt["scS"])
            W("tensor", "vS", cnt["vS"])
            for wi in range(3):
                if wi > 0:
                    W("tensor", "vS", cnt["vS"])
                E("tensor", lambda e, wi=wi: e.matmul(
                    psT1[:, :], onescol_f[:, :], wtot[:, wi:wi + 1], start=True, stop=True), "tS")
                W("vector", "tS", cnt["tS"])
                E("vector", lambda e, wi=wi: e.tensor_copy(wsc[:, wi:wi + 1], psT1[:, :]), "vS")
            W("vector", "vS", cnt["vS"])
            ECH("vector", lambda e: e.tensor_scalar(wsc[:, 3:6], wsc[:, 0:3], 1.0 / (D * INNER), 1e-5,
                                                    ALU.mult, ALU.max))
            ECH("vector", lambda e: e.reciprocal(wsc[:, 0:3], wsc[:, 3:6]))
            ECH("vector", lambda e: e.tensor_scalar(wsc[:, 6:8], wsc[:, 3:5], 1.0 / 127, None, ALU.mult))
            ECH("vector", lambda e: e.tensor_scalar(wsc[:, 5:6], wsc[:, 5:6], 1.0 / 127, None, ALU.mult))
            W("tensor", "vS", cnt["vS"])
            W("tensor", "vS", vAtt)
            for wi in range(3):
                if wi > 0:
                    W("tensor", "vS", cnt["vS"])
                E("tensor", lambda e, wi=wi: e.matmul(psBC0[:, :], onesrow_f[:, :],
                                                      wsc[0:1, wi:wi + 1], start=True, stop=True), "tS")
                W("vector", "tS", cnt["tS"])
                E("vector", lambda e, wi=wi: e.tensor_copy(swv[:, wi:wi + 1], psBC0[:, :]), "vS")
            for wi, scol in enumerate([6, 7, 5]):
                if wi > 0:
                    W("tensor", "vS", cnt["vS"])
                E("tensor", lambda e, scol=scol: e.matmul(psBC1[:, :], onesrow_f[:, :],
                                                          wsc[0:1, scol:scol + 1],
                                                          start=True, stop=True), "tS")
                W("vector", "tS", cnt["tS"])
                E("vector", lambda e, wi=wi: e.tensor_copy(mwv[:, wi:wi + 1], psBC1[:, :]), "vS")
            wsc_ready = cnt["vS"]

            qdst = {0: wgq, 1: wuqq, 2: wdqq}
            qdone = []

            def _qsafe(item):
                wi, idx, _ = item
                return wi == 0 or (wi == 1 and idx < 24)
            ffn_q_order = ([x for x in ffn_srcs if _qsafe(x)]
                           + [x for x in ffn_srcs if not _qsafe(x)])
            qstate = {"i": 0, "tb3": None, "tb3_waited": False}

            def emit_quant(n):
                for _ in range(n):
                    i = qstate["i"]
                    if i >= len(ffn_q_order):
                        return
                    wi, idx, src = ffn_q_order[i]
                    if not _qsafe((wi, idx, src)) and qstate["tb3"] is None:
                        return
                    slot = i % 8
                    lq = "sync" if i % 2 == 0 else "scalar"
                    dsem = "dQ" if i % 2 == 0 else "dQ2"
                    if i < 8:
                        W(lq, "scS", gMean)
                        W(lq, "vS", wsc_ready)
                        W(lq, "tS", tAtt)
                    else:
                        W(lq, "vS", qdone[i - 8])
                    E(lq, lambda e, s=src, sl=slot: e.dma_start(
                        out=fring[:, sl * 512:(sl + 1) * 512], in_=s), dsem)
                    W("vector", dsem, cnt[dsem])
                    if i < 8:
                        W("vector", "vS", wsc_ready)
                    if not _qsafe((wi, idx, src)) and not qstate["tb3_waited"]:
                        W("vector", "tS", qstate["tb3"])
                        qstate["tb3_waited"] = True
                    if wi < 2:
                        dcol = (idx // 8) * INNER + (idx % 8) * 512
                    else:
                        dcol = (idx // 2) * D + (idx % 2) * 512
                    E("vector", lambda e, sl=slot, wi=wi: e.tensor_scalar(
                        fring[:, sl * 512:(sl + 1) * 512], fring[:, sl * 512:(sl + 1) * 512],
                        swv[:, wi:wi + 1], 1.0, ALU.mult, ALU.min))
                    E("vector", lambda e, sl=slot: e.tensor_scalar(
                        fring[:, sl * 512:(sl + 1) * 512], fring[:, sl * 512:(sl + 1) * 512],
                        -1.0, MAGIC, ALU.max, ALU.add))
                    E("vector", lambda e, sl=slot, wi=wi, dcol=dcol: e.tensor_scalar(
                        qdst[wi][:, dcol:dcol + 512], fring[:, sl * 512:(sl + 1) * 512],
                        MAGIC, None, ALU.subtract), "vS")
                    qdone.append(cnt["vS"])
                    qstate["i"] = i + 1
        else:
            def emit_quant(n):
                return

        W("vector", "dS", cnt["dS"])
        E("vector", lambda e: e.reciprocal(l_all[:, :], l_all[:, :]), "vS")
        W("tensor", "scS", cnt["scS"])
        W("tensor", "vS", cnt["vS"])
        wo_ev = {}
        for u in range(4):
            for hp in range(8):
                if (u, hp) != (0, 0):
                    W("tensor", "vS", cnt["vS"])
                E("tensor", lambda e, hp=hp, u=u: e.matmul(
                    psR[:, :], selEs[:, hp * 128:(hp + 1) * 128],
                    l_all[:, u * 512:(u + 1) * 512], start=True, stop=True), "tS")
                W("vector", "tS", cnt["tS"])
                E("vector", lambda e, hp=hp, u=u: e.tensor_mul(
                    OT[:, hp * 2048 + u * 512:hp * 2048 + (u + 1) * 512],
                    OT[:, hp * 2048 + u * 512:hp * 2048 + (u + 1) * 512], psR[:, :]), "vS")
            norm_u = cnt["vS"]
            for tt in range(4):
                ti = u * 4 + tt
                for nbk in range(2):
                    pk2 = u * 8 + tt * 2 + nbk
                    bank = [psW0, psW1, psW2, psW3][pk2 % 4]
                    if pk2 >= 4:
                        W("tensor", "vS", wo_ev[pk2 - 4])
                    elif pk2 >= 2:
                        W("tensor", "vS", vAtt)
                    W("tensor", "vS", norm_u)
                    for hp in range(8):
                        E("tensor", lambda e, b=bank, hp=hp, u=u, tt=tt, nbk=nbk: e.matmul(
                            b[:, :],
                            OT[:, hp * 2048 + u * 512 + tt * 128:hp * 2048 + u * 512 + (tt + 1) * 128],
                            wo_sb[:, hp * D + nbk * 512:hp * D + nbk * 512 + 512],
                            start=(hp == 0), stop=(hp == 7)), "tS" if hp == 7 else None)
                    W("vector", "tS", cnt["tS"])
                    E("vector", lambda e, b=bank, ti=ti, nbk=nbk: e.scalar_tensor_tensor(
                        X[:, ti * D + nbk * 512:ti * D + nbk * 512 + 512], b[:, :], 1.0,
                        X[:, ti * D + nbk * 512:ti * D + nbk * 512 + 512],
                        ALU.mult, ALU.add), "vS")
                    wo_ev[pk2] = cnt["vS"]
                    emit_quant(3)
        tB3 = cnt["tS"]
        vB3 = cnt["vS"]


        if dbg == "B":
            _dump([("X", X[:, :]), ("l_all", l_all[:, :]), ("OT", OT[:, :]),
                   ("biasA", biasA[:, :]), ("biasB", biasB[:, :]),
                   ("KT", KT_s[:, :]), ("QT", QT[:, :]), ("V", V_s[:, :]),
                   ("PT", PT[:, :]), ("qk", qk_sb[:, :]), ("qk2", qk2_sb[:, :]),
                   ("dbgb", dbgb[:, :])])
    if dbg not in ("A", "B"):
        qstate["tb3"] = tB3
        W("gpsimd", "tS", tB3)
        emit_quant(128 - qstate["i"])
        quant_gu = cnt["vS"]
        emit_quant(len(ffn_q_order) - qstate["i"])
        quant_v = cnt["vS"]
        quant_g = cnt["gS"]


        if dbg == "C0":
            _dump([("wgq", wgq[:, :]), ("wuqq", wuqq[:, :]), ("wdqq", wdqq[:, :]),
                   ("wsc", wsc[:, :]), ("swv", swv[:, :]), ("mwv", mwv[:, :]),
                   ("wst", wst[:, :]), ("wtot", wtot[:, :])])
    if dbg not in ("A", "B", "C0"):
        tile_done = {}
        tstate = {}
        qh2ds = {}

        def emit_stageA(ti):
            xap = X[:, ti * D:(ti + 1) * D]
            if ti == 0:
                W("vector", "gS", quant_g)
                W("vector", "tS", tB3)
                W("scalar", "tS", tB3)
            E("vector", lambda e, x=xap: e.reduce_sum(sums[:, 0:1], x, AX.X), "vS")
            W("scalar", "vS", cnt["vS"])
            if "gu_prev" in tstate:
                W("scalar", "vS", tstate["gu_prev"][7][1])
            E("scalar", lambda e, x=xap: e.activation(
                scr4k[:, :], x, AF.Square, accum_out=sumsq[:, 0:1]), "scS")
            W("vector", "scS", cnt["scS"])
            ECH("vector", lambda e: e.tensor_scalar(tmp4a[:, 0:1], sums[:, 0:1], 1.0 / D, None, ALU.mult))
            ECH("vector", lambda e: e.tensor_scalar(tmp4b[:, 0:1], tmp4a[:, 0:1], tmp4a[:, 0:1], -1.0, ALU.mult, ALU.mult))
            ECH("vector", lambda e: e.scalar_tensor_tensor(tmp4b[:, 0:1], sumsq[:, 0:1], 1.0 / D,
                                                         tmp4b[:, 0:1], ALU.mult, ALU.add))
            E("vector", lambda e: e.tensor_scalar(
                rstds[:, 0:1], tmp4b[:, 0:1], LN_EPS, None, ALU.add), "vS")
            W("scalar", "vS", cnt["vS"])
            E("scalar", lambda e: e.activation(rstds[:, 0:1], rstds[:, 0:1], AF.Sqrt), "scS")
            W("vector", "scS", cnt["scS"])
            ECH("vector", lambda e: e.reciprocal(rstds[:, 0:1], rstds[:, 0:1]))
            ECH("vector", lambda e: e.tensor_scalar(negmr[:, 0:1], tmp4a[:, 0:1], rstds[:, 0:1],
                                                  -1.0, ALU.mult, ALU.mult))
            E("vector", lambda e, x=xap: e.tensor_scalar(
                scr4k[:, :], x, rstds[:, 0:1], negmr[:, 0:1], ALU.mult, ALU.add))
            ECH("vector", lambda e: e.tensor_reduce(am_sb[:, 0:1], scr4k[:, :], AX.X, ALU.max,
                                                  apply_absolute_value=True))
            ECH("vector", lambda e: e.tensor_scalar(am_sb[:, 0:1], am_sb[:, 0:1], 1e-5, None, ALU.max))
            ECH("vector", lambda e: e.reciprocal(sa_sb[:, 0:1], am_sb[:, 0:1]))
            ECH("vector", lambda e: e.tensor_scalar(sa_sb[:, 0:1], sa_sb[:, 0:1], 127.0, None, ALU.mult))
            if "gu_prev" in tstate:
                W("vector", "scS", tstate["gu_prev"][7][0])
            E("vector", lambda e: e.tensor_mul(inv_sb[:, 0:1], am_sb[:, 0:1], mwv[:, 0:1]))
            E("vector", lambda e: e.tensor_mul(inv_sb[:, 1:2], am_sb[:, 0:1], mwv[:, 1:2]))
            E("vector", lambda e: e.tensor_scalar(scr4k[:, :], scr4k[:, :], sa_sb[:, 0:1], MAGIC,
                                                  ALU.mult, ALU.add))
            if "qh2_tr" in tstate:
                W("vector", "tS", tstate["qh2_tr"])
            E("vector", lambda e: e.tensor_scalar(qh2_sb[:, :], scr4k[:, :], MAGIC, None,
                                                  ALU.subtract), "vS")
            qh2ds[ti] = cnt["vS"]

        emit_stageA(0)
        for ti in range(NT_OWN):
            W("tensor", "vS", qh2ds[ti])
            if ti == 0:
                W("tensor", "vS", quant_gu)
                W("tensor", "gS", quant_g)
            else:
                W("tensor", "vS", tstate["qh2T_evac"])
                W("tensor", "vS", tstate["ev7_prev"])
            for ck in range(8):
                E("tensor", lambda e, ck=ck: e.transpose(
                    psTrC[:, ck * 128:(ck + 1) * 128], qh2_sb[:, ck * 128:(ck + 1) * 128],
                    ident[:, :]), "tS" if ck == 7 else None)
            tstate["qh2_tr"] = cnt["tS"]
            W("vector", "tS", cnt["tS"])
            if "gate_last" in tstate:
                W("vector", "tS", tstate["gate_last"])
            E("vector", lambda e: e.tensor_copy(qh2T[:, :], psTrC[:, :]), "vS")
            qh2T_done = cnt["vS"]
            tstate["qh2T_evac"] = qh2T_done
            gu_done = []
            for ib in range(8):
                pG = [psG0, psG1][ib % 2]
                pU = [psU0, psU1][ib % 2]
                W("tensor", "vS", qh2T_done)
                if ib >= 2:
                    W("tensor", "vS", gu_done[ib - 2][1])
                    W("tensor", "scS", gu_done[ib - 2][0])
                elif "gu_prev" in tstate:
                    W("tensor", "vS", tstate["gu_prev"][6 + ib][1])
                    W("tensor", "scS", tstate["gu_prev"][6 + ib][0])
                for ck in range(8):
                    E("tensor", lambda e, p=pG, ck=ck, ib=ib: e.matmul(
                        p[:, :], qh2T[:, ck * 128:(ck + 1) * 128],
                        wgq[:, ck * INNER + ib * 512:ck * INNER + ib * 512 + 512],
                        start=(ck == 0), stop=(ck == 7)), "tS" if ck == 7 else None)
                for ck in range(8):
                    E("tensor", lambda e, p=pU, ck=ck, ib=ib: e.matmul(
                        p[:, :], qh2T[:, ck * 128:(ck + 1) * 128],
                        wuqq[:, ck * INNER + ib * 512:ck * INNER + ib * 512 + 512],
                        start=(ck == 0), stop=(ck == 7)), "tS" if ck == 7 else None)
                mm_done = cnt["tS"]
                W("scalar", "tS", mm_done)
                W("scalar", "vS", cnt["vS"])
                E("scalar", lambda e, p=pG: e.activation(
                    gsil[:, :], p[:, :], AF.Silu, scale=inv_sb[:, 0:1]), "scS")
                W("vector", "scS", cnt["scS"])
                if ib == 0 and "qgu_tr_last" in tstate:
                    W("vector", "tS", tstate["qgu_tr_last"])
                E("vector", lambda e, p=pU, ib=ib: e.scalar_tensor_tensor(
                    gu_sb[:, ib * 512:(ib + 1) * 512], p[:, :], inv_sb[:, 1:2], gsil[:, :],
                    ALU.mult, ALU.mult), "vS")
                gu_done.append((cnt["scS"], cnt["vS"]))
                if ib == 7:
                    tstate["gate_last"] = cnt["tS"]
                    tstate["gu_prev"] = list(gu_done)
            ECH("vector", lambda e: e.tensor_reduce(am_sb[:, 1:2], gu_sb[:, :], AX.X, ALU.max,
                                                  apply_absolute_value=True))
            ECH("vector", lambda e: e.tensor_scalar(am_sb[:, 1:2], am_sb[:, 1:2], 1e-5, None, ALU.max))
            ECH("vector", lambda e: e.reciprocal(sa_sb[:, 1:2], am_sb[:, 1:2]))
            ECH("vector", lambda e: e.tensor_scalar(sa_sb[:, 1:2], sa_sb[:, 1:2], 127.0, None, ALU.mult))
            E("vector", lambda e: e.tensor_mul(inv_sb[:, 2:3], am_sb[:, 1:2], mwv[:, 2:3]))
            ev_rounds = []
            tr_cnts = []
            for rr in range(8):
                E("vector", lambda e, rr=rr: e.tensor_scalar(
                    gu_sb[:, rr * 512:(rr + 1) * 512], gu_sb[:, rr * 512:(rr + 1) * 512],
                    sa_sb[:, 1:2], MAGIC, ALU.mult, ALU.add))
                E("vector", lambda e, rr=rr: e.tensor_scalar(
                    gu_sb[:, rr * 512:(rr + 1) * 512], gu_sb[:, rr * 512:(rr + 1) * 512],
                    MAGIC, None, ALU.subtract), "vS")
                q_cnt = cnt["vS"]
                pT = [psTrCf, psTrCf2][rr % 2]
                W("tensor", "vS", q_cnt)
                if rr >= 2:
                    W("tensor", "vS", ev_rounds[rr - 2])
                elif "ev7_prev" in tstate and ti > 0:
                    W("tensor", "vS", tstate["ev7_prev"])
                for ck in range(4):
                    E("tensor", lambda e, pT=pT, rr=rr, ck=ck: e.transpose(
                        pT[:, ck * 128:(ck + 1) * 128],
                        gu_sb[:, rr * 512 + ck * 128:rr * 512 + (ck + 1) * 128],
                        ident_f[:, :]), "tS" if ck == 3 else None)
                tr_cnts.append(cnt["tS"])
                if rr >= 1:
                    prr = rr - 1
                    W("vector", "tS", tr_cnts[prr])
                    if prr == 0 and "down_mm_last" in tstate:
                        W("vector", "tS", tstate["down_mm_last"])
                    E("vector", lambda e, prr=prr: e.tensor_copy(
                        qguT[:, prr * 512:(prr + 1) * 512],
                        [psTrCf, psTrCf2][prr % 2][:, :]), "vS")
                    ev_rounds.append(cnt["vS"])
            W("vector", "tS", tr_cnts[7])
            E("vector", lambda e: e.tensor_copy(qguT[:, 7 * 512:8 * 512], psTrCf2[:, :]), "vS")
            ev_rounds.append(cnt["vS"])
            tstate["qgu_tr_last"] = tr_cnts[7]
            tstate["ev7_prev"] = ev_rounds[7]
            if ti + 1 < NT_OWN:
                emit_stageA(ti + 1)
            for nbk in range(2):
                pD = [psD0, psD1][nbk]
                W("tensor", "vS", ev_rounds[7])
                if ti == 0 and nbk == 0:
                    W("tensor", "vS", quant_v)
                if ti > 0 and nbk == 0:
                    W("tensor", "vS", tstate["dn_prev"])
                for ic in range(32):
                    E("tensor", lambda e, p=pD, ic=ic, nbk=nbk: e.matmul(
                        p[:, :], qguT[:, ic * 128:(ic + 1) * 128],
                        wdqq[:, ic * D + nbk * 512:ic * D + nbk * 512 + 512],
                        start=(ic == 0), stop=(ic == 31)), "tS" if ic == 31 else None)
                W("vector", "tS", cnt["tS"])
                E("vector", lambda e, p=pD, ti=ti, nbk=nbk: e.scalar_tensor_tensor(
                    X[:, ti * D + nbk * 512:ti * D + nbk * 512 + 512], p[:, :], inv_sb[:, 2:3],
                    X[:, ti * D + nbk * 512:ti * D + nbk * 512 + 512], ALU.mult, ALU.add), "vS")
            tile_done[ti] = cnt["vS"]
            tstate["dn_prev"] = cnt["vS"]
            tstate["down_mm_last"] = cnt["tS"]
            W("gpsimd", "vS", tile_done[ti])
            E("gpsimd", lambda e, ti=ti: e.dma_start(
                out=dout[ti * 128:(ti + 1) * 128, :], in_=X[:, ti * D:(ti + 1) * D]), "dS")
            if dbg == "C1" and ti == 0:
                _dump([("qh2", qh2_sb[:, :]), ("qh2T", qh2T[:, :]), ("qguT", qguT[:, :]),
                       ("am", am_sb[:, :]), ("sa", sa_sb[:, :]), ("inv", inv_sb[:, :]),
                       ("X0", X[:, 0:1024]), ("gsil", gsil[:, :]), ("qgu", gu_sb[:, :]),
                       ("rstds", rstds[:, :]), ("negmr", negmr[:, :])])
                break

        W("gpsimd", "dS", cnt["dS"])

    with nc.Block() as block:
        def mk(engname):
            def run(eng):
                for op in OPS[engname]:
                    if op[0] == "w":
                        eng.wait_ge(sems[op[1]], op[2])
                    else:
                        ins = op[1](eng)
                        if op[2] is not None:
                            ins.then_inc(sems[op[2][0]], op[2][1])
            return run
        block.sync(mk("sync"))
        block.tensor(mk("tensor"))
        block.vector(mk("vector"))
        block.scalar(mk("scalar"))
        block.gpsimd(mk("gpsimd"))
    return nc


def _masks_np(half):
    m = np.zeros((128, 2560), dtype=np.float32)
    ii = np.arange(128)[:, None]
    offs03 = [(0, 128), (128, 256), (384, 384), (768, 512)]
    for jp, (o, w) in enumerate(offs03):
        if half == 0:
            m[:, o:o + w] = 1.0
        else:
            jj = np.arange(w)[None, :]
            m[:, o:o + w] = (jj >= 128 * jp + ii).astype(np.float32)
    offs47 = [(1280, 512), (1792, 384), (2176, 256), (2432, 128)]
    for j4, (o, w) in enumerate(offs47):
        if half == 1:
            m[:, o:o + w] = 0.0
        else:
            jj = np.arange(w)[None, :]
            m[:, o:o + w] = (jj >= ii).astype(np.float32)
    return np.asarray(m, dtype=ml_dtypes.bfloat16)


def kernel(L0_attn_x, L0_attn_W_dq, L0_attn_W_uq, L0_attn_W_dkv, L0_attn_W_uk,
           L0_attn_W_uv, L0_attn_W_o, L0_ffn_w_gate, L0_ffn_w_up, L0_ffn_w_down):
    if "nc" not in _CACHE:
        _CACHE["nc"] = _build_nc()
    nc = _CACHE["nc"]

    x = np.ascontiguousarray(np.asarray(L0_attn_x, dtype=np.float32))
    shared = {
        "w_dq": np.ascontiguousarray(np.asarray(L0_attn_W_dq, dtype=np.float32)),
        "w_dkv": np.ascontiguousarray(np.asarray(L0_attn_W_dkv, dtype=np.float32)),
        "w_uq": np.ascontiguousarray(np.asarray(L0_attn_W_uq, dtype=np.float32)),
        "w_uk": np.ascontiguousarray(np.asarray(L0_attn_W_uk, dtype=np.float32)),
        "w_uv": np.ascontiguousarray(np.asarray(L0_attn_W_uv, dtype=np.float32)),
        "w_o": np.ascontiguousarray(np.asarray(L0_attn_W_o, dtype=np.float32)),
        "wg_t": np.ascontiguousarray(np.asarray(L0_ffn_w_gate, dtype=np.float32).T),
        "wu_t": np.ascontiguousarray(np.asarray(L0_ffn_w_up, dtype=np.float32).T),
        "wd_t": np.ascontiguousarray(np.asarray(L0_ffn_w_down, dtype=np.float32).T),
    }
    blocks_of = {0: [1, 3, 5, 7], 1: [0, 2, 4, 6]}
    in_maps = []
    for c in range(8):
        b, h = c >> 1, c & 1
        im = dict(shared)
        im["x_own"] = np.ascontiguousarray(
            np.concatenate([x[b, 512 * k:512 * (k + 1)] for k in blocks_of[h]], axis=0))
        im["x_all"] = np.ascontiguousarray(x[b])
        im["masks"] = _masks_np(h)
        in_maps.append(im)

    res = run_bass_kernel_spmd(nc, in_maps, core_ids=list(range(8)))
    out = np.empty((NBATCH, SEQ, D), dtype=np.float32)
    for c in range(8):
        b, h = c >> 1, c & 1
        r = np.asarray(res.results[c]["out"])
        for j, k in enumerate(blocks_of[h]):
            out[b, 512 * k:512 * (k + 1)] = r[512 * j:512 * (j + 1)]
    return out


# version 20
# speedup vs baseline: 2.0771x; 1.0038x over previous
import numpy as np
import ml_dtypes
import concourse.bass as bass
import concourse.mybir as mybir
from concourse.bass_utils import run_bass_kernel_spmd

F32 = mybir.dt.float32
BF16 = mybir.dt.bfloat16
FP8 = mybir.dt.float8e4
AF = mybir.ActivationFunctionType
ALU = mybir.AluOpType
AX = mybir.AxisListType

D = 1024
SEQ = 4096
NBATCH = 4
DC = 256
INNER = 4096
NT_OWN = 16
NT_ALL = 32
KTU = [8, 16, 24, 32]
MAGIC = 12582912.0
LN_EPS = 1e-5
C_SAFETY = -0.1275

_CACHE = {}


def _build_nc(dbg=None):
    nc = bass.Bass("TRN2", target_bir_lowering=False)

    dx_own = nc.dram_tensor("x_own", [2048, D], F32, kind="ExternalInput").ap()
    dx_all = nc.dram_tensor("x_all", [SEQ, D], F32, kind="ExternalInput").ap()
    dw_dq = nc.dram_tensor("w_dq", [D, DC], F32, kind="ExternalInput").ap()
    dw_dkv = nc.dram_tensor("w_dkv", [D, DC], F32, kind="ExternalInput").ap()
    dw_uq = nc.dram_tensor("w_uq", [DC, D], F32, kind="ExternalInput").ap()
    dw_uk = nc.dram_tensor("w_uk", [DC, D], F32, kind="ExternalInput").ap()
    dw_uv = nc.dram_tensor("w_uv", [DC, D], F32, kind="ExternalInput").ap()
    dw_o = nc.dram_tensor("w_o", [D, D], F32, kind="ExternalInput").ap()
    dwg = nc.dram_tensor("wg_t", [D, INNER], F32, kind="ExternalInput").ap()
    dwu = nc.dram_tensor("wu_t", [D, INNER], F32, kind="ExternalInput").ap()
    dwd = nc.dram_tensor("wd_t", [INNER, D], F32, kind="ExternalInput").ap()
    dmasks = nc.dram_tensor("masks", [128, 2560], BF16, kind="ExternalInput").ap()
    dout = nc.dram_tensor("out", [2048, D], F32, kind="ExternalOutput").ap()

    ident_c = nc.inline_tensor(np.eye(128, dtype=ml_dtypes.bfloat16), "ident_c")
    identf_c = nc.inline_tensor(np.eye(128, dtype=np.float32), "identf_c")
    onescol_c = nc.inline_tensor(np.ones((128, 2), dtype=ml_dtypes.bfloat16), "onescol_c")
    onescol_f_c = nc.inline_tensor(np.ones((128, 1), dtype=np.float32), "onescol_f_c")
    onesrow_f_c = nc.inline_tensor(np.ones((1, 128), dtype=np.float32), "onesrow_f_c")
    bo = np.zeros((128, 2), dtype=np.float32)
    bo[0:64, 0] = 1.0
    bo[64:128, 1] = 1.0
    blockones_c = nc.inline_tensor(bo, "blockones_c")
    sel = np.zeros((2, 256), dtype=np.float32)
    sel[0, 0:128] = 1.0
    sel[1, 128:256] = 1.0
    selAB_c = nc.inline_tensor(sel, "selAB_c")
    selE = np.zeros((16, 8 * 128), dtype=np.float32)
    for hp in range(8):
        selE[2 * hp, hp * 128:hp * 128 + 64] = 1.0
        selE[2 * hp + 1, hp * 128 + 64:hp * 128 + 128] = 1.0
    selE_c = nc.inline_tensor(selE, "selE_c")

    base0 = (nc.sbuf_base + 31) // 32 * 32
    top = nc.sbuf_top // 32 * 32
    cur = [base0]

    sboff = {}

    def sb(name, shape, dtype):
        o = cur[0]
        t = nc.alloc_sbuf_tensor_at(name, shape, dtype, offset=o)
        sboff[name] = o
        nbytes = shape[1] * mybir.dt.size(dtype)
        cur[0] = o + ((nbytes + 31) // 32) * 32
        assert cur[0] <= top, f"sbuf overflow at {name}"
        return t

    X = sb("X", [128, NT_OWN * D], F32)
    ident = sb("ident", [128, 128], BF16)
    ident_f = sb("ident_f", [128, 128], F32)
    onescol = sb("onescol", [128, 2], BF16)
    onescol_f = sb("onescol_f", [128, 1], F32)
    onesrow_f = sb("onesrow_f", [1, 128], F32)
    blockones = sb("blockones", [128, 2], F32)
    selAB = sb("selAB", [2, 256], F32)
    selEs = sb("selEs", [16, 8 * 128], F32)
    masks = sb("masks", [128, 2560], BF16)
    sums = sb("sums", [128, 4], F32)
    sumsq = sb("sumsq", [128, 4], F32)
    rstds = sb("rstds", [128, 4], F32)
    negmr = sb("negmr", [128, 4], F32)
    tmp4a = sb("tmp4a", [128, 4], F32)
    tmp4b = sb("tmp4b", [128, 4], F32)
    biasA = sb("biasA", [128, 1], F32)
    biasB = sb("biasB", [128, 1], F32)
    qk_sb = sb("qk_sb", [2, 2], F32)
    qk2_sb = sb("qk2_sb", [2, 1], F32)
    ltmp = sb("ltmp", [33, 512], F32)
    wst = sb("wst", [128, 192], F32)
    wtot = sb("wtot", [128, 3], F32)
    wsc = sb("wsc", [1, 8], F32)
    swv = sb("swv", [128, 3], F32)
    mwv = sb("mwv", [128, 3], F32)
    am_sb = sb("am_sb", [128, 2], F32)
    sa_sb = sb("sa_sb", [128, 2], F32)
    inv_sb = sb("inv_sb", [128, 3], F32)
    scr4k = sb("scr4k", [128, D], F32)

    R = cur[0]
    avail = top - R
    assert avail >= 126976, f"era region too small: {avail}"

    def at(name, shape, dtype, off, size):
        nbytes = shape[1] * mybir.dt.size(dtype)
        assert nbytes <= size, f"{name}: {nbytes} > {size}"
        assert R + off + nbytes <= top, f"{name} beyond top"
        return nc.alloc_sbuf_tensor_at(name, shape, dtype, offset=R + off)

    xstage = at("xstage", [128, 2 * D], F32, 0, 8192)
    h1tok = at("h1tok", [128, 2 * D], BF16, 8192, 4096)
    h1T = at("h1T", [128, 8 * 512], BF16, 12288, 8192)
    cqT = at("cqT", [128, 2 * 2048], BF16, 20480, 8192)
    ckvT = at("ckvT", [128, 2 * SEQ], BF16, 28672, 16384)
    wdq_sb = at("wdq_sb", [128, 8 * DC], BF16, 45056, 4096)
    wdkv_sb = at("wdkv_sb", [128, 8 * DC], BF16, 49152, 4096)
    fstage = at("fstage", [128, 2 * 512], F32, 53248, 4096)
    wuq_sb = at("wuq_sb", [128, 2 * D], BF16, 57344, 4096)
    wuk_sb = at("wuk_sb", [128, 2 * D], BF16, 61440, 4096)
    wuv_sb = at("wuv_sb", [128, 2 * D], BF16, 65536, 4096)
    wo_sb = at("wo_sb", [128, 8 * D], BF16, 69632, 16384)
    OT = at("OT", [128, 8 * 2048], BF16, 86016, 32768)
    l_all = at("l_all", [16, 2048], F32, 118784, 8192)
    wf_dq = at("wf_dq", [128, 8 * DC], F32, 86016, 8192)
    wf_dkv = at("wf_dkv", [128, 8 * DC], F32, 94208, 8192)
    wf_uq = at("wf_uq", [128, 2 * D], F32, 102400, 8192)
    wf_uk = at("wf_uk", [128, 2 * D], F32, 110592, 8192)
    wf_uv = at("wf_uv", [128, 2 * D], F32, 118784, 8192)
    QT = at("QT", [128, 2048], BF16, 0, 4096)
    PT = at("PT", [128, 8 * 512], BF16, 4096, 8192)
    sqscr = at("sqscr", [128, 2048], BF16, 8192, 4096)
    KT_s = at("KT_s", [128, SEQ], BF16, 12288, 8192)
    V_s = at("V_s", [128, SEQ], BF16, 45056, 8192)
    wgq = at("wgq", [128, 8 * INNER], FP8, 20480, 32768)
    wuqq = at("wuqq", [128, 8 * INNER], FP8, 57344, 32768)
    wdqq = at("wdqq", [128, 32 * D], FP8, 90112, 32768)
    gu_sb = at("gu_sb", [128, INNER], F32, 0, 16384)
    fring = at("fring", [128, 8 * 512], F32, 0, 16384)
    qh2T = at("qh2T", [128, 8 * 128], BF16, 122880, 2048)
    qh2_sb = at("qh2_sb", [128, D], BF16, 124928, 2048)
    qguT = at("qguT", [128, 32 * 128], BF16, 0, 8192)
    gsil = nc.alloc_sbuf_tensor_at("gsil", [128, 512], F32, offset=sboff["scr4k"])
    dbgb = at("dbgb", [128, 1024], F32, 20480, 32768) if dbg else None
    assert R + 126976 <= top

    def ps(name, shape, dtype, bank):
        nc.psum_base = bank
        return nc.alloc_psum_tensor(name, shape, dtype)

    psTrA = ps("psTrA", [128, 1024], BF16, 0)
    psP0 = ps("psP0", [128, 512], F32, 1)
    psP1 = ps("psP1", [128, 512], F32, 2)
    psSA = ps("psSA", [128, 1024], F32, 0)
    psSB = ps("psSB", [128, 1024], F32, 2)
    psSA0 = ps("psSA0", [128, 512], F32, 0)
    psSA1 = ps("psSA1", [128, 512], F32, 1)
    psSB0 = ps("psSB0", [128, 512], F32, 2)
    psSB1 = ps("psSB1", [128, 512], F32, 3)
    psLA = ps("psLA", [128, 512], F32, 4)
    psOA = ps("psOA", [128, 512], F32, 5)
    psLB = ps("psLB", [128, 512], F32, 6)
    psOB = ps("psOB", [128, 512], F32, 7)
    psN0 = ps("psN0", [2, 1], F32, 4)
    psN1 = ps("psN1", [2, 1], F32, 5)
    psBC0 = ps("psBC0", [128, 1], F32, 6)
    psBC1 = ps("psBC1", [128, 1], F32, 7)
    psW0 = ps("psW0", [128, 512], F32, 0)
    psW1 = ps("psW1", [128, 512], F32, 1)
    psW2 = ps("psW2", [128, 512], F32, 4)
    psW3 = ps("psW3", [128, 512], F32, 5)
    psR = ps("psR", [128, 512], F32, 2)
    psT1 = ps("psT1", [1, 1], F32, 3)
    psTrC = ps("psTrC", [128, 1024], BF16, 7)
    psTrCf = ps("psTrCf", [128, 512], F32, 0)
    psTrCf2 = ps("psTrCf2", [128, 512], F32, 7)
    psG0 = ps("psG0", [128, 512], F32, 1)
    psG1 = ps("psG1", [128, 512], F32, 2)
    psU0 = ps("psU0", [128, 512], F32, 3)
    psU1 = ps("psU1", [128, 512], F32, 4)
    psD0 = ps("psD0", [128, 512], F32, 5)
    psD1 = ps("psD1", [128, 512], F32, 6)
    nc.psum_base = 8

    DMA_SEMS = ("dL", "dS", "dF", "dQ", "dQ2")
    sem_names = ["dL", "dS", "dF", "dQ", "dQ2", "tS", "vS", "scS", "gS"]
    sems = {n: nc.alloc_semaphore(n) for n in sem_names}
    cnt = {n: 0 for n in sem_names}
    OPS = {"sync": [], "tensor": [], "vector": [], "scalar": [], "gpsimd": []}

    def W(eng, sem, val):
        if val > 0:
            OPS[eng].append(("w", sem, val))

    def E(eng, fn, inc=None):
        n = 16 if inc in DMA_SEMS else 1
        OPS[eng].append(("e", fn, (inc, n) if inc else None))
        if inc:
            cnt[inc] += n

    _SELF = {"vector": "vS", "scalar": "scS", "gpsimd": "gS", "tensor": "tS"}

    def ECH(eng, fn):
        s = _SELF[eng]
        E(eng, fn, s)
        W(eng, s, cnt[s])

    for dst, src in [(ident, ident_c), (ident_f, identf_c), (onescol, onescol_c), (onescol_f, onescol_f_c),
                     (onesrow_f, onesrow_f_c), (blockones, blockones_c),
                     (selAB, selAB_c), (selEs, selE_c)]:
        E("sync", lambda e, d=dst, s=src: e.dma_start(out=d[:, :], in_=s.ap()), "dL")
    E("sync", lambda e: e.dma_start(out=masks[:, :], in_=dmasks), "dL")
    for t in range(NT_OWN):
        E("sync", lambda e, t=t: e.dma_start(out=X[:, t * D:(t + 1) * D],
                                             in_=dx_own[t * 128:(t + 1) * 128, :]), "dL")
    for ck in range(8):
        E("sync", lambda e, ck=ck: e.dma_start(out=wf_dq[:, ck * DC:(ck + 1) * DC],
                                               in_=dw_dq[ck * 128:(ck + 1) * 128, :]), "dL")
        E("sync", lambda e, ck=ck: e.dma_start(out=wf_dkv[:, ck * DC:(ck + 1) * DC],
                                               in_=dw_dkv[ck * 128:(ck + 1) * 128, :]), "dL")
    for cc in range(2):
        E("sync", lambda e, cc=cc: e.dma_start(out=wf_uq[:, cc * D:(cc + 1) * D],
                                               in_=dw_uq[cc * 128:(cc + 1) * 128, :]), "dL")
        E("sync", lambda e, cc=cc: e.dma_start(out=wf_uk[:, cc * D:(cc + 1) * D],
                                               in_=dw_uk[cc * 128:(cc + 1) * 128, :]), "dL")
        E("sync", lambda e, cc=cc: e.dma_start(out=wf_uv[:, cc * D:(cc + 1) * D],
                                               in_=dw_uv[cc * 128:(cc + 1) * 128, :]), "dL")
    dl_stage5 = cnt["dL"]
    W("vector", "dL", dl_stage5)
    E("vector", lambda e: e.tensor_copy(wdq_sb[:, :], wf_dq[:, :]), "vS")
    E("vector", lambda e: e.tensor_copy(wdkv_sb[:, :], wf_dkv[:, :]), "vS")
    E("vector", lambda e: e.tensor_copy(wuq_sb[:, :], wf_uq[:, :]), "vS")
    E("vector", lambda e: e.tensor_copy(wuk_sb[:, :], wf_uk[:, :]), "vS")
    E("vector", lambda e: e.tensor_copy(wuv_sb[:, :], wf_uv[:, :]), "vS")
    wo_cast = []
    for pc in range(16):
        ck, hh = pc // 2, (pc % 2) * 512
        if pc >= 2:
            W("sync", "vS", wo_cast[pc - 2])
        E("sync", lambda e, ck=ck, hh=hh, sl=pc % 2: e.dma_start(
            out=fstage[:, sl * 512:(sl + 1) * 512],
            in_=dw_o[ck * 128:(ck + 1) * 128, hh:hh + 512]), "dL")
        W("vector", "dL", cnt["dL"])
        E("vector", lambda e, ck=ck, hh=hh, sl=pc % 2: e.tensor_copy(
            wo_sb[:, ck * D + hh:ck * D + hh + 512], fstage[:, sl * 512:(sl + 1) * 512]), "vS")
        wo_cast.append(cnt["vS"])
    wcast_all = cnt["vS"]
    dl_head = cnt["dL"]

    h1evac = {}
    projevac = []

    def emit_projs(g):
        own_ = g >= 8
        wtile = wdq_sb if own_ else wdkv_sb
        dstT = cqT if own_ else ckvT
        gcol = (g - 8) * 512 if own_ else g * 512
        span = 2048 if own_ else SEQ
        for m in range(2):
            npr = len(projevac)
            bank = [psP0, psP1][npr % 2]
            if npr >= 2:
                W("tensor", "vS", projevac[npr - 2])
            W("tensor", "vS", h1evac[(g, 3)])
            W("tensor", "vS", wcast_all)
            for ck in range(8):
                E("tensor", lambda e, b=bank, wt=wtile, ck=ck, m=m: e.matmul(
                    b[:, :], wt[:, ck * DC + m * 128:ck * DC + (m + 1) * 128],
                    h1T[:, ck * 512:(ck + 1) * 512],
                    start=(ck == 0), stop=(ck == 7)), "tS" if ck == 7 else None)
            W("vector", "tS", cnt["tS"])
            E("vector", lambda e, b=bank, d=dstT, m=m, gc=gcol, sp=span: e.tensor_copy(
                d[:, m * sp + gc:m * sp + gc + 512], b[:, :]), "vS")
            projevac.append(cnt["vS"])
    xload = {}
    h1done = {}
    sqdone = {}
    tr_done = {}
    chain_done = {}
    for g in range(12):
        own = g >= 8
        for jp in range(2):
            j0, j1 = 2 * jp, 2 * jp + 1
            tis, xaps = [], []
            for j in (j0, j1):
                ti = g * 4 + j
                if not own:
                    prev = ti - 2
                    if prev >= 0 and prev in h1done:
                        W("sync", "vS", h1done[prev])
                        W("sync", "scS", sqdone[prev])
                    E("sync", lambda e, ti=ti, sl=ti % 2: e.dma_start(
                        out=xstage[:, sl * D:(sl + 1) * D],
                        in_=dx_all[ti * 128:(ti + 1) * 128, :]), "dL")
                    xload[ti] = cnt["dL"]
                    xap = xstage[:, (ti % 2) * D:((ti % 2) + 1) * D]
                else:
                    xload[ti] = dl_head
                    xap = X[:, (ti - 32) * D:(ti - 31) * D]
                tis.append(ti)
                xaps.append(xap)
            W("vector", "dL", xload[tis[0]])
            E("vector", lambda e, x=xaps[0], j=j0: e.reduce_sum(sums[:, j:j + 1], x, AX.X), "vS")
            W("vector", "dL", xload[tis[1]])
            ECH("vector", lambda e, x=xaps[1], j=j1: e.reduce_sum(sums[:, j:j + 1], x, AX.X))
            W("scalar", "dL", xload[tis[0]])
            if tis[0] >= 1:
                W("scalar", "vS", h1done.get(tis[0] - 1, 0))
            E("scalar", lambda e, x=xaps[0], j=j0: e.activation(
                scr4k[:, :], x, AF.Square, accum_out=sumsq[:, j:j + 1]), "scS")
            W("scalar", "dL", xload[tis[1]])
            E("scalar", lambda e, x=xaps[1], j=j1: e.activation(
                scr4k[:, :], x, AF.Square, accum_out=sumsq[:, j:j + 1]), "scS")
            sqdone[tis[0]] = sqdone[tis[1]] = cnt["scS"]
            W("vector", "scS", sqdone[tis[1]])
            ECH("vector", lambda e, j=j0: e.tensor_scalar(
                tmp4a[:, j:j + 2], sums[:, j:j + 2], 1.0 / D, None, ALU.mult))
            ECH("vector", lambda e, j=j0: e.tensor_mul(
                tmp4b[:, j:j + 2], tmp4a[:, j:j + 2], tmp4a[:, j:j + 2]))
            ECH("vector", lambda e, j=j0: e.tensor_scalar(
                tmp4b[:, j:j + 2], tmp4b[:, j:j + 2], -1.0, None, ALU.mult))
            ECH("vector", lambda e, j=j0: e.scalar_tensor_tensor(
                tmp4b[:, j:j + 2], sumsq[:, j:j + 2], 1.0 / D,
                tmp4b[:, j:j + 2], ALU.mult, ALU.add))
            E("vector", lambda e, j=j0: e.tensor_scalar(
                rstds[:, j:j + 2], tmp4b[:, j:j + 2], LN_EPS, None, ALU.add), "vS")
            W("scalar", "vS", cnt["vS"])
            E("scalar", lambda e, j=j0: e.activation(
                rstds[:, j:j + 2], rstds[:, j:j + 2], AF.Sqrt), "scS")
            W("vector", "scS", cnt["scS"])
            ECH("vector", lambda e, j=j0: e.reciprocal(rstds[:, j:j + 2], rstds[:, j:j + 2]))
            ECH("vector", lambda e, j=j0: e.scalar_tensor_tensor(
                negmr[:, j:j + 2], tmp4a[:, j:j + 2], -1.0, rstds[:, j:j + 2],
                ALU.mult, ALU.mult))
            if jp == 0 and g >= 1:
                emit_projs(g - 1)
            for j, ti, xap in ((j0, tis[0], xaps[0]), (j1, tis[1], xaps[1])):
                sl = j % 2
                slot_prev = (g, j - 2) if j >= 2 else (g - 1, j + 2)
                if slot_prev in tr_done:
                    W("vector", "tS", tr_done[slot_prev])
                E("vector", lambda e, x=xap, j=j, sl=sl: e.tensor_scalar(
                    h1tok[:, sl * D:(sl + 1) * D], x, rstds[:, j:j + 1], negmr[:, j:j + 1],
                    ALU.mult, ALU.add), "vS")
                h1done[ti] = cnt["vS"]
                W("tensor", "vS", h1done[ti])
                tr_prev = (g, j - 1) if j >= 1 else (g - 1, 3)
                if tr_prev in h1evac:
                    W("tensor", "vS", h1evac[tr_prev])
                for ck in range(8):
                    E("tensor", lambda e, sl=sl, ck=ck: e.transpose(
                        psTrA[:, ck * 128:(ck + 1) * 128],
                        h1tok[:, sl * D + ck * 128:sl * D + (ck + 1) * 128], ident[:, :]),
                        "tS" if ck == 7 else None)
                tr_done[(g, j)] = cnt["tS"]
                W("vector", "tS", tr_done[(g, j)])
                E("vector", lambda e, j=j: e.tensor_copy(
                    h1T[:, :].rearrange("p (c t) -> p c t", c=8)[:, :, j * 128:(j + 1) * 128],
                    psTrA[:, :].rearrange("p (c t) -> p c t", c=8)), "vS")
                h1evac[(g, j)] = cnt["vS"]
    emit_projs(11)
    phaseA_tS = cnt["tS"]
    phaseA_vS = cnt["vS"]

    def _dump(pairs):
        for nm, ap in pairs:
            dt_ = ap.dtype
            dten = nc.dram_tensor("dump_" + nm, list(ap.shape), dt_,
                                  kind="ExternalOutput").ap()
            W("gpsimd", "vS", cnt["vS"])
            W("gpsimd", "scS", cnt["scS"])
            W("gpsimd", "tS", cnt["tS"])
            E("gpsimd", lambda e, d=dten, s=ap: e.dma_start(out=d, in_=s), "dS")
        W("gpsimd", "dS", cnt["dS"])

    if dbg == "A":
        _dump([("ckvT", ckvT[:, :]), ("cqT", cqT[:, :]), ("wdq", wdq_sb[:, :]),
               ("wuk", wuk_sb[:, :]), ("wo", wo_sb[:, :]), ("X", X[:, 0:2048]),
               ("h1T", h1T[:, :]), ("h1tok", h1tok[:, :]), ("sums", sums[:, :]),
               ("sumsq", sumsq[:, :]), ("rstds", rstds[:, :]), ("negmr", negmr[:, :]),
               ("tmp4a", tmp4a[:, :]), ("tmp4b", tmp4b[:, :])])

    ffn_srcs = []
    for wi, dw in enumerate([dwg, dwu, dwd]):
        nrow = D if wi < 2 else INNER
        ncolw = INNER if wi < 2 else D
        idx = 0
        for ck in range(nrow // 128):
            for hh in range(0, ncolw, 512):
                ffn_srcs.append((wi, idx, dw[ck * 128:(ck + 1) * 128, hh:hh + 512]))
                idx += 1
    assert len(ffn_srcs) == 192
    ffn_ptr = [0]
    ffn_gdone = []

    def emit_ffn_mean(nsteps):
        for _ in range(nsteps):
            i = ffn_ptr[0]
            if i >= len(ffn_srcs):
                return
            wi, idx, src = ffn_srcs[i]
            slot = i % 2
            if i < 2:
                W("gpsimd", "vS", wcast_all)
            if i >= 2:
                W("gpsimd", "scS", ffn_gdone[i - 2])
            E("gpsimd", lambda e, s=src, sl=slot: e.dma_start(
                out=fstage[:, sl * 512:(sl + 1) * 512], in_=s), "dF")
            W("scalar", "dF", cnt["dF"])
            col = wi * 64 + idx
            E("scalar", lambda e, sl=slot, col=col: e.activation(
                fstage[:, sl * 512:(sl + 1) * 512], fstage[:, sl * 512:(sl + 1) * 512], AF.Abs,
                accum_out=wst[:, col:col + 1]), "scS")
            ffn_gdone.append(cnt["scS"])
            ffn_ptr[0] += 1

    if dbg != "A":
        u_evac = {}
        exp_cnt = {}
        expA_cnt = {}
        expB_cnt = {}
        mask_cnt = {}
        lav_cnt = {}
        s_cnt = {}
        last_exp_prev_hp = 0
        last_lav_prev = 0

        for hp in range(8):
            W("tensor", "scS", last_exp_prev_hp)
            if hp == 0:
                W("tensor", "vS", phaseA_vS)
            else:
                W("tensor", "vS", u_evac[(hp - 1, 3)])
            bev = []
            for nb in range(8):
                bank = [psSA0, psSA1][nb % 2]
                if nb >= 2:
                    W("tensor", "vS", bev[nb - 2])
                for cc in range(2):
                    E("tensor", lambda e, b=bank, cc=cc, hp=hp, nb=nb: e.matmul(
                        b[:, :], wuk_sb[:, cc * D + hp * 128:cc * D + (hp + 1) * 128],
                        ckvT[:, cc * SEQ + nb * 512:cc * SEQ + nb * 512 + 512],
                        start=(cc == 0), stop=(cc == 1)), "tS" if cc == 1 else None)
                W("vector", "tS", cnt["tS"])
                E("vector", lambda e, b=bank, nb=nb: e.tensor_copy(
                    KT_s[:, nb * 512:(nb + 1) * 512], b[:, :]), "vS")
                bev.append(cnt["vS"])
            for nb in range(4):
                bank = [psSB0, psSB1][nb % 2]
                if nb >= 2:
                    W("tensor", "vS", bev[8 + nb - 2])
                for cc in range(2):
                    E("tensor", lambda e, b=bank, cc=cc, hp=hp, nb=nb: e.matmul(
                        b[:, :], wuq_sb[:, cc * D + hp * 128:cc * D + (hp + 1) * 128],
                        cqT[:, cc * 2048 + nb * 512:cc * 2048 + nb * 512 + 512],
                        start=(cc == 0), stop=(cc == 1)), "tS" if cc == 1 else None)
                W("vector", "tS", cnt["tS"])
                E("vector", lambda e, b=bank, nb=nb: e.tensor_copy(
                    QT[:, nb * 512:(nb + 1) * 512], b[:, :]), "vS")
                bev.append(cnt["vS"])
            for kt in range(32):
                bank = [psLA, psOA][kt % 2]
                if kt >= 2:
                    W("tensor", "vS", bev[12 + kt - 2])
                for cc in range(2):
                    E("tensor", lambda e, b=bank, cc=cc, hp=hp, kt=kt: e.matmul(
                        b[:, 0:128], ckvT[:, cc * SEQ + kt * 128:cc * SEQ + (kt + 1) * 128],
                        wuv_sb[:, cc * D + hp * 128:cc * D + (hp + 1) * 128],
                        start=(cc == 0), stop=(cc == 1)), "tS" if cc == 1 else None)
                W("vector", "tS", cnt["tS"])
                E("vector", lambda e, b=bank, kt=kt: e.tensor_copy(
                    V_s[:, kt * 128:(kt + 1) * 128], b[:, 0:128]), "vS")
                bev.append(cnt["vS"])
            kv_built = cnt["vS"]

            emit_ffn_mean(24)

            W("scalar", "vS", kv_built)
            W("scalar", "tS", cnt["tS"])
            E("scalar", lambda e: e.activation(sqscr[:, :], QT[:, :], AF.Square), "scS")
            W("vector", "scS", cnt["scS"])
            E("vector", lambda e: e.reduce_max(tmp4a[:, 0:1], sqscr[:, :], AX.X), "vS")
            E("scalar", lambda e: e.activation(sqscr[:, :], KT_s[:, 0:2048], AF.Square), "scS")
            W("vector", "scS", cnt["scS"])
            E("vector", lambda e: e.reduce_max(tmp4a[:, 1:2], sqscr[:, :], AX.X), "vS")
            W("scalar", "vS", cnt["vS"])
            E("scalar", lambda e: e.activation(sqscr[:, :], KT_s[:, 2048:4096], AF.Square), "scS")
            W("vector", "scS", cnt["scS"])
            ECH("vector", lambda e: e.reduce_max(tmp4b[:, 0:1], sqscr[:, :], AX.X))
            E("vector", lambda e: e.tensor_max(tmp4a[:, 1:2], tmp4a[:, 1:2], tmp4b[:, 0:1]), "vS")
            W("tensor", "vS", cnt["vS"])
            E("tensor", lambda e: e.matmul(psN0[:, :], blockones[:, :], tmp4a[:, 0:1],
                                           start=True, stop=True), "tS")
            E("tensor", lambda e: e.matmul(psN1[:, :], blockones[:, :], tmp4a[:, 1:2],
                                           start=True, stop=True), "tS")
            W("vector", "tS", cnt["tS"])
            E("vector", lambda e: e.tensor_copy(qk_sb[:, 0:1], psN0[:, :]), "vS")
            E("vector", lambda e: e.tensor_copy(qk_sb[:, 1:2], psN1[:, :]), "vS")
            W("vector", "vS", cnt["vS"])
            W("scalar", "vS", cnt["vS"])
            E("scalar", lambda e: e.activation(qk_sb[:, :], qk_sb[:, :], AF.Sqrt), "scS")
            W("vector", "scS", cnt["scS"])
            ECH("vector", lambda e: e.tensor_scalar(qk2_sb[:, :], qk_sb[:, 0:1], qk_sb[:, 1:2],
                                                    C_SAFETY, ALU.mult, ALU.mult))
            W("tensor", "vS", cnt["vS"])
            E("tensor", lambda e: e.matmul(psBC0[:, :], selAB[:, 0:128], qk2_sb[:, :],
                                           start=True, stop=True), "tS")
            E("tensor", lambda e: e.matmul(psBC1[:, :], selAB[:, 128:256], qk2_sb[:, :],
                                           start=True, stop=True), "tS")
            W("vector", "tS", cnt["tS"])
            E("vector", lambda e: e.tensor_copy(biasA[:, :], psBC0[:, :]), "vS")
            E("vector", lambda e: e.tensor_copy(biasB[:, :], psBC1[:, :]), "vS")
            bias_ready = cnt["vS"]

            for u in range(4):
                KT = KTU[u]
                if u == 0:
                    W("tensor", "vS", bias_ready)
                    evac_wait = None
                else:
                    evac_wait = u_evac[(hp, u - 1)]

                def slotbase(kt):
                    return ((kt // 2) % 2) * 2048

                def emit_Spair(p, hp=hp, u=u, KT=KT, bias_ready=bias_ready,
                               last_lav_prev=last_lav_prev):
                    if p >= 1:
                        W("tensor", "scS", expA_cnt[(hp, u, p - 1)])
                    elif u > 0:
                        W("tensor", "scS", exp_cnt[(hp, u - 1, KTU[u - 1] - 1)])
                    else:
                        W("tensor", "vS", bias_ready)
                    for kt in (2 * p, 2 * p + 1):
                        o_ = 0 if kt < KT - 4 else 128 * (kt - (KT - 4))
                        sl = (kt % 2) * 512
                        E("tensor", lambda e, o_=o_, kt=kt, sl=sl: e.matmul(
                            psSA[:, sl + o_:sl + 512], KT_s[0:64, kt * 128:(kt + 1) * 128],
                            QT[0:64, u * 512 + o_:(u + 1) * 512],
                            start=True, stop=True, tile_position=(0, 0)), "tS")
                    s_cntA = cnt["tS"]
                    if p >= 1:
                        W("tensor", "scS", expB_cnt[(hp, u, p - 1)])
                    for kt in (2 * p, 2 * p + 1):
                        o_ = 0 if kt < KT - 4 else 128 * (kt - (KT - 4))
                        sl = (kt % 2) * 512
                        E("tensor", lambda e, o_=o_, kt=kt, sl=sl: e.matmul(
                            psSB[:, sl + o_:sl + 512], KT_s[64:128, kt * 128:(kt + 1) * 128],
                            QT[64:128, u * 512 + o_:(u + 1) * 512],
                            start=True, stop=True, tile_position=(64, 0)), "tS")
                    s_cnt[(hp, u, p)] = cnt["tS"]
                    s_cntB = cnt["tS"]
                    base = slotbase(2 * p)
                    W("scalar", "tS", s_cntA)
                    if p >= 2:
                        W("scalar", "tS", lav_cnt[(hp, u, 2 * p - 3)])
                    else:
                        W("scalar", "tS", last_lav_prev)
                    if p == 0:
                        W("scalar", "vS", bias_ready)
                    if 2 * p >= KT - 4:
                        for kt in (2 * p, 2 * p + 1):
                            o_ = 128 * (kt - (KT - 4))
                            c = 512 - o_
                            sl = (kt % 2) * 512
                            E("scalar", lambda e, o_=o_, c=c, sl=sl, base=base: e.activation(
                                PT[:, base + sl:base + sl + c], psSA[:, sl + o_:sl + 512],
                                AF.Exp, bias=biasA[:, :], scale=0.125), "scS")
                        expA_cnt[(hp, u, p)] = cnt["scS"]
                        W("scalar", "tS", s_cntB)
                        for kt in (2 * p, 2 * p + 1):
                            o_ = 128 * (kt - (KT - 4))
                            c = 512 - o_
                            sl = (kt % 2) * 512
                            E("scalar", lambda e, o_=o_, c=c, sl=sl, base=base: e.activation(
                                PT[:, base + 1024 + sl:base + 1024 + sl + c],
                                psSB[:, sl + o_:sl + 512],
                                AF.Exp, bias=biasB[:, :], scale=0.125), "scS")
                    else:
                        E("scalar", lambda e, base=base: e.activation(
                            PT[:, base:base + 1024], psSA[:, :], AF.Exp,
                            bias=biasA[:, :], scale=0.125), "scS")
                        expA_cnt[(hp, u, p)] = cnt["scS"]
                        W("scalar", "tS", s_cntB)
                        E("scalar", lambda e, base=base: e.activation(
                            PT[:, base + 1024:base + 2048], psSB[:, :], AF.Exp,
                            bias=biasB[:, :], scale=0.125), "scS")
                    expB_cnt[(hp, u, p)] = cnt["scS"]
                    exp_cnt[(hp, u, 2 * p)] = cnt["scS"]
                    exp_cnt[(hp, u, 2 * p + 1)] = cnt["scS"]
                    mkts = [kt for kt in (2 * p, 2 * p + 1) if kt >= KT - 8]
                    if mkts:
                        def mgeom(kt):
                            o_ = 0 if kt < KT - 4 else 128 * (kt - (KT - 4))
                            sl = (kt % 2) * 512
                            jp = kt - (KT - 8)
                            if jp < 4:
                                mw, moff = 128 * (jp + 1), [0, 128, 384, 768][jp]
                            else:
                                mw = 512 - o_
                                moff = [1280, 1792, 2176, 2432][jp - 4]
                            return sl, mw, moff
                        W("vector", "scS", expA_cnt[(hp, u, p)])
                        for kt in mkts:
                            sl, mw, moff = mgeom(kt)
                            E("vector", lambda e, sl=sl, base=base, mw=mw, moff=moff: e.tensor_mul(
                                PT[:, base + sl:base + sl + mw],
                                PT[:, base + sl:base + sl + mw],
                                masks[:, moff:moff + mw]), "vS")
                        W("vector", "scS", expB_cnt[(hp, u, p)])
                        for kt in mkts:
                            sl, mw, moff = mgeom(kt)
                            E("vector", lambda e, sl=sl, base=base, mw=mw, moff=moff: e.tensor_mul(
                                PT[:, base + 1024 + sl:base + 1024 + sl + mw],
                                PT[:, base + 1024 + sl:base + 1024 + sl + mw],
                                masks[:, moff:moff + mw]), "vS")
                        for kt in mkts:
                            mask_cnt[(hp, u, kt)] = cnt["vS"]

                def emit_lav(kt, hp=hp, u=u, KT=KT, waits=True):
                    o_ = 0 if kt < KT - 4 else 128 * (kt - (KT - 4))
                    cols = 512 - o_
                    base = slotbase(kt)
                    sl = (kt % 2) * 512
                    if waits:
                        W("tensor", "scS", exp_cnt[(hp, u, kt)])
                    if (hp, u, kt) in mask_cnt:
                        W("tensor", "vS", mask_cnt[(hp, u, kt)])
                    st_, sp_ = (kt == 0), (kt == KT - 1)
                    E("tensor", lambda e, sl=sl, base=base, o_=o_, c=cols, st_=st_, sp_=sp_: e.matmul(
                        psLA[0:1, o_:512], onescol[:, 0:1], PT[:, base + sl:base + sl + c],
                        start=st_, stop=sp_, tile_position=(0, 0)), "tS")
                    E("tensor", lambda e, sl=sl, base=base, o_=o_, c=cols, st_=st_, sp_=sp_: e.matmul(
                        psLB[32:33, o_:512], onescol[:, 1:2],
                        PT[:, base + 1024 + sl:base + 1024 + sl + c],
                        start=st_, stop=sp_, tile_position=(0, 32)), "tS")
                    E("tensor", lambda e, sl=sl, base=base, o_=o_, kt=kt, c=cols, st_=st_, sp_=sp_: e.matmul(
                        psOA[0:64, o_:512], V_s[:, kt * 128:kt * 128 + 64],
                        PT[:, base + sl:base + sl + c],
                        start=st_, stop=sp_, tile_position=(0, 0)), "tS")
                    E("tensor", lambda e, sl=sl, base=base, o_=o_, kt=kt, c=cols, st_=st_, sp_=sp_: e.matmul(
                        psOB[64:128, o_:512], V_s[:, kt * 128 + 64:(kt + 1) * 128],
                        PT[:, base + 1024 + sl:base + 1024 + sl + c],
                        start=st_, stop=sp_, tile_position=(0, 64)), "tS")
                    lav_cnt[(hp, u, kt)] = cnt["tS"]

                for p in range(KT // 2):
                    emit_Spair(p)
                    if p >= 1:
                        if p == 1 and evac_wait is not None:
                            W("tensor", "vS", evac_wait)
                        emit_lav(2 * p - 2, waits=False)
                        emit_lav(2 * p - 1, waits=False)
                emit_lav(KT - 2)
                emit_lav(KT - 1)

                last_lav_prev = lav_cnt[(hp, u, KT - 1)]
                W("vector", "tS", last_lav_prev)
                W("vector", "dS", cnt["dS"])
                E("vector", lambda e, hp=hp, u=u: e.tensor_copy(
                    OT[0:64, hp * 2048 + u * 512:hp * 2048 + (u + 1) * 512], psOA[0:64, :]), "vS")
                E("vector", lambda e, hp=hp, u=u: e.tensor_copy(
                    OT[64:128, hp * 2048 + u * 512:hp * 2048 + (u + 1) * 512], psOB[64:128, :]), "vS")
                E("vector", lambda e: e.tensor_copy(ltmp[0:1, :], psLA[0:1, :]), "vS")
                E("vector", lambda e: e.tensor_copy(ltmp[32:33, :], psLB[32:33, :]), "vS")
                if dbg == "B" and hp == 7 and u == 3:
                    E("vector", lambda e: e.tensor_copy(dbgb[:, 0:512], psLA[:, :]), "vS")
                    E("vector", lambda e: e.tensor_copy(dbgb[:, 512:1024], psOA[:, :]), "vS")
                u_evac[(hp, u)] = cnt["vS"]
                W("gpsimd", "vS", u_evac[(hp, u)])
                E("gpsimd", lambda e, hp=hp, u=u: e.dma_start(
                    out=l_all[2 * hp:2 * hp + 1, u * 512:(u + 1) * 512], in_=ltmp[0:1, :]), "dS")
                E("gpsimd", lambda e, hp=hp, u=u: e.dma_start(
                    out=l_all[2 * hp + 1:2 * hp + 2, u * 512:(u + 1) * 512],
                    in_=ltmp[32:33, :]), "dS")
            last_exp_prev_hp = exp_cnt[(hp, 3, KTU[3] - 1)]

        tAtt = cnt["tS"]
        vAtt = u_evac[(7, 3)]

        if dbg not in ("A", "B"):
            emit_ffn_mean(len(ffn_srcs))
            gMean = cnt["scS"]
            W("vector", "scS", gMean)
            W("vector", "tS", tAtt)
            for wi in range(3):
                E("vector", lambda e, wi=wi: e.reduce_sum(
                    wtot[:, wi:wi + 1], wst[:, wi * 64:(wi + 1) * 64], AX.X), "vS")
            W("tensor", "scS", cnt["scS"])
            W("tensor", "vS", cnt["vS"])
            for wi in range(3):
                if wi > 0:
                    W("tensor", "vS", cnt["vS"])
                E("tensor", lambda e, wi=wi: e.matmul(
                    psT1[:, :], onescol_f[:, :], wtot[:, wi:wi + 1], start=True, stop=True), "tS")
                W("vector", "tS", cnt["tS"])
                E("vector", lambda e, wi=wi: e.tensor_copy(wsc[:, wi:wi + 1], psT1[:, :]), "vS")
            W("vector", "vS", cnt["vS"])
            ECH("vector", lambda e: e.tensor_scalar(wsc[:, 3:6], wsc[:, 0:3], 1.0 / (D * INNER), 1e-5,
                                                    ALU.mult, ALU.max))
            ECH("vector", lambda e: e.reciprocal(wsc[:, 0:3], wsc[:, 3:6]))
            ECH("vector", lambda e: e.tensor_scalar(wsc[:, 6:8], wsc[:, 3:5], 1.0 / 127, None, ALU.mult))
            ECH("vector", lambda e: e.tensor_scalar(wsc[:, 5:6], wsc[:, 5:6], 1.0 / 127, None, ALU.mult))
            W("tensor", "vS", cnt["vS"])
            W("tensor", "vS", vAtt)
            for wi in range(3):
                if wi > 0:
                    W("tensor", "vS", cnt["vS"])
                E("tensor", lambda e, wi=wi: e.matmul(psBC0[:, :], onesrow_f[:, :],
                                                      wsc[0:1, wi:wi + 1], start=True, stop=True), "tS")
                W("vector", "tS", cnt["tS"])
                E("vector", lambda e, wi=wi: e.tensor_copy(swv[:, wi:wi + 1], psBC0[:, :]), "vS")
            for wi, scol in enumerate([6, 7, 5]):
                if wi > 0:
                    W("tensor", "vS", cnt["vS"])
                E("tensor", lambda e, scol=scol: e.matmul(psBC1[:, :], onesrow_f[:, :],
                                                          wsc[0:1, scol:scol + 1],
                                                          start=True, stop=True), "tS")
                W("vector", "tS", cnt["tS"])
                E("vector", lambda e, wi=wi: e.tensor_copy(mwv[:, wi:wi + 1], psBC1[:, :]), "vS")
            wsc_ready = cnt["vS"]

            qdst = {0: wgq, 1: wuqq, 2: wdqq}
            qdone = []

            def _qsafe(item):
                wi, idx, _ = item
                return wi == 0 or (wi == 1 and idx < 24)
            ffn_q_order = ([x for x in ffn_srcs if _qsafe(x)]
                           + [x for x in ffn_srcs if not _qsafe(x)])
            qstate = {"i": 0, "tb3": None, "tb3_waited": False}

            def emit_quant(n):
                for _ in range(n):
                    i = qstate["i"]
                    if i >= len(ffn_q_order):
                        return
                    wi, idx, src = ffn_q_order[i]
                    if not _qsafe((wi, idx, src)) and qstate["tb3"] is None:
                        return
                    slot = i % 8
                    lq = "sync" if i % 2 == 0 else "scalar"
                    dsem = "dQ" if i % 2 == 0 else "dQ2"
                    if i < 8:
                        W(lq, "scS", gMean)
                        W(lq, "vS", wsc_ready)
                        W(lq, "tS", tAtt)
                    else:
                        W(lq, "vS", qdone[i - 8])
                    E(lq, lambda e, s=src, sl=slot: e.dma_start(
                        out=fring[:, sl * 512:(sl + 1) * 512], in_=s), dsem)
                    W("vector", dsem, cnt[dsem])
                    if i < 8:
                        W("vector", "vS", wsc_ready)
                    if not _qsafe((wi, idx, src)) and not qstate["tb3_waited"]:
                        W("vector", "tS", qstate["tb3"])
                        qstate["tb3_waited"] = True
                    if wi < 2:
                        dcol = (idx // 8) * INNER + (idx % 8) * 512
                    else:
                        dcol = (idx // 2) * D + (idx % 2) * 512
                    E("vector", lambda e, sl=slot, wi=wi: e.tensor_scalar(
                        fring[:, sl * 512:(sl + 1) * 512], fring[:, sl * 512:(sl + 1) * 512],
                        swv[:, wi:wi + 1], 1.0, ALU.mult, ALU.min))
                    E("vector", lambda e, sl=slot: e.tensor_scalar(
                        fring[:, sl * 512:(sl + 1) * 512], fring[:, sl * 512:(sl + 1) * 512],
                        -1.0, MAGIC, ALU.max, ALU.add))
                    E("vector", lambda e, sl=slot, wi=wi, dcol=dcol: e.tensor_scalar(
                        qdst[wi][:, dcol:dcol + 512], fring[:, sl * 512:(sl + 1) * 512],
                        MAGIC, None, ALU.subtract), "vS")
                    qdone.append(cnt["vS"])
                    qstate["i"] = i + 1
        else:
            def emit_quant(n):
                return

        W("vector", "dS", cnt["dS"])
        E("vector", lambda e: e.reciprocal(l_all[:, :], l_all[:, :]), "vS")
        W("tensor", "scS", cnt["scS"])
        W("tensor", "vS", cnt["vS"])
        wo_ev = {}
        for u in range(4):
            for hp in range(8):
                if (u, hp) != (0, 0):
                    W("tensor", "vS", cnt["vS"])
                E("tensor", lambda e, hp=hp, u=u: e.matmul(
                    psR[:, :], selEs[:, hp * 128:(hp + 1) * 128],
                    l_all[:, u * 512:(u + 1) * 512], start=True, stop=True), "tS")
                W("vector", "tS", cnt["tS"])
                E("vector", lambda e, hp=hp, u=u: e.tensor_mul(
                    OT[:, hp * 2048 + u * 512:hp * 2048 + (u + 1) * 512],
                    OT[:, hp * 2048 + u * 512:hp * 2048 + (u + 1) * 512], psR[:, :]), "vS")
            norm_u = cnt["vS"]
            for tt in range(4):
                ti = u * 4 + tt
                for nbk in range(2):
                    pk2 = u * 8 + tt * 2 + nbk
                    bank = [psW0, psW1, psW2, psW3][pk2 % 4]
                    if pk2 >= 4:
                        W("tensor", "vS", wo_ev[pk2 - 4])
                    elif pk2 >= 2:
                        W("tensor", "vS", vAtt)
                    W("tensor", "vS", norm_u)
                    for hp in range(8):
                        E("tensor", lambda e, b=bank, hp=hp, u=u, tt=tt, nbk=nbk: e.matmul(
                            b[:, :],
                            OT[:, hp * 2048 + u * 512 + tt * 128:hp * 2048 + u * 512 + (tt + 1) * 128],
                            wo_sb[:, hp * D + nbk * 512:hp * D + nbk * 512 + 512],
                            start=(hp == 0), stop=(hp == 7)), "tS" if hp == 7 else None)
                    W("vector", "tS", cnt["tS"])
                    E("vector", lambda e, b=bank, ti=ti, nbk=nbk: e.scalar_tensor_tensor(
                        X[:, ti * D + nbk * 512:ti * D + nbk * 512 + 512], b[:, :], 1.0,
                        X[:, ti * D + nbk * 512:ti * D + nbk * 512 + 512],
                        ALU.mult, ALU.add), "vS")
                    wo_ev[pk2] = cnt["vS"]
                    emit_quant(3)
        tB3 = cnt["tS"]
        vB3 = cnt["vS"]


        if dbg == "B":
            _dump([("X", X[:, :]), ("l_all", l_all[:, :]), ("OT", OT[:, :]),
                   ("biasA", biasA[:, :]), ("biasB", biasB[:, :]),
                   ("KT", KT_s[:, :]), ("QT", QT[:, :]), ("V", V_s[:, :]),
                   ("PT", PT[:, :]), ("qk", qk_sb[:, :]), ("qk2", qk2_sb[:, :]),
                   ("dbgb", dbgb[:, :])])
    if dbg not in ("A", "B"):
        qstate["tb3"] = tB3
        W("gpsimd", "tS", tB3)
        emit_quant(128 - qstate["i"])
        quant_gu = cnt["vS"]
        emit_quant(len(ffn_q_order) - qstate["i"])
        quant_v = cnt["vS"]
        quant_g = cnt["gS"]


        if dbg == "C0":
            _dump([("wgq", wgq[:, :]), ("wuqq", wuqq[:, :]), ("wdqq", wdqq[:, :]),
                   ("wsc", wsc[:, :]), ("swv", swv[:, :]), ("mwv", mwv[:, :]),
                   ("wst", wst[:, :]), ("wtot", wtot[:, :])])
    if dbg not in ("A", "B", "C0"):
        tile_done = {}
        tstate = {}
        qh2ds = {}

        def emit_stageA(ti):
            xap = X[:, ti * D:(ti + 1) * D]
            if ti == 0:
                W("vector", "gS", quant_g)
                W("vector", "tS", tB3)
                W("scalar", "tS", tB3)
            E("vector", lambda e, x=xap: e.reduce_sum(sums[:, 0:1], x, AX.X), "vS")
            W("scalar", "vS", cnt["vS"])
            if "gu_prev" in tstate:
                W("scalar", "vS", tstate["gu_prev"][7][1])
            E("scalar", lambda e, x=xap: e.activation(
                scr4k[:, :], x, AF.Square, accum_out=sumsq[:, 0:1]), "scS")
            W("vector", "scS", cnt["scS"])
            ECH("vector", lambda e: e.tensor_scalar(tmp4a[:, 0:1], sums[:, 0:1], 1.0 / D, None, ALU.mult))
            ECH("vector", lambda e: e.tensor_scalar(tmp4b[:, 0:1], tmp4a[:, 0:1], tmp4a[:, 0:1], -1.0, ALU.mult, ALU.mult))
            ECH("vector", lambda e: e.scalar_tensor_tensor(tmp4b[:, 0:1], sumsq[:, 0:1], 1.0 / D,
                                                         tmp4b[:, 0:1], ALU.mult, ALU.add))
            E("vector", lambda e: e.tensor_scalar(
                rstds[:, 0:1], tmp4b[:, 0:1], LN_EPS, None, ALU.add), "vS")
            W("scalar", "vS", cnt["vS"])
            E("scalar", lambda e: e.activation(rstds[:, 0:1], rstds[:, 0:1], AF.Sqrt), "scS")
            W("vector", "scS", cnt["scS"])
            ECH("vector", lambda e: e.reciprocal(rstds[:, 0:1], rstds[:, 0:1]))
            ECH("vector", lambda e: e.tensor_scalar(negmr[:, 0:1], tmp4a[:, 0:1], rstds[:, 0:1],
                                                  -1.0, ALU.mult, ALU.mult))
            E("vector", lambda e, x=xap: e.tensor_scalar(
                scr4k[:, :], x, rstds[:, 0:1], negmr[:, 0:1], ALU.mult, ALU.add))
            ECH("vector", lambda e: e.tensor_reduce(am_sb[:, 0:1], scr4k[:, :], AX.X, ALU.max,
                                                  apply_absolute_value=True))
            ECH("vector", lambda e: e.tensor_scalar(am_sb[:, 0:1], am_sb[:, 0:1], 1e-5, None, ALU.max))
            ECH("vector", lambda e: e.reciprocal(sa_sb[:, 0:1], am_sb[:, 0:1]))
            ECH("vector", lambda e: e.tensor_scalar(sa_sb[:, 0:1], sa_sb[:, 0:1], 127.0, None, ALU.mult))
            if "gu_prev" in tstate:
                W("vector", "scS", tstate["gu_prev"][7][0])
            E("vector", lambda e: e.tensor_mul(inv_sb[:, 0:1], am_sb[:, 0:1], mwv[:, 0:1]))
            E("vector", lambda e: e.tensor_mul(inv_sb[:, 1:2], am_sb[:, 0:1], mwv[:, 1:2]))
            E("vector", lambda e: e.tensor_scalar(scr4k[:, :], scr4k[:, :], sa_sb[:, 0:1], MAGIC,
                                                  ALU.mult, ALU.add))
            if "qh2_tr" in tstate:
                W("vector", "tS", tstate["qh2_tr"])
            E("vector", lambda e: e.tensor_scalar(qh2_sb[:, :], scr4k[:, :], MAGIC, None,
                                                  ALU.subtract), "vS")
            qh2ds[ti] = cnt["vS"]

        emit_stageA(0)
        for ti in range(NT_OWN):
            W("tensor", "vS", qh2ds[ti])
            if ti == 0:
                W("tensor", "vS", quant_gu)
                W("tensor", "gS", quant_g)
            else:
                W("tensor", "vS", tstate["qh2T_evac"])
                W("tensor", "vS", tstate["ev7_prev"])
            for ck in range(8):
                E("tensor", lambda e, ck=ck: e.transpose(
                    psTrC[:, ck * 128:(ck + 1) * 128], qh2_sb[:, ck * 128:(ck + 1) * 128],
                    ident[:, :]), "tS" if ck == 7 else None)
            tstate["qh2_tr"] = cnt["tS"]
            W("vector", "tS", cnt["tS"])
            if "gate_last" in tstate:
                W("vector", "tS", tstate["gate_last"])
            E("vector", lambda e: e.tensor_copy(qh2T[:, :], psTrC[:, :]), "vS")
            qh2T_done = cnt["vS"]
            tstate["qh2T_evac"] = qh2T_done
            gu_done = []
            for ib in range(8):
                pG = [psG0, psG1][ib % 2]
                pU = [psU0, psU1][ib % 2]
                W("tensor", "vS", qh2T_done)
                if ib >= 2:
                    W("tensor", "vS", gu_done[ib - 2][1])
                    W("tensor", "scS", gu_done[ib - 2][0])
                elif "gu_prev" in tstate:
                    W("tensor", "vS", tstate["gu_prev"][6 + ib][1])
                    W("tensor", "scS", tstate["gu_prev"][6 + ib][0])
                for ck in range(8):
                    E("tensor", lambda e, p=pG, ck=ck, ib=ib: e.matmul(
                        p[:, :], qh2T[:, ck * 128:(ck + 1) * 128],
                        wgq[:, ck * INNER + ib * 512:ck * INNER + ib * 512 + 512],
                        start=(ck == 0), stop=(ck == 7)), "tS" if ck == 7 else None)
                for ck in range(8):
                    E("tensor", lambda e, p=pU, ck=ck, ib=ib: e.matmul(
                        p[:, :], qh2T[:, ck * 128:(ck + 1) * 128],
                        wuqq[:, ck * INNER + ib * 512:ck * INNER + ib * 512 + 512],
                        start=(ck == 0), stop=(ck == 7)), "tS" if ck == 7 else None)
                mm_done = cnt["tS"]
                W("scalar", "tS", mm_done)
                W("scalar", "vS", cnt["vS"])
                E("scalar", lambda e, p=pG: e.activation(
                    gsil[:, :], p[:, :], AF.Silu, scale=inv_sb[:, 0:1]), "scS")
                W("vector", "scS", cnt["scS"])
                if ib == 0 and "qgu_tr_last" in tstate:
                    W("vector", "tS", tstate["qgu_tr_last"])
                E("vector", lambda e, p=pU, ib=ib: e.scalar_tensor_tensor(
                    gu_sb[:, ib * 512:(ib + 1) * 512], p[:, :], inv_sb[:, 1:2], gsil[:, :],
                    ALU.mult, ALU.mult), "vS")
                gu_done.append((cnt["scS"], cnt["vS"]))
                if ib == 7:
                    tstate["gate_last"] = cnt["tS"]
                    tstate["gu_prev"] = list(gu_done)
            ECH("vector", lambda e: e.tensor_reduce(am_sb[:, 1:2], gu_sb[:, :], AX.X, ALU.max,
                                                  apply_absolute_value=True))
            ECH("vector", lambda e: e.tensor_scalar(am_sb[:, 1:2], am_sb[:, 1:2], 1e-5, None, ALU.max))
            ECH("vector", lambda e: e.reciprocal(sa_sb[:, 1:2], am_sb[:, 1:2]))
            ECH("vector", lambda e: e.tensor_scalar(sa_sb[:, 1:2], sa_sb[:, 1:2], 127.0, None, ALU.mult))
            E("vector", lambda e: e.tensor_mul(inv_sb[:, 2:3], am_sb[:, 1:2], mwv[:, 2:3]))
            ev_rounds = []
            tr_cnts = []
            for rr in range(8):
                E("vector", lambda e, rr=rr: e.tensor_scalar(
                    gu_sb[:, rr * 512:(rr + 1) * 512], gu_sb[:, rr * 512:(rr + 1) * 512],
                    sa_sb[:, 1:2], MAGIC, ALU.mult, ALU.add))
                E("vector", lambda e, rr=rr: e.tensor_scalar(
                    gu_sb[:, rr * 512:(rr + 1) * 512], gu_sb[:, rr * 512:(rr + 1) * 512],
                    MAGIC, None, ALU.subtract), "vS")
                q_cnt = cnt["vS"]
                pT = [psTrCf, psTrCf2][rr % 2]
                W("tensor", "vS", q_cnt)
                if rr >= 2:
                    W("tensor", "vS", ev_rounds[rr - 2])
                elif "ev7_prev" in tstate and ti > 0:
                    W("tensor", "vS", tstate["ev7_prev"])
                for ck in range(4):
                    E("tensor", lambda e, pT=pT, rr=rr, ck=ck: e.transpose(
                        pT[:, ck * 128:(ck + 1) * 128],
                        gu_sb[:, rr * 512 + ck * 128:rr * 512 + (ck + 1) * 128],
                        ident_f[:, :]), "tS" if ck == 3 else None)
                tr_cnts.append(cnt["tS"])
                if rr >= 1:
                    prr = rr - 1
                    W("vector", "tS", tr_cnts[prr])
                    if prr == 0 and "down_mm_last" in tstate:
                        W("vector", "tS", tstate["down_mm_last"])
                    E("vector", lambda e, prr=prr: e.tensor_copy(
                        qguT[:, prr * 512:(prr + 1) * 512],
                        [psTrCf, psTrCf2][prr % 2][:, :]), "vS")
                    ev_rounds.append(cnt["vS"])
            W("vector", "tS", tr_cnts[7])
            E("vector", lambda e: e.tensor_copy(qguT[:, 7 * 512:8 * 512], psTrCf2[:, :]), "vS")
            ev_rounds.append(cnt["vS"])
            tstate["qgu_tr_last"] = tr_cnts[7]
            tstate["ev7_prev"] = ev_rounds[7]
            if ti + 1 < NT_OWN:
                emit_stageA(ti + 1)
            for nbk in range(2):
                pD = [psD0, psD1][nbk]
                W("tensor", "vS", ev_rounds[7])
                if ti == 0 and nbk == 0:
                    W("tensor", "vS", quant_v)
                if ti > 0 and nbk == 0:
                    W("tensor", "vS", tstate["dn_prev"])
                for ic in range(32):
                    E("tensor", lambda e, p=pD, ic=ic, nbk=nbk: e.matmul(
                        p[:, :], qguT[:, ic * 128:(ic + 1) * 128],
                        wdqq[:, ic * D + nbk * 512:ic * D + nbk * 512 + 512],
                        start=(ic == 0), stop=(ic == 31)), "tS" if ic == 31 else None)
                W("vector", "tS", cnt["tS"])
                E("vector", lambda e, p=pD, ti=ti, nbk=nbk: e.scalar_tensor_tensor(
                    X[:, ti * D + nbk * 512:ti * D + nbk * 512 + 512], p[:, :], inv_sb[:, 2:3],
                    X[:, ti * D + nbk * 512:ti * D + nbk * 512 + 512], ALU.mult, ALU.add), "vS")
            tile_done[ti] = cnt["vS"]
            tstate["dn_prev"] = cnt["vS"]
            tstate["down_mm_last"] = cnt["tS"]
            W("gpsimd", "vS", tile_done[ti])
            E("gpsimd", lambda e, ti=ti: e.dma_start(
                out=dout[ti * 128:(ti + 1) * 128, :], in_=X[:, ti * D:(ti + 1) * D]), "dS")
            if dbg == "C1" and ti == 0:
                _dump([("qh2", qh2_sb[:, :]), ("qh2T", qh2T[:, :]), ("qguT", qguT[:, :]),
                       ("am", am_sb[:, :]), ("sa", sa_sb[:, :]), ("inv", inv_sb[:, :]),
                       ("X0", X[:, 0:1024]), ("gsil", gsil[:, :]), ("qgu", gu_sb[:, :]),
                       ("rstds", rstds[:, :]), ("negmr", negmr[:, :])])
                break

        W("gpsimd", "dS", cnt["dS"])

    with nc.Block() as block:
        def mk(engname):
            def run(eng):
                for op in OPS[engname]:
                    if op[0] == "w":
                        eng.wait_ge(sems[op[1]], op[2])
                    else:
                        ins = op[1](eng)
                        if op[2] is not None:
                            ins.then_inc(sems[op[2][0]], op[2][1])
            return run
        block.sync(mk("sync"))
        block.tensor(mk("tensor"))
        block.vector(mk("vector"))
        block.scalar(mk("scalar"))
        block.gpsimd(mk("gpsimd"))
    return nc


def _masks_np(half):
    m = np.zeros((128, 2560), dtype=np.float32)
    ii = np.arange(128)[:, None]
    offs03 = [(0, 128), (128, 256), (384, 384), (768, 512)]
    for jp, (o, w) in enumerate(offs03):
        if half == 0:
            m[:, o:o + w] = 1.0
        else:
            jj = np.arange(w)[None, :]
            m[:, o:o + w] = (jj >= 128 * jp + ii).astype(np.float32)
    offs47 = [(1280, 512), (1792, 384), (2176, 256), (2432, 128)]
    for j4, (o, w) in enumerate(offs47):
        if half == 1:
            m[:, o:o + w] = 0.0
        else:
            jj = np.arange(w)[None, :]
            m[:, o:o + w] = (jj >= ii).astype(np.float32)
    return np.asarray(m, dtype=ml_dtypes.bfloat16)


def kernel(L0_attn_x, L0_attn_W_dq, L0_attn_W_uq, L0_attn_W_dkv, L0_attn_W_uk,
           L0_attn_W_uv, L0_attn_W_o, L0_ffn_w_gate, L0_ffn_w_up, L0_ffn_w_down):
    if "nc" not in _CACHE:
        _CACHE["nc"] = _build_nc()
    nc = _CACHE["nc"]

    x = np.ascontiguousarray(np.asarray(L0_attn_x, dtype=np.float32))
    shared = {
        "w_dq": np.ascontiguousarray(np.asarray(L0_attn_W_dq, dtype=np.float32)),
        "w_dkv": np.ascontiguousarray(np.asarray(L0_attn_W_dkv, dtype=np.float32)),
        "w_uq": np.ascontiguousarray(np.asarray(L0_attn_W_uq, dtype=np.float32)),
        "w_uk": np.ascontiguousarray(np.asarray(L0_attn_W_uk, dtype=np.float32)),
        "w_uv": np.ascontiguousarray(np.asarray(L0_attn_W_uv, dtype=np.float32)),
        "w_o": np.ascontiguousarray(np.asarray(L0_attn_W_o, dtype=np.float32)),
        "wg_t": np.ascontiguousarray(np.asarray(L0_ffn_w_gate, dtype=np.float32).T),
        "wu_t": np.ascontiguousarray(np.asarray(L0_ffn_w_up, dtype=np.float32).T),
        "wd_t": np.ascontiguousarray(np.asarray(L0_ffn_w_down, dtype=np.float32).T),
    }
    blocks_of = {0: [1, 3, 5, 7], 1: [0, 2, 4, 6]}
    in_maps = []
    for c in range(8):
        b, h = c >> 1, c & 1
        im = dict(shared)
        im["x_own"] = np.ascontiguousarray(
            np.concatenate([x[b, 512 * k:512 * (k + 1)] for k in blocks_of[h]], axis=0))
        im["x_all"] = np.ascontiguousarray(x[b])
        im["masks"] = _masks_np(h)
        in_maps.append(im)

    res = run_bass_kernel_spmd(nc, in_maps, core_ids=list(range(8)))
    out = np.empty((NBATCH, SEQ, D), dtype=np.float32)
    for c in range(8):
        b, h = c >> 1, c & 1
        r = np.asarray(res.results[c]["out"])
        for j, k in enumerate(blocks_of[h]):
            out[b, 512 * k:512 * (k + 1)] = r[512 * j:512 * (j + 1)]
    return out


# version 21
# speedup vs baseline: 2.0925x; 1.0074x over previous
import numpy as np
import ml_dtypes
import concourse.bass as bass
import concourse.mybir as mybir
from concourse.bass_utils import run_bass_kernel_spmd

F32 = mybir.dt.float32
BF16 = mybir.dt.bfloat16
FP8 = mybir.dt.float8e4
AF = mybir.ActivationFunctionType
ALU = mybir.AluOpType
AX = mybir.AxisListType

D = 1024
SEQ = 4096
NBATCH = 4
DC = 256
INNER = 4096
NT_OWN = 16
NT_ALL = 32
KTU = [8, 16, 24, 32]
MAGIC = 12582912.0
LN_EPS = 1e-5
C_SAFETY = -0.1275

_CACHE = {}


def _build_nc(dbg=None):
    nc = bass.Bass("TRN2", target_bir_lowering=False)

    dx_own = nc.dram_tensor("x_own", [2048, D], F32, kind="ExternalInput").ap()
    dx_all = nc.dram_tensor("x_all", [SEQ, D], F32, kind="ExternalInput").ap()
    dw_dq = nc.dram_tensor("w_dq", [D, DC], F32, kind="ExternalInput").ap()
    dw_dkv = nc.dram_tensor("w_dkv", [D, DC], F32, kind="ExternalInput").ap()
    dw_uq = nc.dram_tensor("w_uq", [DC, D], F32, kind="ExternalInput").ap()
    dw_uk = nc.dram_tensor("w_uk", [DC, D], F32, kind="ExternalInput").ap()
    dw_uv = nc.dram_tensor("w_uv", [DC, D], F32, kind="ExternalInput").ap()
    dw_o = nc.dram_tensor("w_o", [D, D], F32, kind="ExternalInput").ap()
    dwg = nc.dram_tensor("wg_t", [D, INNER], F32, kind="ExternalInput").ap()
    dwu = nc.dram_tensor("wu_t", [D, INNER], F32, kind="ExternalInput").ap()
    dwd = nc.dram_tensor("wd_t", [INNER, D], F32, kind="ExternalInput").ap()
    dmasks = nc.dram_tensor("masks", [128, 2560], BF16, kind="ExternalInput").ap()
    dout = nc.dram_tensor("out", [2048, D], F32, kind="ExternalOutput").ap()

    ident_c = nc.inline_tensor(np.eye(128, dtype=ml_dtypes.bfloat16), "ident_c")
    identf_c = nc.inline_tensor(np.eye(128, dtype=np.float32), "identf_c")
    onescol_c = nc.inline_tensor(np.ones((128, 2), dtype=ml_dtypes.bfloat16), "onescol_c")
    onescol_f_c = nc.inline_tensor(np.ones((128, 1), dtype=np.float32), "onescol_f_c")
    onesrow_f_c = nc.inline_tensor(np.ones((1, 128), dtype=np.float32), "onesrow_f_c")
    bo = np.zeros((128, 2), dtype=np.float32)
    bo[0:64, 0] = 1.0
    bo[64:128, 1] = 1.0
    blockones_c = nc.inline_tensor(bo, "blockones_c")
    sel = np.zeros((2, 256), dtype=np.float32)
    sel[0, 0:128] = 1.0
    sel[1, 128:256] = 1.0
    selAB_c = nc.inline_tensor(sel, "selAB_c")
    selE = np.zeros((16, 8 * 128), dtype=np.float32)
    for hp in range(8):
        selE[2 * hp, hp * 128:hp * 128 + 64] = 1.0
        selE[2 * hp + 1, hp * 128 + 64:hp * 128 + 128] = 1.0
    selE_c = nc.inline_tensor(selE, "selE_c")

    base0 = (nc.sbuf_base + 31) // 32 * 32
    top = nc.sbuf_top // 32 * 32
    cur = [base0]

    sboff = {}

    def sb(name, shape, dtype):
        o = cur[0]
        t = nc.alloc_sbuf_tensor_at(name, shape, dtype, offset=o)
        sboff[name] = o
        nbytes = shape[1] * mybir.dt.size(dtype)
        cur[0] = o + ((nbytes + 31) // 32) * 32
        assert cur[0] <= top, f"sbuf overflow at {name}"
        return t

    X = sb("X", [128, NT_OWN * D], F32)
    ident = sb("ident", [128, 128], BF16)
    ident_f = sb("ident_f", [128, 128], F32)
    onescol = sb("onescol", [128, 2], BF16)
    onescol_f = sb("onescol_f", [128, 1], F32)
    onesrow_f = sb("onesrow_f", [1, 128], F32)
    blockones = sb("blockones", [128, 2], F32)
    selAB = sb("selAB", [2, 256], F32)
    selEs = sb("selEs", [16, 8 * 128], F32)
    masks = sb("masks", [128, 2560], BF16)
    sums = sb("sums", [128, 4], F32)
    sumsq = sb("sumsq", [128, 4], F32)
    rstds = sb("rstds", [128, 4], F32)
    negmr = sb("negmr", [128, 4], F32)
    tmp4a = sb("tmp4a", [128, 4], F32)
    tmp4b = sb("tmp4b", [128, 4], F32)
    biasA = sb("biasA", [128, 1], F32)
    biasB = sb("biasB", [128, 1], F32)
    qk_sb = sb("qk_sb", [2, 2], F32)
    qk2_sb = sb("qk2_sb", [2, 1], F32)
    ltmp = sb("ltmp", [33, 512], F32)
    wst = sb("wst", [128, 192], F32)
    wtot = sb("wtot", [128, 3], F32)
    wsc = sb("wsc", [1, 8], F32)
    swv = sb("swv", [128, 3], F32)
    mwv = sb("mwv", [128, 3], F32)
    am_sb = sb("am_sb", [128, 2], F32)
    sa_sb = sb("sa_sb", [128, 2], F32)
    inv_sb = sb("inv_sb", [128, 3], F32)
    scr4k = sb("scr4k", [128, D], F32)

    R = cur[0]
    avail = top - R
    assert avail >= 126976, f"era region too small: {avail}"

    def at(name, shape, dtype, off, size):
        nbytes = shape[1] * mybir.dt.size(dtype)
        assert nbytes <= size, f"{name}: {nbytes} > {size}"
        assert R + off + nbytes <= top, f"{name} beyond top"
        return nc.alloc_sbuf_tensor_at(name, shape, dtype, offset=R + off)

    xstage = at("xstage", [128, 2 * D], F32, 0, 8192)
    h1tok = at("h1tok", [128, 2 * D], BF16, 8192, 4096)
    h1T = at("h1T", [128, 8 * 512], BF16, 12288, 8192)
    cqT = at("cqT", [128, 2 * 2048], BF16, 20480, 8192)
    ckvT = at("ckvT", [128, 2 * SEQ], BF16, 28672, 16384)
    wdq_sb = at("wdq_sb", [128, 8 * DC], BF16, 45056, 4096)
    wdkv_sb = at("wdkv_sb", [128, 8 * DC], BF16, 49152, 4096)
    fstage = at("fstage", [128, 2 * 512], F32, 53248, 4096)
    wuq_sb = at("wuq_sb", [128, 2 * D], BF16, 57344, 4096)
    wuk_sb = at("wuk_sb", [128, 2 * D], BF16, 61440, 4096)
    wuv_sb = at("wuv_sb", [128, 2 * D], BF16, 65536, 4096)
    wo_sb = at("wo_sb", [128, 8 * D], BF16, 69632, 16384)
    OT = at("OT", [128, 8 * 2048], BF16, 86016, 32768)
    l_all = at("l_all", [16, 2048], F32, 118784, 8192)
    wf_dq = at("wf_dq", [128, 8 * DC], F32, 86016, 8192)
    wf_dkv = at("wf_dkv", [128, 8 * DC], F32, 94208, 8192)
    wf_uq = at("wf_uq", [128, 2 * D], F32, 102400, 8192)
    wf_uk = at("wf_uk", [128, 2 * D], F32, 110592, 8192)
    wf_uv = at("wf_uv", [128, 2 * D], F32, 118784, 8192)
    QT = at("QT", [128, 2048], BF16, 0, 4096)
    PT = at("PT", [128, 8 * 512], BF16, 4096, 8192)
    sqscr = at("sqscr", [128, 2048], BF16, 8192, 4096)
    KT_s = at("KT_s", [128, SEQ], BF16, 12288, 8192)
    V_s = at("V_s", [128, SEQ], BF16, 45056, 8192)
    wgq = at("wgq", [128, 8 * INNER], FP8, 20480, 32768)
    wuqq = at("wuqq", [128, 8 * INNER], FP8, 57344, 32768)
    wdqq = at("wdqq", [128, 32 * D], FP8, 90112, 32768)
    gu_sb = at("gu_sb", [128, INNER], F32, 0, 16384)
    fring = at("fring", [128, 8 * 512], F32, 0, 16384)
    qh2T = at("qh2T", [128, 8 * 128], BF16, 122880, 2048)
    qh2_sb = at("qh2_sb", [128, D], BF16, 124928, 2048)
    qguT = at("qguT", [128, 32 * 128], BF16, 0, 8192)
    gsil = nc.alloc_sbuf_tensor_at("gsil", [128, 512], F32, offset=sboff["scr4k"])
    dbgb = at("dbgb", [128, 1024], F32, 20480, 32768) if dbg else None
    assert R + 126976 <= top

    def ps(name, shape, dtype, bank):
        nc.psum_base = bank
        return nc.alloc_psum_tensor(name, shape, dtype)

    psTrA = ps("psTrA", [128, 1024], BF16, 0)
    psP0 = ps("psP0", [128, 512], F32, 1)
    psP1 = ps("psP1", [128, 512], F32, 2)
    psSA = ps("psSA", [128, 1024], F32, 0)
    psSB = ps("psSB", [128, 1024], F32, 2)
    psSA0 = ps("psSA0", [128, 512], F32, 0)
    psSA1 = ps("psSA1", [128, 512], F32, 1)
    psSB0 = ps("psSB0", [128, 512], F32, 2)
    psSB1 = ps("psSB1", [128, 512], F32, 3)
    psLA = ps("psLA", [128, 512], F32, 4)
    psOA = ps("psOA", [128, 512], F32, 5)
    psLB = ps("psLB", [128, 512], F32, 6)
    psOB = ps("psOB", [128, 512], F32, 7)
    psN0 = ps("psN0", [2, 1], F32, 4)
    psN1 = ps("psN1", [2, 1], F32, 5)
    psBC0 = ps("psBC0", [128, 1], F32, 6)
    psBC1 = ps("psBC1", [128, 1], F32, 7)
    psW0 = ps("psW0", [128, 512], F32, 0)
    psW1 = ps("psW1", [128, 512], F32, 1)
    psW2 = ps("psW2", [128, 512], F32, 4)
    psW3 = ps("psW3", [128, 512], F32, 5)
    psR = ps("psR", [128, 512], F32, 2)
    psT1 = ps("psT1", [1, 1], F32, 3)
    psTrC = ps("psTrC", [128, 1024], BF16, 7)
    psTrCf = ps("psTrCf", [128, 512], F32, 0)
    psTrCf2 = ps("psTrCf2", [128, 512], F32, 7)
    psG0 = ps("psG0", [128, 512], F32, 1)
    psG1 = ps("psG1", [128, 512], F32, 2)
    psU0 = ps("psU0", [128, 512], F32, 3)
    psU1 = ps("psU1", [128, 512], F32, 4)
    psD0 = ps("psD0", [128, 512], F32, 5)
    psD1 = ps("psD1", [128, 512], F32, 6)
    nc.psum_base = 8

    DMA_SEMS = ("dL", "dS", "dF0", "dF1", "dW0", "dW1", "dX0", "dX1",
                "dQs0", "dQs1", "dQs2", "dQs3", "dQs4", "dQs5", "dQs6", "dQs7")
    sem_names = list(DMA_SEMS) + ["tS", "vS", "scS", "gS"]
    sems = {n: nc.alloc_semaphore(n) for n in sem_names}
    cnt = {n: 0 for n in sem_names}
    OPS = {"sync": [], "tensor": [], "vector": [], "scalar": [], "gpsimd": []}

    def W(eng, sem, val):
        if val > 0:
            OPS[eng].append(("w", sem, val))

    def E(eng, fn, inc=None):
        n = 16 if inc in DMA_SEMS else 1
        OPS[eng].append(("e", fn, (inc, n) if inc else None))
        if inc:
            cnt[inc] += n

    _SELF = {"vector": "vS", "scalar": "scS", "gpsimd": "gS", "tensor": "tS"}

    def ECH(eng, fn):
        s = _SELF[eng]
        E(eng, fn, s)
        W(eng, s, cnt[s])

    for dst, src in [(ident, ident_c), (ident_f, identf_c), (onescol, onescol_c), (onescol_f, onescol_f_c),
                     (onesrow_f, onesrow_f_c), (blockones, blockones_c),
                     (selAB, selAB_c), (selEs, selE_c)]:
        E("sync", lambda e, d=dst, s=src: e.dma_start(out=d[:, :], in_=s.ap()), "dL")
    E("sync", lambda e: e.dma_start(out=masks[:, :], in_=dmasks), "dL")
    for t in range(NT_OWN):
        E("sync", lambda e, t=t: e.dma_start(out=X[:, t * D:(t + 1) * D],
                                             in_=dx_own[t * 128:(t + 1) * 128, :]), "dL")
    for ck in range(8):
        E("sync", lambda e, ck=ck: e.dma_start(out=wf_dq[:, ck * DC:(ck + 1) * DC],
                                               in_=dw_dq[ck * 128:(ck + 1) * 128, :]), "dL")
        E("sync", lambda e, ck=ck: e.dma_start(out=wf_dkv[:, ck * DC:(ck + 1) * DC],
                                               in_=dw_dkv[ck * 128:(ck + 1) * 128, :]), "dL")
    for cc in range(2):
        E("sync", lambda e, cc=cc: e.dma_start(out=wf_uq[:, cc * D:(cc + 1) * D],
                                               in_=dw_uq[cc * 128:(cc + 1) * 128, :]), "dL")
        E("sync", lambda e, cc=cc: e.dma_start(out=wf_uk[:, cc * D:(cc + 1) * D],
                                               in_=dw_uk[cc * 128:(cc + 1) * 128, :]), "dL")
        E("sync", lambda e, cc=cc: e.dma_start(out=wf_uv[:, cc * D:(cc + 1) * D],
                                               in_=dw_uv[cc * 128:(cc + 1) * 128, :]), "dL")
    dl_stage5 = cnt["dL"]
    W("vector", "dL", dl_stage5)
    E("vector", lambda e: e.tensor_copy(wdq_sb[:, :], wf_dq[:, :]), "vS")
    E("vector", lambda e: e.tensor_copy(wdkv_sb[:, :], wf_dkv[:, :]), "vS")
    E("vector", lambda e: e.tensor_copy(wuq_sb[:, :], wf_uq[:, :]), "vS")
    E("vector", lambda e: e.tensor_copy(wuk_sb[:, :], wf_uk[:, :]), "vS")
    E("vector", lambda e: e.tensor_copy(wuv_sb[:, :], wf_uv[:, :]), "vS")
    wo_cast = []
    for pc in range(16):
        ck, hh = pc // 2, (pc % 2) * 512
        if pc >= 2:
            W("sync", "vS", wo_cast[pc - 2])
        dsem = "dW0" if pc % 2 == 0 else "dW1"
        E("sync", lambda e, ck=ck, hh=hh, sl=pc % 2: e.dma_start(
            out=fstage[:, sl * 512:(sl + 1) * 512],
            in_=dw_o[ck * 128:(ck + 1) * 128, hh:hh + 512]), dsem)
        W("vector", dsem, cnt[dsem])
        E("vector", lambda e, ck=ck, hh=hh, sl=pc % 2: e.tensor_copy(
            wo_sb[:, ck * D + hh:ck * D + hh + 512], fstage[:, sl * 512:(sl + 1) * 512]), "vS")
        wo_cast.append(cnt["vS"])
    wcast_all = cnt["vS"]
    dl_head = cnt["dL"]

    h1evac = {}
    projevac = []

    def emit_projs(g):
        own_ = g >= 8
        wtile = wdq_sb if own_ else wdkv_sb
        dstT = cqT if own_ else ckvT
        gcol = (g - 8) * 512 if own_ else g * 512
        span = 2048 if own_ else SEQ
        for m in range(2):
            npr = len(projevac)
            bank = [psP0, psP1][npr % 2]
            if npr >= 2:
                W("tensor", "vS", projevac[npr - 2])
            W("tensor", "vS", h1evac[(g, 3)])
            W("tensor", "vS", wcast_all)
            for ck in range(8):
                E("tensor", lambda e, b=bank, wt=wtile, ck=ck, m=m: e.matmul(
                    b[:, :], wt[:, ck * DC + m * 128:ck * DC + (m + 1) * 128],
                    h1T[:, ck * 512:(ck + 1) * 512],
                    start=(ck == 0), stop=(ck == 7)), "tS" if ck == 7 else None)
            W("vector", "tS", cnt["tS"])
            E("vector", lambda e, b=bank, d=dstT, m=m, gc=gcol, sp=span: e.tensor_copy(
                d[:, m * sp + gc:m * sp + gc + 512], b[:, :]), "vS")
            projevac.append(cnt["vS"])
    xload = {}
    h1done = {}
    sqdone = {}
    tr_done = {}
    chain_done = {}
    for g in range(12):
        own = g >= 8
        for jp in range(2):
            j0, j1 = 2 * jp, 2 * jp + 1
            tis, xaps = [], []
            for j in (j0, j1):
                ti = g * 4 + j
                if not own:
                    prev = ti - 2
                    if prev >= 0 and prev in h1done:
                        W("sync", "vS", h1done[prev])
                        W("sync", "scS", sqdone[prev])
                    xsem = "dX0" if ti % 2 == 0 else "dX1"
                    E("sync", lambda e, ti=ti, sl=ti % 2: e.dma_start(
                        out=xstage[:, sl * D:(sl + 1) * D],
                        in_=dx_all[ti * 128:(ti + 1) * 128, :]), xsem)
                    xload[ti] = (xsem, cnt[xsem])
                    xap = xstage[:, (ti % 2) * D:((ti % 2) + 1) * D]
                else:
                    xload[ti] = ("dL", dl_head)
                    xap = X[:, (ti - 32) * D:(ti - 31) * D]
                tis.append(ti)
                xaps.append(xap)
            W("vector", *xload[tis[0]])
            E("vector", lambda e, x=xaps[0], j=j0: e.reduce_sum(sums[:, j:j + 1], x, AX.X), "vS")
            W("vector", *xload[tis[1]])
            ECH("vector", lambda e, x=xaps[1], j=j1: e.reduce_sum(sums[:, j:j + 1], x, AX.X))
            W("scalar", *xload[tis[0]])
            if tis[0] >= 1:
                W("scalar", "vS", h1done.get(tis[0] - 1, 0))
            E("scalar", lambda e, x=xaps[0], j=j0: e.activation(
                scr4k[:, :], x, AF.Square, accum_out=sumsq[:, j:j + 1]), "scS")
            W("scalar", *xload[tis[1]])
            E("scalar", lambda e, x=xaps[1], j=j1: e.activation(
                scr4k[:, :], x, AF.Square, accum_out=sumsq[:, j:j + 1]), "scS")
            sqdone[tis[0]] = sqdone[tis[1]] = cnt["scS"]
            W("vector", "scS", sqdone[tis[1]])
            ECH("vector", lambda e, j=j0: e.tensor_scalar(
                tmp4a[:, j:j + 2], sums[:, j:j + 2], 1.0 / D, None, ALU.mult))
            ECH("vector", lambda e, j=j0: e.tensor_mul(
                tmp4b[:, j:j + 2], tmp4a[:, j:j + 2], tmp4a[:, j:j + 2]))
            ECH("vector", lambda e, j=j0: e.tensor_scalar(
                tmp4b[:, j:j + 2], tmp4b[:, j:j + 2], -1.0, None, ALU.mult))
            ECH("vector", lambda e, j=j0: e.scalar_tensor_tensor(
                tmp4b[:, j:j + 2], sumsq[:, j:j + 2], 1.0 / D,
                tmp4b[:, j:j + 2], ALU.mult, ALU.add))
            E("vector", lambda e, j=j0: e.tensor_scalar(
                rstds[:, j:j + 2], tmp4b[:, j:j + 2], LN_EPS, None, ALU.add), "vS")
            W("scalar", "vS", cnt["vS"])
            E("scalar", lambda e, j=j0: e.activation(
                rstds[:, j:j + 2], rstds[:, j:j + 2], AF.Sqrt), "scS")
            W("vector", "scS", cnt["scS"])
            ECH("vector", lambda e, j=j0: e.reciprocal(rstds[:, j:j + 2], rstds[:, j:j + 2]))
            ECH("vector", lambda e, j=j0: e.scalar_tensor_tensor(
                negmr[:, j:j + 2], tmp4a[:, j:j + 2], -1.0, rstds[:, j:j + 2],
                ALU.mult, ALU.mult))
            if jp == 0 and g >= 1:
                emit_projs(g - 1)
            for j, ti, xap in ((j0, tis[0], xaps[0]), (j1, tis[1], xaps[1])):
                sl = j % 2
                slot_prev = (g, j - 2) if j >= 2 else (g - 1, j + 2)
                if slot_prev in tr_done:
                    W("vector", "tS", tr_done[slot_prev])
                E("vector", lambda e, x=xap, j=j, sl=sl: e.tensor_scalar(
                    h1tok[:, sl * D:(sl + 1) * D], x, rstds[:, j:j + 1], negmr[:, j:j + 1],
                    ALU.mult, ALU.add), "vS")
                h1done[ti] = cnt["vS"]
                W("tensor", "vS", h1done[ti])
                tr_prev = (g, j - 1) if j >= 1 else (g - 1, 3)
                if tr_prev in h1evac:
                    W("tensor", "vS", h1evac[tr_prev])
                for ck in range(8):
                    E("tensor", lambda e, sl=sl, ck=ck: e.transpose(
                        psTrA[:, ck * 128:(ck + 1) * 128],
                        h1tok[:, sl * D + ck * 128:sl * D + (ck + 1) * 128], ident[:, :]),
                        "tS" if ck == 7 else None)
                tr_done[(g, j)] = cnt["tS"]
                W("vector", "tS", tr_done[(g, j)])
                E("vector", lambda e, j=j: e.tensor_copy(
                    h1T[:, :].rearrange("p (c t) -> p c t", c=8)[:, :, j * 128:(j + 1) * 128],
                    psTrA[:, :].rearrange("p (c t) -> p c t", c=8)), "vS")
                h1evac[(g, j)] = cnt["vS"]
    emit_projs(11)
    phaseA_tS = cnt["tS"]
    phaseA_vS = cnt["vS"]

    def _dump(pairs):
        for nm, ap in pairs:
            dt_ = ap.dtype
            dten = nc.dram_tensor("dump_" + nm, list(ap.shape), dt_,
                                  kind="ExternalOutput").ap()
            W("gpsimd", "vS", cnt["vS"])
            W("gpsimd", "scS", cnt["scS"])
            W("gpsimd", "tS", cnt["tS"])
            E("gpsimd", lambda e, d=dten, s=ap: e.dma_start(out=d, in_=s), "dS")
        W("gpsimd", "dS", cnt["dS"])

    if dbg == "A":
        _dump([("ckvT", ckvT[:, :]), ("cqT", cqT[:, :]), ("wdq", wdq_sb[:, :]),
               ("wuk", wuk_sb[:, :]), ("wo", wo_sb[:, :]), ("X", X[:, 0:2048]),
               ("h1T", h1T[:, :]), ("h1tok", h1tok[:, :]), ("sums", sums[:, :]),
               ("sumsq", sumsq[:, :]), ("rstds", rstds[:, :]), ("negmr", negmr[:, :]),
               ("tmp4a", tmp4a[:, :]), ("tmp4b", tmp4b[:, :])])

    ffn_srcs = []
    for wi, dw in enumerate([dwg, dwu, dwd]):
        nrow = D if wi < 2 else INNER
        ncolw = INNER if wi < 2 else D
        idx = 0
        for ck in range(nrow // 128):
            for hh in range(0, ncolw, 512):
                ffn_srcs.append((wi, idx, dw[ck * 128:(ck + 1) * 128, hh:hh + 512]))
                idx += 1
    assert len(ffn_srcs) == 192
    ffn_ptr = [0]
    ffn_gdone = []

    def emit_ffn_mean(nsteps):
        for _ in range(nsteps):
            i = ffn_ptr[0]
            if i >= len(ffn_srcs):
                return
            wi, idx, src = ffn_srcs[i]
            slot = i % 2
            if i < 2:
                W("gpsimd", "vS", wcast_all)
            if i >= 2:
                W("gpsimd", "scS", ffn_gdone[i - 2])
            fsem = "dF0" if slot == 0 else "dF1"
            E("gpsimd", lambda e, s=src, sl=slot: e.dma_start(
                out=fstage[:, sl * 512:(sl + 1) * 512], in_=s), fsem)
            W("scalar", fsem, cnt[fsem])
            col = wi * 64 + idx
            E("scalar", lambda e, sl=slot, col=col: e.activation(
                fstage[:, sl * 512:(sl + 1) * 512], fstage[:, sl * 512:(sl + 1) * 512], AF.Abs,
                accum_out=wst[:, col:col + 1]), "scS")
            ffn_gdone.append(cnt["scS"])
            ffn_ptr[0] += 1

    if dbg != "A":
        u_evac = {}
        exp_cnt = {}
        expA_cnt = {}
        expB_cnt = {}
        mask_cnt = {}
        lav_cnt = {}
        s_cnt = {}
        last_exp_prev_hp = 0
        last_lav_prev = 0

        for hp in range(8):
            W("tensor", "scS", last_exp_prev_hp)
            if hp == 0:
                W("tensor", "vS", phaseA_vS)
            else:
                W("tensor", "vS", u_evac[(hp - 1, 3)])
            bev = []
            for nb in range(8):
                bank = [psSA0, psSA1][nb % 2]
                if nb >= 2:
                    W("tensor", "vS", bev[nb - 2])
                for cc in range(2):
                    E("tensor", lambda e, b=bank, cc=cc, hp=hp, nb=nb: e.matmul(
                        b[:, :], wuk_sb[:, cc * D + hp * 128:cc * D + (hp + 1) * 128],
                        ckvT[:, cc * SEQ + nb * 512:cc * SEQ + nb * 512 + 512],
                        start=(cc == 0), stop=(cc == 1)), "tS" if cc == 1 else None)
                W("vector", "tS", cnt["tS"])
                E("vector", lambda e, b=bank, nb=nb: e.tensor_copy(
                    KT_s[:, nb * 512:(nb + 1) * 512], b[:, :]), "vS")
                bev.append(cnt["vS"])
            for nb in range(4):
                bank = [psSB0, psSB1][nb % 2]
                if nb >= 2:
                    W("tensor", "vS", bev[8 + nb - 2])
                for cc in range(2):
                    E("tensor", lambda e, b=bank, cc=cc, hp=hp, nb=nb: e.matmul(
                        b[:, :], wuq_sb[:, cc * D + hp * 128:cc * D + (hp + 1) * 128],
                        cqT[:, cc * 2048 + nb * 512:cc * 2048 + nb * 512 + 512],
                        start=(cc == 0), stop=(cc == 1)), "tS" if cc == 1 else None)
                W("vector", "tS", cnt["tS"])
                E("vector", lambda e, b=bank, nb=nb: e.tensor_copy(
                    QT[:, nb * 512:(nb + 1) * 512], b[:, :]), "vS")
                bev.append(cnt["vS"])
            for kt in range(32):
                bank = [psLA, psOA][kt % 2]
                if kt >= 2:
                    W("tensor", "vS", bev[12 + kt - 2])
                for cc in range(2):
                    E("tensor", lambda e, b=bank, cc=cc, hp=hp, kt=kt: e.matmul(
                        b[:, 0:128], ckvT[:, cc * SEQ + kt * 128:cc * SEQ + (kt + 1) * 128],
                        wuv_sb[:, cc * D + hp * 128:cc * D + (hp + 1) * 128],
                        start=(cc == 0), stop=(cc == 1)), "tS" if cc == 1 else None)
                W("vector", "tS", cnt["tS"])
                E("vector", lambda e, b=bank, kt=kt: e.tensor_copy(
                    V_s[:, kt * 128:(kt + 1) * 128], b[:, 0:128]), "vS")
                bev.append(cnt["vS"])
            kv_built = cnt["vS"]

            emit_ffn_mean(24)

            W("scalar", "vS", kv_built)
            W("scalar", "tS", cnt["tS"])
            E("scalar", lambda e: e.activation(sqscr[:, :], QT[:, :], AF.Square), "scS")
            W("vector", "scS", cnt["scS"])
            E("vector", lambda e: e.reduce_max(tmp4a[:, 0:1], sqscr[:, :], AX.X), "vS")
            E("scalar", lambda e: e.activation(sqscr[:, :], KT_s[:, 0:2048], AF.Square), "scS")
            W("vector", "scS", cnt["scS"])
            E("vector", lambda e: e.reduce_max(tmp4a[:, 1:2], sqscr[:, :], AX.X), "vS")
            W("scalar", "vS", cnt["vS"])
            E("scalar", lambda e: e.activation(sqscr[:, :], KT_s[:, 2048:4096], AF.Square), "scS")
            W("vector", "scS", cnt["scS"])
            ECH("vector", lambda e: e.reduce_max(tmp4b[:, 0:1], sqscr[:, :], AX.X))
            E("vector", lambda e: e.tensor_max(tmp4a[:, 1:2], tmp4a[:, 1:2], tmp4b[:, 0:1]), "vS")
            W("tensor", "vS", cnt["vS"])
            E("tensor", lambda e: e.matmul(psN0[:, :], blockones[:, :], tmp4a[:, 0:1],
                                           start=True, stop=True), "tS")
            E("tensor", lambda e: e.matmul(psN1[:, :], blockones[:, :], tmp4a[:, 1:2],
                                           start=True, stop=True), "tS")
            W("vector", "tS", cnt["tS"])
            E("vector", lambda e: e.tensor_copy(qk_sb[:, 0:1], psN0[:, :]), "vS")
            E("vector", lambda e: e.tensor_copy(qk_sb[:, 1:2], psN1[:, :]), "vS")
            W("vector", "vS", cnt["vS"])
            W("scalar", "vS", cnt["vS"])
            E("scalar", lambda e: e.activation(qk_sb[:, :], qk_sb[:, :], AF.Sqrt), "scS")
            W("vector", "scS", cnt["scS"])
            ECH("vector", lambda e: e.tensor_scalar(qk2_sb[:, :], qk_sb[:, 0:1], qk_sb[:, 1:2],
                                                    C_SAFETY, ALU.mult, ALU.mult))
            W("tensor", "vS", cnt["vS"])
            E("tensor", lambda e: e.matmul(psBC0[:, :], selAB[:, 0:128], qk2_sb[:, :],
                                           start=True, stop=True), "tS")
            E("tensor", lambda e: e.matmul(psBC1[:, :], selAB[:, 128:256], qk2_sb[:, :],
                                           start=True, stop=True), "tS")
            W("vector", "tS", cnt["tS"])
            E("vector", lambda e: e.tensor_copy(biasA[:, :], psBC0[:, :]), "vS")
            E("vector", lambda e: e.tensor_copy(biasB[:, :], psBC1[:, :]), "vS")
            bias_ready = cnt["vS"]

            for u in range(4):
                KT = KTU[u]
                if u == 0:
                    W("tensor", "vS", bias_ready)
                    evac_wait = None
                else:
                    evac_wait = u_evac[(hp, u - 1)]

                def slotbase(kt):
                    return ((kt // 2) % 2) * 2048

                def emit_Spair(p, hp=hp, u=u, KT=KT, bias_ready=bias_ready,
                               last_lav_prev=last_lav_prev):
                    if p >= 1:
                        W("tensor", "scS", expA_cnt[(hp, u, p - 1)])
                    elif u > 0:
                        W("tensor", "scS", exp_cnt[(hp, u - 1, KTU[u - 1] - 1)])
                    else:
                        W("tensor", "vS", bias_ready)
                    for kt in (2 * p, 2 * p + 1):
                        o_ = 0 if kt < KT - 4 else 128 * (kt - (KT - 4))
                        sl = (kt % 2) * 512
                        E("tensor", lambda e, o_=o_, kt=kt, sl=sl: e.matmul(
                            psSA[:, sl + o_:sl + 512], KT_s[0:64, kt * 128:(kt + 1) * 128],
                            QT[0:64, u * 512 + o_:(u + 1) * 512],
                            start=True, stop=True, tile_position=(0, 0)), "tS")
                    s_cntA = cnt["tS"]
                    if p >= 1:
                        W("tensor", "scS", expB_cnt[(hp, u, p - 1)])
                    for kt in (2 * p, 2 * p + 1):
                        o_ = 0 if kt < KT - 4 else 128 * (kt - (KT - 4))
                        sl = (kt % 2) * 512
                        E("tensor", lambda e, o_=o_, kt=kt, sl=sl: e.matmul(
                            psSB[:, sl + o_:sl + 512], KT_s[64:128, kt * 128:(kt + 1) * 128],
                            QT[64:128, u * 512 + o_:(u + 1) * 512],
                            start=True, stop=True, tile_position=(64, 0)), "tS")
                    s_cnt[(hp, u, p)] = cnt["tS"]
                    s_cntB = cnt["tS"]
                    base = slotbase(2 * p)
                    W("scalar", "tS", s_cntA)
                    if p >= 2:
                        W("scalar", "tS", lav_cnt[(hp, u, 2 * p - 3)])
                    else:
                        W("scalar", "tS", last_lav_prev)
                    if p == 0:
                        W("scalar", "vS", bias_ready)
                    if 2 * p >= KT - 4:
                        for kt in (2 * p, 2 * p + 1):
                            o_ = 128 * (kt - (KT - 4))
                            c = 512 - o_
                            sl = (kt % 2) * 512
                            E("scalar", lambda e, o_=o_, c=c, sl=sl, base=base: e.activation(
                                PT[:, base + sl:base + sl + c], psSA[:, sl + o_:sl + 512],
                                AF.Exp, bias=biasA[:, :], scale=0.125), "scS")
                        expA_cnt[(hp, u, p)] = cnt["scS"]
                        W("scalar", "tS", s_cntB)
                        for kt in (2 * p, 2 * p + 1):
                            o_ = 128 * (kt - (KT - 4))
                            c = 512 - o_
                            sl = (kt % 2) * 512
                            E("scalar", lambda e, o_=o_, c=c, sl=sl, base=base: e.activation(
                                PT[:, base + 1024 + sl:base + 1024 + sl + c],
                                psSB[:, sl + o_:sl + 512],
                                AF.Exp, bias=biasB[:, :], scale=0.125), "scS")
                    else:
                        E("scalar", lambda e, base=base: e.activation(
                            PT[:, base:base + 1024], psSA[:, :], AF.Exp,
                            bias=biasA[:, :], scale=0.125), "scS")
                        expA_cnt[(hp, u, p)] = cnt["scS"]
                        W("scalar", "tS", s_cntB)
                        E("scalar", lambda e, base=base: e.activation(
                            PT[:, base + 1024:base + 2048], psSB[:, :], AF.Exp,
                            bias=biasB[:, :], scale=0.125), "scS")
                    expB_cnt[(hp, u, p)] = cnt["scS"]
                    exp_cnt[(hp, u, 2 * p)] = cnt["scS"]
                    exp_cnt[(hp, u, 2 * p + 1)] = cnt["scS"]
                    mkts = [kt for kt in (2 * p, 2 * p + 1) if kt >= KT - 8]
                    if mkts:
                        def mgeom(kt):
                            o_ = 0 if kt < KT - 4 else 128 * (kt - (KT - 4))
                            sl = (kt % 2) * 512
                            jp = kt - (KT - 8)
                            if jp < 4:
                                mw, moff = 128 * (jp + 1), [0, 128, 384, 768][jp]
                            else:
                                mw = 512 - o_
                                moff = [1280, 1792, 2176, 2432][jp - 4]
                            return sl, mw, moff
                        W("vector", "scS", expA_cnt[(hp, u, p)])
                        for kt in mkts:
                            sl, mw, moff = mgeom(kt)
                            E("vector", lambda e, sl=sl, base=base, mw=mw, moff=moff: e.tensor_mul(
                                PT[:, base + sl:base + sl + mw],
                                PT[:, base + sl:base + sl + mw],
                                masks[:, moff:moff + mw]), "vS")
                        W("vector", "scS", expB_cnt[(hp, u, p)])
                        for kt in mkts:
                            sl, mw, moff = mgeom(kt)
                            E("vector", lambda e, sl=sl, base=base, mw=mw, moff=moff: e.tensor_mul(
                                PT[:, base + 1024 + sl:base + 1024 + sl + mw],
                                PT[:, base + 1024 + sl:base + 1024 + sl + mw],
                                masks[:, moff:moff + mw]), "vS")
                        for kt in mkts:
                            mask_cnt[(hp, u, kt)] = cnt["vS"]

                def emit_lav(kt, hp=hp, u=u, KT=KT, waits=True):
                    o_ = 0 if kt < KT - 4 else 128 * (kt - (KT - 4))
                    cols = 512 - o_
                    base = slotbase(kt)
                    sl = (kt % 2) * 512
                    if waits:
                        W("tensor", "scS", exp_cnt[(hp, u, kt)])
                    if (hp, u, kt) in mask_cnt:
                        W("tensor", "vS", mask_cnt[(hp, u, kt)])
                    st_, sp_ = (kt == 0), (kt == KT - 1)
                    E("tensor", lambda e, sl=sl, base=base, o_=o_, c=cols, st_=st_, sp_=sp_: e.matmul(
                        psLA[0:1, o_:512], onescol[:, 0:1], PT[:, base + sl:base + sl + c],
                        start=st_, stop=sp_, tile_position=(0, 0)), "tS")
                    E("tensor", lambda e, sl=sl, base=base, o_=o_, c=cols, st_=st_, sp_=sp_: e.matmul(
                        psLB[32:33, o_:512], onescol[:, 1:2],
                        PT[:, base + 1024 + sl:base + 1024 + sl + c],
                        start=st_, stop=sp_, tile_position=(0, 32)), "tS")
                    E("tensor", lambda e, sl=sl, base=base, o_=o_, kt=kt, c=cols, st_=st_, sp_=sp_: e.matmul(
                        psOA[0:64, o_:512], V_s[:, kt * 128:kt * 128 + 64],
                        PT[:, base + sl:base + sl + c],
                        start=st_, stop=sp_, tile_position=(0, 0)), "tS")
                    E("tensor", lambda e, sl=sl, base=base, o_=o_, kt=kt, c=cols, st_=st_, sp_=sp_: e.matmul(
                        psOB[64:128, o_:512], V_s[:, kt * 128 + 64:(kt + 1) * 128],
                        PT[:, base + 1024 + sl:base + 1024 + sl + c],
                        start=st_, stop=sp_, tile_position=(0, 64)), "tS")
                    lav_cnt[(hp, u, kt)] = cnt["tS"]

                for p in range(KT // 2):
                    emit_Spair(p)
                    if p >= 1:
                        if p == 1 and evac_wait is not None:
                            W("tensor", "vS", evac_wait)
                        emit_lav(2 * p - 2, waits=False)
                        emit_lav(2 * p - 1, waits=False)
                emit_lav(KT - 2)
                emit_lav(KT - 1)

                last_lav_prev = lav_cnt[(hp, u, KT - 1)]
                W("vector", "tS", last_lav_prev)
                W("vector", "dS", cnt["dS"])
                E("vector", lambda e, hp=hp, u=u: e.tensor_copy(
                    OT[0:64, hp * 2048 + u * 512:hp * 2048 + (u + 1) * 512], psOA[0:64, :]), "vS")
                E("vector", lambda e, hp=hp, u=u: e.tensor_copy(
                    OT[64:128, hp * 2048 + u * 512:hp * 2048 + (u + 1) * 512], psOB[64:128, :]), "vS")
                E("vector", lambda e: e.tensor_copy(ltmp[0:1, :], psLA[0:1, :]), "vS")
                E("vector", lambda e: e.tensor_copy(ltmp[32:33, :], psLB[32:33, :]), "vS")
                if dbg == "B" and hp == 7 and u == 3:
                    E("vector", lambda e: e.tensor_copy(dbgb[:, 0:512], psLA[:, :]), "vS")
                    E("vector", lambda e: e.tensor_copy(dbgb[:, 512:1024], psOA[:, :]), "vS")
                u_evac[(hp, u)] = cnt["vS"]
                W("gpsimd", "vS", u_evac[(hp, u)])
                E("gpsimd", lambda e, hp=hp, u=u: e.dma_start(
                    out=l_all[2 * hp:2 * hp + 1, u * 512:(u + 1) * 512], in_=ltmp[0:1, :]), "dS")
                E("gpsimd", lambda e, hp=hp, u=u: e.dma_start(
                    out=l_all[2 * hp + 1:2 * hp + 2, u * 512:(u + 1) * 512],
                    in_=ltmp[32:33, :]), "dS")
            last_exp_prev_hp = exp_cnt[(hp, 3, KTU[3] - 1)]

        tAtt = cnt["tS"]
        vAtt = u_evac[(7, 3)]

        if dbg not in ("A", "B"):
            emit_ffn_mean(len(ffn_srcs))
            gMean = cnt["scS"]
            W("vector", "scS", gMean)
            W("vector", "tS", tAtt)
            for wi in range(3):
                E("vector", lambda e, wi=wi: e.reduce_sum(
                    wtot[:, wi:wi + 1], wst[:, wi * 64:(wi + 1) * 64], AX.X), "vS")
            W("tensor", "scS", cnt["scS"])
            W("tensor", "vS", cnt["vS"])
            for wi in range(3):
                if wi > 0:
                    W("tensor", "vS", cnt["vS"])
                E("tensor", lambda e, wi=wi: e.matmul(
                    psT1[:, :], onescol_f[:, :], wtot[:, wi:wi + 1], start=True, stop=True), "tS")
                W("vector", "tS", cnt["tS"])
                E("vector", lambda e, wi=wi: e.tensor_copy(wsc[:, wi:wi + 1], psT1[:, :]), "vS")
            W("vector", "vS", cnt["vS"])
            ECH("vector", lambda e: e.tensor_scalar(wsc[:, 3:6], wsc[:, 0:3], 1.0 / (D * INNER), 1e-5,
                                                    ALU.mult, ALU.max))
            ECH("vector", lambda e: e.reciprocal(wsc[:, 0:3], wsc[:, 3:6]))
            ECH("vector", lambda e: e.tensor_scalar(wsc[:, 6:8], wsc[:, 3:5], 1.0 / 127, None, ALU.mult))
            ECH("vector", lambda e: e.tensor_scalar(wsc[:, 5:6], wsc[:, 5:6], 1.0 / 127, None, ALU.mult))
            W("tensor", "vS", cnt["vS"])
            W("tensor", "vS", vAtt)
            for wi in range(3):
                if wi > 0:
                    W("tensor", "vS", cnt["vS"])
                E("tensor", lambda e, wi=wi: e.matmul(psBC0[:, :], onesrow_f[:, :],
                                                      wsc[0:1, wi:wi + 1], start=True, stop=True), "tS")
                W("vector", "tS", cnt["tS"])
                E("vector", lambda e, wi=wi: e.tensor_copy(swv[:, wi:wi + 1], psBC0[:, :]), "vS")
            for wi, scol in enumerate([6, 7, 5]):
                if wi > 0:
                    W("tensor", "vS", cnt["vS"])
                E("tensor", lambda e, scol=scol: e.matmul(psBC1[:, :], onesrow_f[:, :],
                                                          wsc[0:1, scol:scol + 1],
                                                          start=True, stop=True), "tS")
                W("vector", "tS", cnt["tS"])
                E("vector", lambda e, wi=wi: e.tensor_copy(mwv[:, wi:wi + 1], psBC1[:, :]), "vS")
            wsc_ready = cnt["vS"]

            qdst = {0: wgq, 1: wuqq, 2: wdqq}
            qdone = []

            def _qsafe(item):
                wi, idx, _ = item
                return wi == 0 or (wi == 1 and idx < 24)
            ffn_q_order = ([x for x in ffn_srcs if _qsafe(x)]
                           + [x for x in ffn_srcs if not _qsafe(x)])
            qstate = {"i": 0, "tb3": None, "tb3_waited": False}

            def emit_quant(n):
                for _ in range(n):
                    i = qstate["i"]
                    if i >= len(ffn_q_order):
                        return
                    wi, idx, src = ffn_q_order[i]
                    if not _qsafe((wi, idx, src)) and qstate["tb3"] is None:
                        return
                    slot = i % 8
                    lq = "sync" if i % 2 == 0 else "scalar"
                    dsem = "dQs%d" % slot
                    if i < 8:
                        W(lq, "scS", gMean)
                        W(lq, "vS", wsc_ready)
                        W(lq, "tS", tAtt)
                    else:
                        W(lq, "vS", qdone[i - 8])
                    E(lq, lambda e, s=src, sl=slot: e.dma_start(
                        out=fring[:, sl * 512:(sl + 1) * 512], in_=s), dsem)
                    W("vector", dsem, cnt[dsem])
                    if i < 8:
                        W("vector", "vS", wsc_ready)
                    if not _qsafe((wi, idx, src)) and not qstate["tb3_waited"]:
                        W("vector", "tS", qstate["tb3"])
                        qstate["tb3_waited"] = True
                    if wi < 2:
                        dcol = (idx // 8) * INNER + (idx % 8) * 512
                    else:
                        dcol = (idx // 2) * D + (idx % 2) * 512
                    E("vector", lambda e, sl=slot, wi=wi: e.tensor_scalar(
                        fring[:, sl * 512:(sl + 1) * 512], fring[:, sl * 512:(sl + 1) * 512],
                        swv[:, wi:wi + 1], 1.0, ALU.mult, ALU.min))
                    E("vector", lambda e, sl=slot: e.tensor_scalar(
                        fring[:, sl * 512:(sl + 1) * 512], fring[:, sl * 512:(sl + 1) * 512],
                        -1.0, MAGIC, ALU.max, ALU.add))
                    E("vector", lambda e, sl=slot, wi=wi, dcol=dcol: e.tensor_scalar(
                        qdst[wi][:, dcol:dcol + 512], fring[:, sl * 512:(sl + 1) * 512],
                        MAGIC, None, ALU.subtract), "vS")
                    qdone.append(cnt["vS"])
                    qstate["i"] = i + 1
        else:
            def emit_quant(n):
                return

        W("vector", "dS", cnt["dS"])
        E("vector", lambda e: e.reciprocal(l_all[:, :], l_all[:, :]), "vS")
        W("tensor", "scS", cnt["scS"])
        W("tensor", "vS", cnt["vS"])
        wo_ev = {}
        for u in range(4):
            for hp in range(8):
                if (u, hp) != (0, 0):
                    W("tensor", "vS", cnt["vS"])
                E("tensor", lambda e, hp=hp, u=u: e.matmul(
                    psR[:, :], selEs[:, hp * 128:(hp + 1) * 128],
                    l_all[:, u * 512:(u + 1) * 512], start=True, stop=True), "tS")
                W("vector", "tS", cnt["tS"])
                E("vector", lambda e, hp=hp, u=u: e.tensor_mul(
                    OT[:, hp * 2048 + u * 512:hp * 2048 + (u + 1) * 512],
                    OT[:, hp * 2048 + u * 512:hp * 2048 + (u + 1) * 512], psR[:, :]), "vS")
            norm_u = cnt["vS"]
            for tt in range(4):
                ti = u * 4 + tt
                for nbk in range(2):
                    pk2 = u * 8 + tt * 2 + nbk
                    bank = [psW0, psW1, psW2, psW3][pk2 % 4]
                    if pk2 >= 4:
                        W("tensor", "vS", wo_ev[pk2 - 4])
                    elif pk2 >= 2:
                        W("tensor", "vS", vAtt)
                    W("tensor", "vS", norm_u)
                    for hp in range(8):
                        E("tensor", lambda e, b=bank, hp=hp, u=u, tt=tt, nbk=nbk: e.matmul(
                            b[:, :],
                            OT[:, hp * 2048 + u * 512 + tt * 128:hp * 2048 + u * 512 + (tt + 1) * 128],
                            wo_sb[:, hp * D + nbk * 512:hp * D + nbk * 512 + 512],
                            start=(hp == 0), stop=(hp == 7)), "tS" if hp == 7 else None)
                    W("vector", "tS", cnt["tS"])
                    E("vector", lambda e, b=bank, ti=ti, nbk=nbk: e.scalar_tensor_tensor(
                        X[:, ti * D + nbk * 512:ti * D + nbk * 512 + 512], b[:, :], 1.0,
                        X[:, ti * D + nbk * 512:ti * D + nbk * 512 + 512],
                        ALU.mult, ALU.add), "vS")
                    wo_ev[pk2] = cnt["vS"]
                    emit_quant(3)
        tB3 = cnt["tS"]
        vB3 = cnt["vS"]


        if dbg == "B":
            _dump([("X", X[:, :]), ("l_all", l_all[:, :]), ("OT", OT[:, :]),
                   ("biasA", biasA[:, :]), ("biasB", biasB[:, :]),
                   ("KT", KT_s[:, :]), ("QT", QT[:, :]), ("V", V_s[:, :]),
                   ("PT", PT[:, :]), ("qk", qk_sb[:, :]), ("qk2", qk2_sb[:, :]),
                   ("dbgb", dbgb[:, :])])
    if dbg not in ("A", "B"):
        qstate["tb3"] = tB3
        W("gpsimd", "tS", tB3)
        emit_quant(128 - qstate["i"])
        quant_gu = cnt["vS"]
        emit_quant(len(ffn_q_order) - qstate["i"])
        quant_v = cnt["vS"]
        quant_g = cnt["gS"]


        if dbg == "C0":
            _dump([("wgq", wgq[:, :]), ("wuqq", wuqq[:, :]), ("wdqq", wdqq[:, :]),
                   ("wsc", wsc[:, :]), ("swv", swv[:, :]), ("mwv", mwv[:, :]),
                   ("wst", wst[:, :]), ("wtot", wtot[:, :])])
    if dbg not in ("A", "B", "C0"):
        tile_done = {}
        tstate = {}
        qh2ds = {}

        def emit_stageA(ti):
            xap = X[:, ti * D:(ti + 1) * D]
            if ti == 0:
                W("vector", "gS", quant_g)
                W("vector", "tS", tB3)
                W("scalar", "tS", tB3)
            E("vector", lambda e, x=xap: e.reduce_sum(sums[:, 0:1], x, AX.X), "vS")
            W("scalar", "vS", cnt["vS"])
            if "gu_prev" in tstate:
                W("scalar", "vS", tstate["gu_prev"][7][1])
            E("scalar", lambda e, x=xap: e.activation(
                scr4k[:, :], x, AF.Square, accum_out=sumsq[:, 0:1]), "scS")
            W("vector", "scS", cnt["scS"])
            ECH("vector", lambda e: e.tensor_scalar(tmp4a[:, 0:1], sums[:, 0:1], 1.0 / D, None, ALU.mult))
            ECH("vector", lambda e: e.tensor_scalar(tmp4b[:, 0:1], tmp4a[:, 0:1], tmp4a[:, 0:1], -1.0, ALU.mult, ALU.mult))
            ECH("vector", lambda e: e.scalar_tensor_tensor(tmp4b[:, 0:1], sumsq[:, 0:1], 1.0 / D,
                                                         tmp4b[:, 0:1], ALU.mult, ALU.add))
            E("vector", lambda e: e.tensor_scalar(
                rstds[:, 0:1], tmp4b[:, 0:1], LN_EPS, None, ALU.add), "vS")
            W("scalar", "vS", cnt["vS"])
            E("scalar", lambda e: e.activation(rstds[:, 0:1], rstds[:, 0:1], AF.Sqrt), "scS")
            W("vector", "scS", cnt["scS"])
            ECH("vector", lambda e: e.reciprocal(rstds[:, 0:1], rstds[:, 0:1]))
            ECH("vector", lambda e: e.tensor_scalar(negmr[:, 0:1], tmp4a[:, 0:1], rstds[:, 0:1],
                                                  -1.0, ALU.mult, ALU.mult))
            E("vector", lambda e, x=xap: e.tensor_scalar(
                scr4k[:, :], x, rstds[:, 0:1], negmr[:, 0:1], ALU.mult, ALU.add))
            ECH("vector", lambda e: e.tensor_reduce(am_sb[:, 0:1], scr4k[:, :], AX.X, ALU.max,
                                                  apply_absolute_value=True))
            ECH("vector", lambda e: e.tensor_scalar(am_sb[:, 0:1], am_sb[:, 0:1], 1e-5, None, ALU.max))
            ECH("vector", lambda e: e.reciprocal(sa_sb[:, 0:1], am_sb[:, 0:1]))
            ECH("vector", lambda e: e.tensor_scalar(sa_sb[:, 0:1], sa_sb[:, 0:1], 127.0, None, ALU.mult))
            if "gu_prev" in tstate:
                W("vector", "scS", tstate["gu_prev"][7][0])
            E("vector", lambda e: e.tensor_mul(inv_sb[:, 0:1], am_sb[:, 0:1], mwv[:, 0:1]))
            E("vector", lambda e: e.tensor_mul(inv_sb[:, 1:2], am_sb[:, 0:1], mwv[:, 1:2]))
            E("vector", lambda e: e.tensor_scalar(scr4k[:, :], scr4k[:, :], sa_sb[:, 0:1], MAGIC,
                                                  ALU.mult, ALU.add))
            if "qh2_tr" in tstate:
                W("vector", "tS", tstate["qh2_tr"])
            E("vector", lambda e: e.tensor_scalar(qh2_sb[:, :], scr4k[:, :], MAGIC, None,
                                                  ALU.subtract), "vS")
            qh2ds[ti] = cnt["vS"]

        emit_stageA(0)
        for ti in range(NT_OWN):
            W("tensor", "vS", qh2ds[ti])
            if ti == 0:
                W("tensor", "vS", quant_gu)
                W("tensor", "gS", quant_g)
            else:
                W("tensor", "vS", tstate["qh2T_evac"])
                W("tensor", "vS", tstate["ev7_prev"])
            for ck in range(8):
                E("tensor", lambda e, ck=ck: e.transpose(
                    psTrC[:, ck * 128:(ck + 1) * 128], qh2_sb[:, ck * 128:(ck + 1) * 128],
                    ident[:, :]), "tS" if ck == 7 else None)
            tstate["qh2_tr"] = cnt["tS"]
            W("vector", "tS", cnt["tS"])
            if "gate_last" in tstate:
                W("vector", "tS", tstate["gate_last"])
            E("vector", lambda e: e.tensor_copy(qh2T[:, :], psTrC[:, :]), "vS")
            qh2T_done = cnt["vS"]
            tstate["qh2T_evac"] = qh2T_done
            gu_done = []
            for ib in range(8):
                pG = [psG0, psG1][ib % 2]
                pU = [psU0, psU1][ib % 2]
                W("tensor", "vS", qh2T_done)
                if ib >= 2:
                    W("tensor", "vS", gu_done[ib - 2][1])
                    W("tensor", "scS", gu_done[ib - 2][0])
                elif "gu_prev" in tstate:
                    W("tensor", "vS", tstate["gu_prev"][6 + ib][1])
                    W("tensor", "scS", tstate["gu_prev"][6 + ib][0])
                for ck in range(8):
                    E("tensor", lambda e, p=pG, ck=ck, ib=ib: e.matmul(
                        p[:, :], qh2T[:, ck * 128:(ck + 1) * 128],
                        wgq[:, ck * INNER + ib * 512:ck * INNER + ib * 512 + 512],
                        start=(ck == 0), stop=(ck == 7)), "tS" if ck == 7 else None)
                for ck in range(8):
                    E("tensor", lambda e, p=pU, ck=ck, ib=ib: e.matmul(
                        p[:, :], qh2T[:, ck * 128:(ck + 1) * 128],
                        wuqq[:, ck * INNER + ib * 512:ck * INNER + ib * 512 + 512],
                        start=(ck == 0), stop=(ck == 7)), "tS" if ck == 7 else None)
                mm_done = cnt["tS"]
                W("scalar", "tS", mm_done)
                W("scalar", "vS", cnt["vS"])
                E("scalar", lambda e, p=pG: e.activation(
                    gsil[:, :], p[:, :], AF.Silu, scale=inv_sb[:, 0:1]), "scS")
                W("vector", "scS", cnt["scS"])
                if ib == 0 and "qgu_tr_last" in tstate:
                    W("vector", "tS", tstate["qgu_tr_last"])
                E("vector", lambda e, p=pU, ib=ib: e.scalar_tensor_tensor(
                    gu_sb[:, ib * 512:(ib + 1) * 512], p[:, :], inv_sb[:, 1:2], gsil[:, :],
                    ALU.mult, ALU.mult), "vS")
                gu_done.append((cnt["scS"], cnt["vS"]))
                if ib == 7:
                    tstate["gate_last"] = cnt["tS"]
                    tstate["gu_prev"] = list(gu_done)
            ECH("vector", lambda e: e.tensor_reduce(am_sb[:, 1:2], gu_sb[:, :], AX.X, ALU.max,
                                                  apply_absolute_value=True))
            ECH("vector", lambda e: e.tensor_scalar(am_sb[:, 1:2], am_sb[:, 1:2], 1e-5, None, ALU.max))
            ECH("vector", lambda e: e.reciprocal(sa_sb[:, 1:2], am_sb[:, 1:2]))
            ECH("vector", lambda e: e.tensor_scalar(sa_sb[:, 1:2], sa_sb[:, 1:2], 127.0, None, ALU.mult))
            E("vector", lambda e: e.tensor_mul(inv_sb[:, 2:3], am_sb[:, 1:2], mwv[:, 2:3]))
            ev_rounds = []
            tr_cnts = []
            for rr in range(8):
                E("vector", lambda e, rr=rr: e.tensor_scalar(
                    gu_sb[:, rr * 512:(rr + 1) * 512], gu_sb[:, rr * 512:(rr + 1) * 512],
                    sa_sb[:, 1:2], MAGIC, ALU.mult, ALU.add))
                E("vector", lambda e, rr=rr: e.tensor_scalar(
                    gu_sb[:, rr * 512:(rr + 1) * 512], gu_sb[:, rr * 512:(rr + 1) * 512],
                    MAGIC, None, ALU.subtract), "vS")
                q_cnt = cnt["vS"]
                pT = [psTrCf, psTrCf2][rr % 2]
                W("tensor", "vS", q_cnt)
                if rr >= 2:
                    W("tensor", "vS", ev_rounds[rr - 2])
                elif "ev7_prev" in tstate and ti > 0:
                    W("tensor", "vS", tstate["ev7_prev"])
                for ck in range(4):
                    E("tensor", lambda e, pT=pT, rr=rr, ck=ck: e.transpose(
                        pT[:, ck * 128:(ck + 1) * 128],
                        gu_sb[:, rr * 512 + ck * 128:rr * 512 + (ck + 1) * 128],
                        ident_f[:, :]), "tS" if ck == 3 else None)
                tr_cnts.append(cnt["tS"])
                if rr >= 1:
                    prr = rr - 1
                    W("vector", "tS", tr_cnts[prr])
                    if prr == 0 and "down_mm_last" in tstate:
                        W("vector", "tS", tstate["down_mm_last"])
                    E("vector", lambda e, prr=prr: e.tensor_copy(
                        qguT[:, prr * 512:(prr + 1) * 512],
                        [psTrCf, psTrCf2][prr % 2][:, :]), "vS")
                    ev_rounds.append(cnt["vS"])
            W("vector", "tS", tr_cnts[7])
            E("vector", lambda e: e.tensor_copy(qguT[:, 7 * 512:8 * 512], psTrCf2[:, :]), "vS")
            ev_rounds.append(cnt["vS"])
            tstate["qgu_tr_last"] = tr_cnts[7]
            tstate["ev7_prev"] = ev_rounds[7]
            if ti + 1 < NT_OWN:
                emit_stageA(ti + 1)
            for nbk in range(2):
                pD = [psD0, psD1][nbk]
                W("tensor", "vS", ev_rounds[7])
                if ti == 0 and nbk == 0:
                    W("tensor", "vS", quant_v)
                if ti > 0 and nbk == 0:
                    W("tensor", "vS", tstate["dn_prev"])
                for ic in range(32):
                    E("tensor", lambda e, p=pD, ic=ic, nbk=nbk: e.matmul(
                        p[:, :], qguT[:, ic * 128:(ic + 1) * 128],
                        wdqq[:, ic * D + nbk * 512:ic * D + nbk * 512 + 512],
                        start=(ic == 0), stop=(ic == 31)), "tS" if ic == 31 else None)
                W("vector", "tS", cnt["tS"])
                E("vector", lambda e, p=pD, ti=ti, nbk=nbk: e.scalar_tensor_tensor(
                    X[:, ti * D + nbk * 512:ti * D + nbk * 512 + 512], p[:, :], inv_sb[:, 2:3],
                    X[:, ti * D + nbk * 512:ti * D + nbk * 512 + 512], ALU.mult, ALU.add), "vS")
            tile_done[ti] = cnt["vS"]
            tstate["dn_prev"] = cnt["vS"]
            tstate["down_mm_last"] = cnt["tS"]
            W("gpsimd", "vS", tile_done[ti])
            E("gpsimd", lambda e, ti=ti: e.dma_start(
                out=dout[ti * 128:(ti + 1) * 128, :], in_=X[:, ti * D:(ti + 1) * D]), "dS")
            if dbg == "C1" and ti == 0:
                _dump([("qh2", qh2_sb[:, :]), ("qh2T", qh2T[:, :]), ("qguT", qguT[:, :]),
                       ("am", am_sb[:, :]), ("sa", sa_sb[:, :]), ("inv", inv_sb[:, :]),
                       ("X0", X[:, 0:1024]), ("gsil", gsil[:, :]), ("qgu", gu_sb[:, :]),
                       ("rstds", rstds[:, :]), ("negmr", negmr[:, :])])
                break

        W("gpsimd", "dS", cnt["dS"])

    with nc.Block() as block:
        def mk(engname):
            def run(eng):
                for op in OPS[engname]:
                    if op[0] == "w":
                        eng.wait_ge(sems[op[1]], op[2])
                    else:
                        ins = op[1](eng)
                        if op[2] is not None:
                            ins.then_inc(sems[op[2][0]], op[2][1])
            return run
        block.sync(mk("sync"))
        block.tensor(mk("tensor"))
        block.vector(mk("vector"))
        block.scalar(mk("scalar"))
        block.gpsimd(mk("gpsimd"))
    return nc


def _masks_np(half):
    m = np.zeros((128, 2560), dtype=np.float32)
    ii = np.arange(128)[:, None]
    offs03 = [(0, 128), (128, 256), (384, 384), (768, 512)]
    for jp, (o, w) in enumerate(offs03):
        if half == 0:
            m[:, o:o + w] = 1.0
        else:
            jj = np.arange(w)[None, :]
            m[:, o:o + w] = (jj >= 128 * jp + ii).astype(np.float32)
    offs47 = [(1280, 512), (1792, 384), (2176, 256), (2432, 128)]
    for j4, (o, w) in enumerate(offs47):
        if half == 1:
            m[:, o:o + w] = 0.0
        else:
            jj = np.arange(w)[None, :]
            m[:, o:o + w] = (jj >= ii).astype(np.float32)
    return np.asarray(m, dtype=ml_dtypes.bfloat16)


def kernel(L0_attn_x, L0_attn_W_dq, L0_attn_W_uq, L0_attn_W_dkv, L0_attn_W_uk,
           L0_attn_W_uv, L0_attn_W_o, L0_ffn_w_gate, L0_ffn_w_up, L0_ffn_w_down):
    if "nc" not in _CACHE:
        _CACHE["nc"] = _build_nc()
    nc = _CACHE["nc"]

    x = np.ascontiguousarray(np.asarray(L0_attn_x, dtype=np.float32))
    shared = {
        "w_dq": np.ascontiguousarray(np.asarray(L0_attn_W_dq, dtype=np.float32)),
        "w_dkv": np.ascontiguousarray(np.asarray(L0_attn_W_dkv, dtype=np.float32)),
        "w_uq": np.ascontiguousarray(np.asarray(L0_attn_W_uq, dtype=np.float32)),
        "w_uk": np.ascontiguousarray(np.asarray(L0_attn_W_uk, dtype=np.float32)),
        "w_uv": np.ascontiguousarray(np.asarray(L0_attn_W_uv, dtype=np.float32)),
        "w_o": np.ascontiguousarray(np.asarray(L0_attn_W_o, dtype=np.float32)),
        "wg_t": np.ascontiguousarray(np.asarray(L0_ffn_w_gate, dtype=np.float32).T),
        "wu_t": np.ascontiguousarray(np.asarray(L0_ffn_w_up, dtype=np.float32).T),
        "wd_t": np.ascontiguousarray(np.asarray(L0_ffn_w_down, dtype=np.float32).T),
    }
    blocks_of = {0: [1, 3, 5, 7], 1: [0, 2, 4, 6]}
    in_maps = []
    for c in range(8):
        b, h = c >> 1, c & 1
        im = dict(shared)
        im["x_own"] = np.ascontiguousarray(
            np.concatenate([x[b, 512 * k:512 * (k + 1)] for k in blocks_of[h]], axis=0))
        im["x_all"] = np.ascontiguousarray(x[b])
        im["masks"] = _masks_np(h)
        in_maps.append(im)

    res = run_bass_kernel_spmd(nc, in_maps, core_ids=list(range(8)))
    out = np.empty((NBATCH, SEQ, D), dtype=np.float32)
    for c in range(8):
        b, h = c >> 1, c & 1
        r = np.asarray(res.results[c]["out"])
        for j, k in enumerate(blocks_of[h]):
            out[b, 512 * k:512 * (k + 1)] = r[512 * j:512 * (j + 1)]
    return out
